# version 08
# baseline (speedup 1.0000x reference)
import contextlib
import math

import numpy as np
import ml_dtypes

import concourse.bass as bass
import concourse.mybir as mybir
from concourse.bass_utils import run_bass_kernel_spmd

F32 = mybir.dt.float32
BF16 = mybir.dt.bfloat16
I8 = mybir.dt.int8
AF = mybir.ActivationFunctionType
ALU = mybir.AluOpType
AXX = mybir.AxisListType.X

NCORES = 8


class CFG:
    def __init__(self, D=2048, I=8192, T=2048, H=2, TS=512, QB=512, eps=1e-5):
        self.D, self.I, self.T, self.H = D, I, T, H
        self.S = 2 * T
        self.DH = D // H
        self.TS, self.QB = TS, QB
        self.KC = D // 128
        self.IC = I // 128
        self.NT = T // TS
        self.NQB = T // QB
        self.T2C = self.S // 128
        self.DQC = self.DH // 128
        self.SC = 1.0 / math.sqrt(self.DH)
        self.eps = eps


def build_program(cfg: CFG, debug=False, phases=None):
    nc = bass.Bass("TRN2", target_bir_lowering=False, detect_race_conditions=False)
    D, I, T, H = cfg.D, cfg.I, cfg.T, cfg.H
    KC, IC, TS, NT = cfg.KC, cfg.IC, cfg.TS, cfg.NT
    QB, NQB, T2C, DQC = cfg.QB, cfg.NQB, cfg.T2C, cfg.DQC
    DH, S = cfg.DH, cfg.S
    D2C = KC

    x_t = nc.dram_tensor("x_t", [KC, 128, T], F32, kind="ExternalInput").ap()
    wqkv_t = nc.dram_tensor("wqkv_t", [KC, 128, 3 * D], F32, kind="ExternalInput").ap()
    wout_t = nc.dram_tensor("wout_t", [KC, 128, D], F32, kind="ExternalInput").ap()
    wg_t = nc.dram_tensor("wg_t", [IC, 128, D], I8, kind="ExternalInput").ap()
    wu_t = nc.dram_tensor("wu_t", [IC, 128, D], I8, kind="ExternalInput").ap()
    wd_t = nc.dram_tensor("wd_t", [IC, 128, D], I8, kind="ExternalInput").ap()
    sg_pp = nc.dram_tensor("sg_pp", [128, IC], F32, kind="ExternalInput").ap()
    su_pp = nc.dram_tensor("su_pp", [128, IC], F32, kind="ExternalInput").ap()
    sd_pp = nc.dram_tensor("sd_pp", [128, KC], F32, kind="ExternalInput").ap()
    bout_pp = nc.dram_tensor("bout_pp", [128, KC], F32, kind="ExternalInput").ap()
    cos_t = nc.dram_tensor("cos_t", [DQC, 128, T], F32, kind="ExternalInput").ap()
    sin_t = nc.dram_tensor("sin_t", [DQC, 128, T], F32, kind="ExternalInput").ap()

    out_t = nc.dram_tensor("out_t", [KC, 128, T], F32, kind="ExternalOutput").ap()

    xmid = nc.dram_tensor("xmid", [KC, 128, T], F32, kind="Internal").ap()
    xmid2 = nc.dram_tensor("xmid2", [KC, 128, T], F32, kind="Internal").ap()
    qkv_hi = nc.dram_tensor("qkv_hi", [KC, 128, 3 * D], BF16, kind="Internal").ap()
    qkv_lo = nc.dram_tensor("qkv_lo", [KC, 128, 3 * D], BF16, kind="Internal").ap()
    wo_hi = nc.dram_tensor("wo_hi", [KC, 128, D], BF16, kind="Internal").ap()
    wo_lo = nc.dram_tensor("wo_lo", [KC, 128, D], BF16, kind="Internal").ap()
    fuse_d = nc.dram_tensor("fuse_d", [IC, 2, 128, TS], BF16, kind="Internal").ap()
    qpark = nc.dram_tensor("qpark", [2, KC, 128, T], BF16, kind="Internal").ap()
    NT_ = T // TS
    NT1_ = TS // 128
    k_in = [nc.dram_tensor(f"k_in{i}", [2, KC, 128, TS], BF16, kind="Internal")
            for i in range(NT_)]
    k_all = [nc.dram_tensor(f"k_all{i}", [2, 2, KC, 128, TS], BF16, kind="Internal")
             for i in range(NT_)]
    v_in = [nc.dram_tensor(f"v_in{i}", [2, NT1_, 128, D], BF16, kind="Internal")
            for i in range(NT_)]
    v_all = [nc.dram_tensor(f"v_all{i}", [2, 2, NT1_, 128, D], BF16, kind="Internal")
             for i in range(NT_)]

    dbg = {}
    if debug:
        dbg["dbg_xmid"] = nc.dram_tensor("dbg_xmid", [KC, 128, T], F32,
                                         kind="ExternalOutput").ap()
        dbg["dbg_xmid2"] = nc.dram_tensor("dbg_xmid2", [KC, 128, T], F32,
                                          kind="ExternalOutput").ap()
        dbg["dbg_k"] = nc.dram_tensor("dbg_k", [NT_, 2, KC, 128, TS], BF16,
                                      kind="ExternalOutput").ap()
        dbg["dbg_v"] = nc.dram_tensor("dbg_v", [NT_, 2, NT1_, 128, D], BF16,
                                      kind="ExternalOutput").ap()
        dbg["dbg_q"] = nc.dram_tensor("dbg_q", [2, KC, 128, T], BF16,
                                      kind="ExternalOutput").ap()

    ones_col_f32 = nc.inline_tensor(np.ones((128, 1), dtype=np.float32), "ones_col_f32")
    ones_row_f32 = nc.inline_tensor(np.ones((1, 128), dtype=np.float32), "ones_row_f32")
    ones_col_bf = nc.inline_tensor(np.ones((128, 1), dtype=ml_dtypes.bfloat16), "ones_col_bf")

    class Ctr:
        def __init__(self, handle, step=1):
            self.h, self.n, self.step = handle, 0, step

        def inc(self, ins, k=None):
            k = self.step if k is None else k
            self.n += k
            ins.then_inc(self.h, k)
            return self.n

    class DStream:

        def __init__(self, ctrs, dpg):
            self.c = ctrs
            self.dpg = dpg
            self.g = 0

        def inc(self, g, ins):
            self.c[g % len(self.c)].inc(ins)

        def wait(self, eng, g):
            eng.wait_ge(self.c[g % len(self.c)].h,
                        16 * self.dpg * (g // len(self.c) + 1))

    with contextlib.ExitStack() as stk:
        e = stk.enter_context

        O_SG, O_SU = 0, IC
        O_SD, O_BO = 2 * IC, 2 * IC + D2C
        O_OC = 2 * IC + 2 * D2C
        O_NM = O_OC + 1
        CONST = e(nc.sbuf_tensor("CONST", [128, O_NM + 1], F32))
        ONESB = e(nc.sbuf_tensor("ONESB", [128, 1], BF16))
        ONESR = e(nc.sbuf_tensor("ONESR", [1, 128], F32))
        SMALL = e(nc.sbuf_tensor("SMALL", [1, 4 * max(TS, QB)], F32))
        MROW = e(nc.sbuf_tensor("MROW", [1, 8], F32))
        HI = e(nc.sbuf_tensor("HI", [128, KC * TS], BF16))
        LO = e(nc.sbuf_tensor("LO", [128, KC * TS], BF16))
        PS = e(nc.psum_tensor("PS", [128, 8 * 512], F32))

        def bank(j, w=None):
            w = TS if w is None else w
            return PS[:, j * 512:j * 512 + w]

        def bankrow(j, w=None):
            w = TS if w is None else w
            return PS[0:1, j * 512:j * 512 + w]

        sS = Ctr(e(nc.semaphore("sS")), 16)
        sA = Ctr(e(nc.semaphore("sA")), 16)
        gD = Ctr(e(nc.semaphore("gD")), 16)
        cT = Ctr(e(nc.semaphore("cT")))
        cV = Ctr(e(nc.semaphore("cV")))
        cA = Ctr(e(nc.semaphore("cA")))
        cC = Ctr(e(nc.semaphore("cC")))
        bar = {n: Ctr(e(nc.semaphore("b" + n))) for n in "STVAG"}

        def mkst(name, ns, dpg):
            return DStream([Ctr(e(nc.semaphore(f"{name}{i}")), 16)
                            for i in range(ns)], dpg)

        stL = mkst("stL", 2, 1)
        stWC = mkst("stWC", 2, 2)
        stW = mkst("stW", 2, 2)
        stWD = mkst("stWD", 2, 1)
        stF = mkst("stF", 4, 2)
        stAF = mkst("stAF", 2, 2)
        stR = mkst("stR", 2, 2)
        stQW = mkst("stQW", 2, 2)
        stK = mkst("stK", 2, 2 * DQC)
        stQ = mkst("stQ", 1, 2 * DQC)
        stV = mkst("stV", 2, 2)
        stX = mkst("stX", 2, 1)
        stNB = mkst("stNB", 2, KC)
        block = e(nc.Block())
        ENG = {"S": block.sync, "T": block.tensor, "V": block.vector,
               "A": block.scalar, "G": block.gpsimd}

        nbar = [0]

        def barrier():
            nbar[0] += 1
            k = nbar[0]
            for name in "STVAG":
                def f(eng, name=name, k=k):
                    eng.nop().then_inc(bar[name].h, 1)
                    for o in "STVAG":
                        if o != name:
                            eng.wait_ge(bar[o].h, k)
                ENG[name](f)
                bar[name].n += 1

        def p_consts():
            def fsync(sync):
                for dst_, src_ in [(CONST[:, O_SG:O_SG + IC], sg_pp),
                                   (CONST[:, O_SU:O_SU + IC], su_pp),
                                   (CONST[:, O_SD:O_SD + D2C], sd_pp),
                                   (CONST[:, O_BO:O_BO + D2C], bout_pp),
                                   (CONST[:, O_OC:O_OC + 1], ones_col_f32.ap()),
                                   (ONESB[:, :], ones_col_bf.ap()),
                                   (ONESR[:, :], ones_row_f32.ap())]:
                    sS.inc(sync.dma_start(out=dst_, in_=src_))
                sync.wait_ge(sS.h, sS.n)
            block.sync(fsync)

            def fscal(scalar):
                scalar.wait_ge(sS.h, sS.n)
                cA.inc(scalar.mul(CONST[:, O_SD:O_SD + D2C],
                                  CONST[:, O_SD:O_SD + D2C], 0.5))
            block.scalar(fscal)

            def fvec(vector):
                vector.memset(MROW[:, :], 0.0)
                cV.inc(vector.memset(MROW[:, 7:8], float(cfg.eps)))
            block.vector(fvec)
            barrier()

        def p_wconv():
            W = 3 * D
            tiles = [(wqkv_t, qkv_hi, qkv_lo, 3 * D, kc) for kc in range(KC)] + \
                    [(wout_t, wo_hi, wo_lo, D, kc) for kc in range(KC)]
            with nc.sbuf_tensor("cv_W32", [128, 2 * W], F32) as W32, \
                 nc.sbuf_tensor("cv_WHI", [128, 2 * W], BF16) as WHI, \
                 nc.sbuf_tensor("cv_WLO", [128, 2 * W], BF16) as WLO:
                cV0 = cV.n
                gL, gC = stL.g, stWC.g

                def fsync(sync):
                    for i, (src, _, _, w, kc) in enumerate(tiles):
                        if i >= 2:
                            sync.wait_ge(cV.h, cV0 + (i - 1))
                        stL.inc(gL + i, sync.dma_start(
                            out=W32[:, (i % 2) * W:(i % 2) * W + w], in_=src[kc]))
                block.sync(fsync)

                def fvec(vector):
                    for i, (_, _, _, w, kc) in enumerate(tiles):
                        o = (i % 2) * W
                        stL.wait(vector, gL + i)
                        if i >= 2:
                            stWC.wait(vector, gC + i - 2)
                        vector.tensor_copy(WHI[:, o:o + w], W32[:, o:o + w])
                        cV.inc(vector.tensor_tensor(WLO[:, o:o + w], W32[:, o:o + w],
                                                    WHI[:, o:o + w], ALU.subtract))
                block.vector(fvec)

                def fscal(scalar):
                    for i, (_, dhi, dlo, w, kc) in enumerate(tiles):
                        o = (i % 2) * W
                        scalar.wait_ge(cV.h, cV0 + (i + 1))
                        stWC.inc(gC + i, scalar.dma_start(out=dhi[kc], in_=WHI[:, o:o + w]))
                        stWC.inc(gC + i, scalar.dma_start(out=dlo[kc], in_=WLO[:, o:o + w]))
                block.scalar(fscal)
                stL.g += len(tiles)
                stWC.g += len(tiles)
                barrier()

        def emit_ln(ts, src, XA, XB):
            def fsync(sync):
                for c in range(KC):
                    sS.inc(sync.dma_start(out=XA[:, c * TS:(c + 1) * TS],
                                          in_=src[c][:, ts:ts + TS]))
            block.sync(fsync)
            s_loaded = sS.n

            def fscal(scalar):
                scalar.wait_ge(sS.h, s_loaded)
                cA.inc(scalar.square(XB[:, 0:KC * TS], XA[:, 0:KC * TS]))
            block.scalar(fscal)
            a_sq = cA.n

            def ftens(tensor):
                tensor.wait_ge(sS.h, s_loaded)
                for c in range(KC):
                    tensor.matmul(bankrow(6), CONST[:, O_OC:O_OC + 1],
                                  XA[:, c * TS:(c + 1) * TS],
                                  start=(c == 0), stop=(c == KC - 1))
                tensor.wait_ge(cA.h, a_sq)
                for c in range(KC):
                    mmi = tensor.matmul(bankrow(7), CONST[:, O_OC:O_OC + 1],
                                        XB[:, c * TS:(c + 1) * TS],
                                        start=(c == 0), stop=(c == KC - 1))
                cT.inc(mmi)
            block.tensor(ftens)
            t_stats = cT.n

            def fscal2(scalar):
                scalar.wait_ge(cT.h, t_stats)
                scalar.mul(SMALL[:, 0:TS], bankrow(6), 1.0 / D)
                scalar.mul(SMALL[:, TS:2 * TS], bankrow(7), 1.0 / D)
                cA.inc(scalar.square(SMALL[:, 2 * TS:3 * TS], SMALL[:, 0:TS]))
            block.scalar(fscal2)
            a_m = cA.n

            def fvec(vector):
                vector.wait_ge(cA.h, a_m)
                cV.inc(vector.tensor_sub(SMALL[:, TS:2 * TS], SMALL[:, TS:2 * TS],
                                         SMALL[:, 2 * TS:3 * TS]))
            block.vector(fvec)
            v_var = cV.n

            def fscal3(scalar):
                scalar.wait_ge(cV.h, v_var)
                cA.inc(scalar.activation(SMALL[:, 2 * TS:3 * TS], SMALL[:, TS:2 * TS],
                                         AF.Sqrt, bias=MROW[:, 7:8]))
            block.scalar(fscal3)
            a_sd = cA.n

            def fvec2(vector):
                vector.wait_ge(cA.h, a_sd)
                cV.inc(vector.reciprocal(SMALL[:, TS:2 * TS], SMALL[:, 2 * TS:3 * TS]))
            block.vector(fvec2)
            v_r = cV.n

            def ftens2(tensor):
                tensor.wait_ge(cV.h, v_r)
                tensor.matmul(bank(4), ONESR[:, :], SMALL[:, 0:TS], start=True, stop=True)
                mmi = tensor.matmul(bank(5), ONESR[:, :], SMALL[:, TS:2 * TS],
                                    start=True, stop=True)
                cT.inc(mmi)
            block.tensor(ftens2)
            t_b = cT.n

            def fvec3(vector):
                vector.wait_ge(cT.h, t_b)
                for c in range(KC):
                    sl = slice(c * TS, (c + 1) * TS)
                    vector.tensor_sub(XB[:, sl], XA[:, sl], bank(4))
                    vector.tensor_mul(XB[:, sl], XB[:, sl], bank(5))
                vector.tensor_copy(HI[:, 0:KC * TS], XB[:, 0:KC * TS])
                cV.inc(vector.tensor_tensor(LO[:, 0:KC * TS], XB[:, 0:KC * TS],
                                            HI[:, 0:KC * TS], ALU.subtract))
            block.vector(fvec3)
            return cV.n

        def p_ffn(src, dst, tag, final=False, parts=("ln", "gu", "dn")):
            with nc.sbuf_tensor(f"{tag}_XA", [128, KC * TS], F32) as XA, \
                 nc.sbuf_tensor(f"{tag}_XB", [128, KC * TS], F32) as XB, \
                 nc.sbuf_tensor(f"{tag}_WG", [128, 2 * D], BF16) as WG, \
                 nc.sbuf_tensor(f"{tag}_WU", [128, 2 * D], BF16) as WU, \
                 nc.sbuf_tensor(f"{tag}_WD", [128, 2 * D], BF16) as WD, \
                 nc.sbuf_tensor(f"{tag}_GS", [128, 2 * TS], F32) as GS, \
                 nc.sbuf_tensor(f"{tag}_FH", [128, 2 * TS], BF16) as FH, \
                 nc.sbuf_tensor(f"{tag}_FL", [128, 2 * TS], BF16) as FL, \
                 nc.sbuf_tensor(f"{tag}_FHR", [128, 4 * TS], BF16) as FHR, \
                 nc.sbuf_tensor(f"{tag}_FLR", [128, 4 * TS], BF16) as FLR:
                for it in range(NT):
                    ts = it * TS
                    v_ln = emit_ln(ts, src, XA, XB)
                    if "gu" not in parts:
                        barrier()
                        continue

                    t0, v0, a0 = cT.n, cV.n, cA.n
                    gW0 = stW.g

                    def fg(gpsimd, gW0=gW0, t0=t0):
                        for ic in range(IC):
                            if ic >= 2:
                                gpsimd.wait_ge(cT.h, t0 + (ic - 1))
                            o = (ic % 2) * D
                            stW.inc(gW0 + ic, gpsimd.dma_start(out=WG[:, o:o + D], in_=wg_t[ic]))
                            stW.inc(gW0 + ic, gpsimd.dma_start(out=WU[:, o:o + D], in_=wu_t[ic]))
                    block.gpsimd(fg)

                    def ft(tensor, gW0=gW0, v0=v0, v_ln=v_ln):
                        tensor.wait_ge(cV.h, v_ln)
                        for ic in range(IC):
                            stW.wait(tensor, gW0 + ic)
                            if ic >= 2:
                                tensor.wait_ge(cV.h, v0 + (ic - 1))
                            o = (ic % 2) * D
                            pg = bank(ic % 2)
                            pu = bank(2 + ic % 2)
                            for kc in range(KC):
                                wgc = WG[:, o + kc * 128:o + (kc + 1) * 128]
                                wuc = WU[:, o + kc * 128:o + (kc + 1) * 128]
                                xh = HI[:, kc * TS:(kc + 1) * TS]
                                xl = LO[:, kc * TS:(kc + 1) * TS]
                                tensor.matmul(pg, wgc, xh, start=(kc == 0), stop=False)
                                tensor.matmul(pg, wgc, xl, start=False, stop=(kc == KC - 1))
                                tensor.matmul(pu, wuc, xh, start=(kc == 0), stop=False)
                                mmi = tensor.matmul(pu, wuc, xl, start=False,
                                                    stop=(kc == KC - 1))
                            cT.inc(mmi)
                    block.tensor(ft)

                    gAF0 = stAF.g

                    def fa(scalar, t0=t0, v0=v0, gAF0=gAF0):
                        for ic in range(IC + 1):
                            if ic < IC:
                                scalar.wait_ge(cT.h, t0 + (ic + 1))
                                if ic >= 2:
                                    scalar.wait_ge(cV.h, v0 + (ic - 1))
                                cA.inc(scalar.activation(
                                    GS[:, (ic % 2) * TS:(ic % 2 + 1) * TS], bank(ic % 2),
                                    AF.Silu, scale=CONST[:, O_SG + ic:O_SG + ic + 1]))
                            if ic >= 1:
                                j = ic - 1
                                scalar.wait_ge(cV.h, v0 + (j + 1))
                                sl = slice((j % 2) * TS, (j % 2 + 1) * TS)
                                stAF.inc(gAF0 + j, scalar.dma_start(out=fuse_d[j, 0], in_=FH[:, sl]))
                                stAF.inc(gAF0 + j, scalar.dma_start(out=fuse_d[j, 1], in_=FL[:, sl]))
                    block.scalar(fa)

                    def fv(vector, a0=a0, gAF0=gAF0):
                        for ic in range(IC):
                            vector.wait_ge(cA.h, a0 + (ic + 1))
                            if ic >= 2:
                                stAF.wait(vector, gAF0 + ic - 2)
                            sl = slice((ic % 2) * TS, (ic % 2 + 1) * TS)
                            vector.scalar_tensor_tensor(GS[:, sl], bank(2 + ic % 2),
                                                        CONST[:, O_SU + ic:O_SU + ic + 1],
                                                        GS[:, sl], ALU.mult, ALU.mult)
                            vector.tensor_copy(FH[:, sl], GS[:, sl])
                            cV.inc(vector.tensor_tensor(FL[:, sl], GS[:, sl], FH[:, sl],
                                                        ALU.subtract))
                    block.vector(fv)
                    stAF.g += IC
                    af_t = [stAF.c[s].n for s in range(2)]

                    def wait_fuse(eng):
                        for s in range(2):
                            eng.wait_ge(stAF.c[s].h, af_t[s])
                    if "dn" not in parts:
                        def fz(scalar):
                            scalar.wait_ge(sA.h, sA.n)
                        block.scalar(fz)
                        barrier()
                        continue

                    t1, v1 = cT.n, cV.n
                    gF0, gWD0 = stF.g, stWD.g

                    def fs_dn(sync, gF0=gF0, t1=t1):
                        wait_fuse(sync)
                        for i in range(2 * IC):
                            ic = i % IC
                            if i >= 4:
                                sync.wait_ge(cT.h, t1 + (i - 3))
                            b = i % 4
                            stF.inc(gF0 + i, sync.dma_start(out=FHR[:, b * TS:(b + 1) * TS],
                                                            in_=fuse_d[ic, 0]))
                            stF.inc(gF0 + i, sync.dma_start(out=FLR[:, b * TS:(b + 1) * TS],
                                                            in_=fuse_d[ic, 1]))
                    block.sync(fs_dn)

                    def fg_dn(gpsimd, gWD0=gWD0, t1=t1):
                        wait_fuse(gpsimd)
                        for i in range(2 * IC):
                            half, ic = i // IC, i % IC
                            if i >= 2:
                                gpsimd.wait_ge(cT.h, t1 + (i - 1))
                            o = (i % 2) * D
                            stWD.inc(gWD0 + i, gpsimd.dma_start(
                                out=WD[:, o:o + D // 2],
                                in_=wd_t[ic][:, half * (D // 2):(half + 1) * (D // 2)]))
                    block.gpsimd(fg_dn)

                    def ft_dn(tensor, gF0=gF0, gWD0=gWD0, v1=v1):
                        for i in range(2 * IC):
                            half, ic = i // IC, i % IC
                            stF.wait(tensor, gF0 + i)
                            stWD.wait(tensor, gWD0 + i)
                            if half == 1 and ic == 0:
                                tensor.wait_ge(cV.h, v1 + D2C // 2)
                            b = i % 4
                            o = (i % 2) * D
                            for j in range(D2C // 2):
                                py = bank(j)
                                wdc = WD[:, o + j * 128:o + (j + 1) * 128]
                                tensor.matmul(py, wdc, FHR[:, b * TS:(b + 1) * TS],
                                              start=(ic == 0), stop=False)
                                mmi = tensor.matmul(py, wdc, FLR[:, b * TS:(b + 1) * TS],
                                                    start=False, stop=(ic == IC - 1))
                            cT.inc(mmi)
                    block.tensor(ft_dn)
                    stF.g += 2 * IC
                    stWD.g += 2 * IC

                    def fv_dn(vector, t1=t1):
                        for half in range(2):
                            vector.wait_ge(cT.h, t1 + (half + 1) * IC)
                            for j in range(D2C // 2):
                                c = half * (D2C // 2) + j
                                cV.inc(vector.scalar_tensor_tensor(
                                    XB[:, c * TS:(c + 1) * TS], bank(j),
                                    CONST[:, O_SD + c:O_SD + c + 1],
                                    XA[:, c * TS:(c + 1) * TS], ALU.mult, ALU.add))
                    block.vector(fv_dn)
                    v_dn0 = cV.n - D2C

                    def fa3(scalar, v_dn0=v_dn0, ts=ts):
                        for c in range(D2C):
                            scalar.wait_ge(cV.h, v_dn0 + c + 1)
                            sA.inc(scalar.dma_start(out=dst[c][:, ts:ts + TS],
                                                    in_=XB[:, c * TS:(c + 1) * TS]))
                    block.scalar(fa3)
                    if final and it == NT - 1:
                        def fa4(scalar):
                            scalar.wait_ge(sA.h, sA.n)
                        block.scalar(fa4)
                    barrier()

        def p_qkv():
            OCG = min(8, KC)
            NGH = KC // OCG
            NT1 = TS // 128
            VB = min(512, D // 2)
            NDB = (D // 2) // VB
            SLT = max(OCG * 128, D // 2)
            with nc.sbuf_tensor("qk_XA", [128, KC * TS], F32) as XA, \
                 nc.sbuf_tensor("qk_XB", [128, KC * TS], F32) as XB, \
                 nc.sbuf_tensor("qk_WQH", [128, 2 * SLT], BF16) as WQH, \
                 nc.sbuf_tensor("qk_WQL", [128, 2 * SLT], BF16) as WQL, \
                 nc.sbuf_tensor("qk_RC", [128, 2 * TS], F32) as RC, \
                 nc.sbuf_tensor("qk_RS", [128, 2 * TS], F32) as RS, \
                 nc.sbuf_tensor("qk_QH", [128, KC * TS], BF16) as QH, \
                 nc.sbuf_tensor("qk_QL", [128, KC * TS], BF16) as QL, \
                 nc.sbuf_tensor("qk_KH", [128, KC * TS], BF16) as KH, \
                 nc.sbuf_tensor("qk_KL", [128, KC * TS], BF16) as KL:
                for it in range(NT):
                    ts = it * TS
                    sA_prev = sA.n
                    v_ln = emit_ln(ts, xmid, XA, XB)

                    for p in range(2):
                        t2, v2 = cT.n, cV.n
                        gQ0 = stQW.g

                        def fg_qk(gpsimd, gQ0=gQ0, t2=t2, p=p):
                            gpsimd.wait_ge(cT.h, t2)
                            for gi in range(NGH):
                                for kc in range(KC):
                                    i = gi * KC + kc
                                    if i >= 2:
                                        gpsimd.wait_ge(cT.h, t2 + (i - 1))
                                    o = (i % 2) * SLT
                                    co = (p * KC + gi * OCG) * 128
                                    stQW.inc(gQ0 + i, gpsimd.dma_start(
                                        out=WQH[:, o:o + OCG * 128],
                                        in_=qkv_hi[kc][:, co:co + OCG * 128]))
                                    stQW.inc(gQ0 + i, gpsimd.dma_start(
                                        out=WQL[:, o:o + OCG * 128],
                                        in_=qkv_lo[kc][:, co:co + OCG * 128]))
                        block.gpsimd(fg_qk)

                        def ft_qk(tensor, gQ0=gQ0, v2=v2, v_ln=v_ln, p=p):
                            tensor.wait_ge(cV.h, v_ln)
                            for gi in range(NGH):
                                if gi >= 1:
                                    tensor.wait_ge(cV.h, v2 + gi)
                                for kc in range(KC):
                                    i = gi * KC + kc
                                    stQW.wait(tensor, gQ0 + i)
                                    o = (i % 2) * SLT
                                    xh = HI[:, kc * TS:(kc + 1) * TS]
                                    xl = LO[:, kc * TS:(kc + 1) * TS]
                                    for j in range(OCG):
                                        pj = bank(j)
                                        wh = WQH[:, o + j * 128:o + (j + 1) * 128]
                                        wl = WQL[:, o + j * 128:o + (j + 1) * 128]
                                        tensor.matmul(pj, wh, xh, start=(kc == 0), stop=False)
                                        tensor.matmul(pj, wh, xl, start=False, stop=False)
                                        mmi = tensor.matmul(pj, wl, xh, start=False,
                                                            stop=(kc == KC - 1))
                                    cT.inc(mmi)
                        block.tensor(ft_qk)
                        stQW.g += NGH * KC
                        t_qk0 = cT.n - NGH * KC

                        def fv_ev(vector, t_qk0=t_qk0):
                            for gi in range(NGH):
                                vector.wait_ge(cT.h, t_qk0 + (gi + 1) * KC)
                                for j in range(OCG):
                                    oc = gi * OCG + j
                                    vi = vector.tensor_copy(XB[:, oc * TS:(oc + 1) * TS],
                                                            bank(j))
                                cV.inc(vi)
                        block.vector(fv_ev)

                        gR0 = stR.g
                        v_rope0 = cV.n

                        def fv_rope(vector, gR0=gR0, p=p, sA_prev=sA_prev, it=it):
                            if p == 0 and it > 0:
                                vector.wait_ge(sA.h, sA_prev)
                            for c in range(DQC):
                                b = c % 2
                                stR.wait(vector, gR0 + c)
                                s0 = slice(c * TS, (c + 1) * TS)
                                s1_ = slice((DQC + c) * TS, (DQC + c + 1) * TS)
                                rc = RC[:, b * TS:(b + 1) * TS]
                                rs_ = RS[:, b * TS:(b + 1) * TS]
                                vector.tensor_mul(XA[:, 0:TS], XB[:, s1_], rs_)
                                vector.tensor_mul(XA[:, TS:2 * TS], XB[:, s0], rs_)
                                vector.tensor_mul(XB[:, s0], XB[:, s0], rc)
                                vector.tensor_sub(XB[:, s0], XB[:, s0], XA[:, 0:TS])
                                vector.tensor_mul(XB[:, s1_], XB[:, s1_], rc)
                                cV.inc(vector.tensor_add(XB[:, s1_], XB[:, s1_],
                                                         XA[:, TS:2 * TS]))
                            DH_, DL_ = (QH, QL) if p == 0 else (KH, KL)
                            vector.tensor_copy(DH_[:, 0:KC * TS], XB[:, 0:KC * TS])
                            cV.inc(vector.tensor_tensor(DL_[:, 0:KC * TS], XB[:, 0:KC * TS],
                                                        DH_[:, 0:KC * TS], ALU.subtract))
                        block.vector(fv_rope)
                        v_cast = cV.n

                        def fs_rt(sync, ts=ts, v_rope0=v_rope0, gR0=gR0):
                            sync.wait_ge(cV.h, v_rope0)
                            for c in range(DQC):
                                b = c % 2
                                if c >= 2:
                                    sync.wait_ge(cV.h, v_rope0 + (c - 1))
                                stR.inc(gR0 + c, sync.dma_start(out=RC[:, b * TS:(b + 1) * TS],
                                                                in_=cos_t[c][:, ts:ts + TS]))
                                stR.inc(gR0 + c, sync.dma_start(out=RS[:, b * TS:(b + 1) * TS],
                                                                in_=sin_t[c][:, ts:ts + TS]))
                        block.sync(fs_rt)
                        stR.g += DQC

                        def fa_pk(scalar, v_cast=v_cast, ts=ts, p=p, it=it):
                            scalar.wait_ge(cV.h, v_cast)
                            for c in range(KC):
                                if p == 0:
                                    sA.inc(scalar.dma_start(
                                        out=qpark[0, c][:, ts:ts + TS],
                                        in_=QH[:, c * TS:(c + 1) * TS]))
                                    sA.inc(scalar.dma_start(
                                        out=qpark[1, c][:, ts:ts + TS],
                                        in_=QL[:, c * TS:(c + 1) * TS]))
                                else:
                                    sA.inc(scalar.dma_start(
                                        out=k_in[it].ap()[0, c],
                                        in_=KH[:, c * TS:(c + 1) * TS]))
                                    sA.inc(scalar.dma_start(
                                        out=k_in[it].ap()[1, c],
                                        in_=KL[:, c * TS:(c + 1) * TS]))
                        block.scalar(fa_pk)
                        if p == 1:
                            a_kstage = sA.n

                            def fg_agk(gpsimd, a_kstage=a_kstage, it=it):
                                gpsimd.wait_ge(sA.h, a_kstage)
                                cC.inc(gpsimd.collective_compute(
                                    "AllGather", ALU.bypass,
                                    replica_groups=[[2 * i, 2 * i + 1]
                                                    for i in range(NCORES // 2)],
                                    ins=[k_in[it].ap().opt()],
                                    outs=[k_all[it].ap().opt()]))
                            block.gpsimd(fg_agk)
                    a_park = sA.n

                    for dvh in range(2):
                        t3 = cT.n
                        gV0 = stQW.g

                        def fg_v(gpsimd, gV0=gV0, t3=t3, dvh=dvh):
                            gpsimd.wait_ge(cT.h, t3)
                            for kc in range(KC):
                                if kc >= 2:
                                    gpsimd.wait_ge(cT.h, t3 + (kc - 1))
                                o = (kc % 2) * SLT
                                vo = 2 * D + dvh * (D // 2)
                                stQW.inc(gV0 + kc, gpsimd.dma_start(
                                    out=WQH[:, o:o + D // 2],
                                    in_=qkv_hi[kc][:, vo:vo + D // 2]))
                                stQW.inc(gV0 + kc, gpsimd.dma_start(
                                    out=WQL[:, o:o + D // 2],
                                    in_=qkv_lo[kc][:, vo:vo + D // 2]))
                        block.gpsimd(fg_v)

                        def ft_v(tensor, gV0=gV0, dvh=dvh, v=cV.n):
                            tensor.wait_ge(cV.h, v)
                            for kc in range(KC):
                                stQW.wait(tensor, gV0 + kc)
                                o = (kc % 2) * SLT
                                for t1 in range(NT1):
                                    xh = HI[:, kc * TS + t1 * 128:kc * TS + (t1 + 1) * 128]
                                    xl = LO[:, kc * TS + t1 * 128:kc * TS + (t1 + 1) * 128]
                                    for db_ in range(NDB):
                                        pj = bank(t1 * NDB + db_, VB)
                                        wh = WQH[:, o + db_ * VB:o + (db_ + 1) * VB]
                                        wl = WQL[:, o + db_ * VB:o + (db_ + 1) * VB]
                                        tensor.matmul(pj, xh, wh, start=(kc == 0), stop=False)
                                        tensor.matmul(pj, xh, wl, start=False, stop=False)
                                        mmi = tensor.matmul(pj, xl, wh, start=False,
                                                            stop=(kc == KC - 1))
                                cT.inc(mmi)
                        block.tensor(ft_v)
                        stQW.g += KC
                        t_v_end = cT.n

                        def fv_v(vector, t_v_end=t_v_end, a_park=a_park, dvh=dvh):
                            vector.wait_ge(cT.h, t_v_end)
                            if dvh == 0:
                                vector.wait_ge(sA.h, a_park)
                            n = NT1 * NDB
                            for s_ in range(n):
                                sl = slice(s_ * VB, (s_ + 1) * VB)
                                vector.tensor_copy(XB[:, sl], bank(s_, VB))
                            for s_ in range(n):
                                sl = slice(s_ * VB, (s_ + 1) * VB)
                                vector.tensor_copy(QH[:, sl], XB[:, sl])
                                vi = vector.tensor_tensor(QL[:, sl], XB[:, sl], QH[:, sl],
                                                          ALU.subtract)
                            cV.inc(vi)
                        block.vector(fv_v)
                        v_vc = cV.n

                        def fa_v(scalar, v_vc=v_vc, dvh=dvh, ts=ts, it=it):
                            scalar.wait_ge(cV.h, v_vc)
                            for t1 in range(NT1):
                                for db_ in range(NDB):
                                    s_ = t1 * NDB + db_
                                    sl = slice(s_ * VB, (s_ + 1) * VB)
                                    vo = dvh * (D // 2) + db_ * VB
                                    sA.inc(scalar.dma_start(
                                        out=v_in[it].ap()[0, t1][:, vo:vo + VB],
                                        in_=QH[:, sl]))
                                    sA.inc(scalar.dma_start(
                                        out=v_in[it].ap()[1, t1][:, vo:vo + VB],
                                        in_=QL[:, sl]))
                        block.scalar(fa_v)
                        if dvh == 1:
                            a_vstage = sA.n

                            def fg_agv(gpsimd, a_vstage=a_vstage, it=it):
                                gpsimd.wait_ge(sA.h, a_vstage)
                                cC.inc(gpsimd.collective_compute(
                                    "AllGather", ALU.bypass,
                                    replica_groups=[[2 * i, 2 * i + 1]
                                                    for i in range(NCORES // 2)],
                                    ins=[v_in[it].ap().opt()],
                                    outs=[v_all[it].ap().opt()]))
                            block.gpsimd(fg_agv)
                    barrier()

                gNB0 = stNB.g
                a_vdone = sA.n
                v_nb0 = cV.n
                a_nb0 = cA.n
                t_nb0 = cT.n
                steps = []
                for it2 in range(NT):
                    steps.append(("q", it2))
                    steps.append(("k", it2))

                def fs_nb(sync, gNB0=gNB0, a_vdone=a_vdone, a_nb0=a_nb0):
                    sync.wait_ge(sA.h, a_vdone)
                    for s_, (qk, it2) in enumerate(steps):
                        if s_ >= 2:
                            sync.wait_ge(cA.h, a_nb0 + s_ - 1)
                        buf = QH if s_ % 2 == 0 else QL
                        for c in range(KC):
                            srcap = (qpark[0, c][:, it2 * TS:(it2 + 1) * TS] if qk == "q"
                                     else k_in[it2].ap()[0, c])
                            stNB.inc(gNB0 + s_, sync.dma_start(
                                out=buf[:, c * TS:(c + 1) * TS], in_=srcap))
                block.sync(fs_nb)
                stNB.g += len(steps)

                def fa_nb(scalar, gNB0=gNB0, t_nb0=t_nb0):
                    for s_, (qk, it2) in enumerate(steps):
                        stNB.wait(scalar, gNB0 + s_)
                        if s_ >= 1:
                            scalar.wait_ge(cT.h, t_nb0 + s_)
                        buf = QH if s_ % 2 == 0 else QL
                        cA.inc(scalar.square(KH[:, 0:KC * TS], buf[:, 0:KC * TS]))
                block.scalar(fa_nb)

                def ft_nb(tensor, a_nb0=a_nb0, v_nb0=v_nb0):
                    tensor.wait_ge(cV.h, v_nb0)
                    for s_, (qk, it2) in enumerate(steps):
                        tensor.wait_ge(cA.h, a_nb0 + s_ + 1)
                        bk = 4 if qk == "q" else 5
                        if s_ >= 2:
                            tensor.wait_ge(cV.h, v_nb0 + s_ - 1)
                        for c in range(KC):
                            mmi = tensor.matmul(bankrow(bk), ONESB[:, :],
                                                KH[:, c * TS:(c + 1) * TS],
                                                start=(c == 0), stop=(c == KC - 1))
                        cT.inc(mmi)
                block.tensor(ft_nb)

                def fv_nb(vector, t_nb0=t_nb0, v_nb0=v_nb0):
                    for s_, (qk, it2) in enumerate(steps):
                        vector.wait_ge(cT.h, t_nb0 + s_ + 1)
                        bk = 4 if qk == "q" else 5
                        m = 0 if qk == "q" else 1
                        vector.reduce_max(MROW[:, 4:5], bankrow(bk), axis=AXX)
                        cV.inc(vector.tensor_max(MROW[:, m:m + 1], MROW[:, m:m + 1],
                                                 MROW[:, 4:5]))
                block.vector(fv_nb)

                def fv_m(vector):
                    cV.inc(vector.tensor_mul(MROW[:, 2:3], MROW[:, 0:1], MROW[:, 1:2]))
                block.vector(fv_m)

                def fa_m(scalar, v=cV.n):
                    scalar.wait_ge(cV.h, v)
                    scalar.activation(MROW[:, 3:4], MROW[:, 2:3], AF.Sqrt)
                    cA.inc(scalar.mul(MROW[:, 3:4], MROW[:, 3:4], -float(cfg.SC)))
                block.scalar(fa_m)

                def ft_m(tensor, a=cA.n):
                    tensor.wait_ge(cA.h, a)
                    mmi = tensor.matmul(PS[:, 0:1], ONESR[:, :], MROW[:, 3:4],
                                        start=True, stop=True)
                    cT.inc(mmi)
                block.tensor(ft_m)

                def fv_m2(vector, t=cT.n):
                    vector.wait_ge(cT.h, t)
                    cV.inc(vector.tensor_copy(CONST[:, O_NM:O_NM + 1], PS[:, 0:1]))
                block.vector(fv_m2)
                barrier()

        def p_collective():
            if debug:
                def fg2(gpsimd):
                    gpsimd.wait_ge(cC.h, cC.n)
                    for i in range(NT):
                        gD.inc(gpsimd.dma_start(out=dbg["dbg_k"][i], in_=k_in[i].ap()))
                        gD.inc(gpsimd.dma_start(out=dbg["dbg_v"][i], in_=v_in[i].ap()))
                    gD.inc(gpsimd.dma_start(out=dbg["dbg_q"], in_=qpark))
                    gpsimd.wait_ge(gD.h, gD.n)
                block.gpsimd(fg2)
                barrier()

        def p_attn():
            NKSL = S // 512
            NB = 512 // 128
            with nc.sbuf_tensor("at_EH", [128, T2C * QB], BF16) as EH, \
                 nc.sbuf_tensor("at_EL", [128, T2C * QB], BF16) as EL, \
                 nc.sbuf_tensor("at_QSH", [128, DQC * QB], BF16) as QSH, \
                 nc.sbuf_tensor("at_QSL", [128, DQC * QB], BF16) as QSL, \
                 nc.sbuf_tensor("at_KSH", [128, 2 * DQC * 512], BF16) as KSH, \
                 nc.sbuf_tensor("at_KSL", [128, 2 * DQC * 512], BF16) as KSL, \
                 nc.sbuf_tensor("at_VSH", [128, 2 * DH], BF16) as VSH, \
                 nc.sbuf_tensor("at_VSL", [128, 2 * DH], BF16) as VSL, \
                 nc.sbuf_tensor("at_WOH", [128, 2 * (D // 2)], BF16) as WOH, \
                 nc.sbuf_tensor("at_WOL", [128, 2 * (D // 2)], BF16) as WOL, \
                 nc.sbuf_tensor("at_E32", [128, 2 * 512], F32) as E32, \
                 nc.sbuf_tensor("at_RB", [128, QB], F32) as RB, \
                 nc.sbuf_tensor("at_OB", [128, DQC * QB], F32) as OB, \
                 nc.sbuf_tensor("at_XS", [128, 2 * 512], F32) as XS, \
                 nc.sbuf_tensor("at_YS", [128, 2 * 512], F32) as YS:
                prev_t_av = [0]
                for qb in range(NQB):
                    tq = qb * QB
                    for h in range(H):
                        v_prev = cV.n
                        gQb = stQ.g
                        gKb = stK.g

                        def fs_q(sync, tq=tq, h=h, v_prev=v_prev, qb=qb, gQb=gQb):
                            if not (qb == 0 and h == 0):
                                sync.wait_ge(cV.h, v_prev)
                            for ver, dst_ in ((0, QSH), (1, QSL)):
                                for c in range(DQC):
                                    stQ.inc(gQb, sync.dma_start(
                                        out=dst_[:, c * QB:(c + 1) * QB],
                                        in_=qpark[ver, h * DQC + c][:, tq:tq + QB]))
                        block.sync(fs_q)
                        stQ.g += 1

                        t4, v4, a4 = cT.n, cV.n, cA.n

                        def fs_k(sync, gKb=gKb, t4=t4, h=h, qb=qb):
                            for sl_ in range(NKSL):
                                r, lo_ = sl_ // (NKSL // 2), (sl_ % (NKSL // 2)) * 512
                                kt, off = lo_ // TS, lo_ % TS
                                if qb == 0 and h == 0:
                                    sync.wait_ge(cC.h, 2 * kt + 1)
                                if sl_ >= 2:
                                    sync.wait_ge(cT.h, t4 + (sl_ - 1) * NB)
                                ko = (sl_ % 2) * DQC * 512
                                for ver, dst_ in ((0, KSH), (1, KSL)):
                                    for c in range(DQC):
                                        stK.inc(gKb + sl_, sync.dma_start(
                                            out=dst_[:, ko + c * 512:ko + (c + 1) * 512],
                                            in_=k_all[kt].ap()[r, ver, h * DQC + c]
                                                [:, off:off + 512]))
                        block.sync(fs_k)
                        stK.g += NKSL

                        def ft_s(tensor, gQb=gQb, gKb=gKb, a4=a4, v4=v4, v_prev=v_prev,
                                 qb=qb, h=h):
                            if not (qb == 0 and h == 0):
                                tensor.wait_ge(cV.h, v_prev)
                            stQ.wait(tensor, gQb)
                            for sl_ in range(NKSL):
                                stK.wait(tensor, gKb + sl_)
                                ko = (sl_ % 2) * DQC * 512
                                for b in range(NB):
                                    t2c = sl_ * NB + b
                                    pj = bank(t2c % 4, QB)
                                    if t2c >= 4:
                                        tensor.wait_ge(cA.h, a4 + (t2c - 3))
                                    for c in range(DQC):
                                        kh = KSH[:, ko + c * 512 + b * 128:ko + c * 512 + (b + 1) * 128]
                                        kl = KSL[:, ko + c * 512 + b * 128:ko + c * 512 + (b + 1) * 128]
                                        qh = QSH[:, c * QB:(c + 1) * QB]
                                        ql = QSL[:, c * QB:(c + 1) * QB]
                                        tensor.matmul(pj, kh, qh, start=(c == 0), stop=False)
                                        tensor.matmul(pj, kh, ql, start=False, stop=False)
                                        mmi = tensor.matmul(pj, kl, qh, start=False,
                                                            stop=(c == DQC - 1))
                                    cT.inc(mmi)
                            for t2c in range(T2C):
                                tensor.wait_ge(cV.h, v4 + t2c + 1)
                                tensor.matmul(bankrow(4, QB), ONESB[:, :],
                                              EH[:, t2c * QB:(t2c + 1) * QB],
                                              start=(t2c == 0), stop=False)
                                mmi = tensor.matmul(bankrow(4, QB), ONESB[:, :],
                                                    EL[:, t2c * QB:(t2c + 1) * QB],
                                                    start=False, stop=(t2c == T2C - 1))
                            cT.inc(mmi)
                        block.tensor(ft_s)
                        t_sums = cT.n

                        def fa_e(scalar, t4=t4, v4=v4):
                            for t2c in range(T2C):
                                scalar.wait_ge(cT.h, t4 + t2c + 1)
                                if t2c >= 2:
                                    scalar.wait_ge(cV.h, v4 + t2c - 1)
                                cA.inc(scalar.activation(
                                    E32[:, (t2c % 2) * 512:(t2c % 2) * 512 + QB],
                                    bank(t2c % 4, QB), AF.Exp,
                                    bias=CONST[:, O_NM:O_NM + 1], scale=float(cfg.SC)))
                        block.scalar(fa_e)

                        def fv_e(vector, a4=a4):
                            for t2c in range(T2C):
                                vector.wait_ge(cA.h, a4 + t2c + 1)
                                sl_ = slice(t2c * QB, (t2c + 1) * QB)
                                e3 = E32[:, (t2c % 2) * 512:(t2c % 2) * 512 + QB]
                                vector.tensor_copy(EH[:, sl_], e3)
                                cV.inc(vector.tensor_tensor(EL[:, sl_], e3, EH[:, sl_],
                                                            ALU.subtract))
                        block.vector(fv_e)

                        def fv_r(vector, t=t_sums):
                            vector.wait_ge(cT.h, t)
                            cV.inc(vector.reciprocal(SMALL[:, 0:QB], bankrow(4, QB)))
                        block.vector(fv_r)

                        def ft_rb(tensor, v=cV.n):
                            tensor.wait_ge(cV.h, v)
                            mmi = tensor.matmul(bank(5, QB), ONESR[:, :], SMALL[:, 0:QB],
                                                start=True, stop=True)
                            cT.inc(mmi)
                        block.tensor(ft_rb)

                        def fv_rb(vector, t=cT.n):
                            vector.wait_ge(cT.h, t)
                            cV.inc(vector.tensor_copy(RB[:, 0:QB], bank(5, QB)))
                        block.vector(fv_rb)
                        v_rb = cV.n

                        gVb = stV.g
                        tav_base = cT.n

                        def fg_v(gpsimd, gVb=gVb, t5=prev_t_av[0], tav_base=tav_base,
                                 h=h, qb=qb):
                            gpsimd.wait_ge(cT.h, t5)
                            for t2c in range(T2C):
                                r, lc = t2c // (T2C // 2), t2c % (T2C // 2)
                                vt, t1l = lc // NT1_, lc % NT1_
                                if qb == 0 and h == 0:
                                    gpsimd.wait_ge(cC.h, 2 * vt + 2)
                                if t2c >= 2:
                                    gpsimd.wait_ge(cT.h, tav_base + t2c - 1)
                                o = (t2c % 2) * DH
                                stV.inc(gVb + t2c, gpsimd.dma_start(
                                    out=VSH[:, o:o + DH],
                                    in_=v_all[vt].ap()[r, 0, t1l][:, h * DH:(h + 1) * DH]))
                                stV.inc(gVb + t2c, gpsimd.dma_start(
                                    out=VSL[:, o:o + DH],
                                    in_=v_all[vt].ap()[r, 1, t1l][:, h * DH:(h + 1) * DH]))
                        block.gpsimd(fg_v)

                        def ft_av(tensor, gVb=gVb, v_rb=v_rb):
                            tensor.wait_ge(cV.h, v_rb)
                            for t2c in range(T2C):
                                stV.wait(tensor, gVb + t2c)
                                o = (t2c % 2) * DH
                                eh = EH[:, t2c * QB:(t2c + 1) * QB]
                                el = EL[:, t2c * QB:(t2c + 1) * QB]
                                for c in range(DQC):
                                    pj = bank(c, QB)
                                    vh = VSH[:, o + c * 128:o + (c + 1) * 128]
                                    vl = VSL[:, o + c * 128:o + (c + 1) * 128]
                                    tensor.matmul(pj, vh, eh, start=(t2c == 0), stop=False)
                                    tensor.matmul(pj, vh, el, start=False, stop=False)
                                    mmi = tensor.matmul(pj, vl, eh, start=False,
                                                        stop=(t2c == T2C - 1))
                                cT.inc(mmi)
                        block.tensor(ft_av)
                        stV.g += T2C
                        t_av = cT.n
                        prev_t_av[0] = t_av

                        def fv_o(vector, t_av=t_av, v_rb=v_rb, h=h):
                            vector.wait_ge(cT.h, t_av)
                            vector.wait_ge(cV.h, v_rb)
                            for c in range(DQC):
                                sl_ = slice(c * QB, (c + 1) * QB)
                                vector.tensor_mul(OB[:, sl_], bank(c, QB), RB[:, 0:QB])
                                g = h * DQC + c
                                vector.tensor_copy(HI[:, g * QB:(g + 1) * QB], OB[:, sl_])
                                vi = vector.tensor_tensor(LO[:, g * QB:(g + 1) * QB], OB[:, sl_],
                                                          HI[:, g * QB:(g + 1) * QB],
                                                          ALU.subtract)
                            cV.inc(vi)
                        block.vector(fv_o)

                    v_oh = cV.n
                    for d2h in range(2):
                        t6 = cT.n
                        gWb = stQW.g

                        def fg_wo(gpsimd, gWb=gWb, t6=t6, d2h=d2h):
                            gpsimd.wait_ge(cT.h, t6)
                            for dvc in range(KC):
                                if dvc >= 2:
                                    gpsimd.wait_ge(cT.h, t6 + dvc - 1)
                                o = (dvc % 2) * (D // 2)
                                stQW.inc(gWb + dvc, gpsimd.dma_start(
                                    out=WOH[:, o:o + D // 2],
                                    in_=wo_hi[dvc][:, d2h * (D // 2):(d2h + 1) * (D // 2)]))
                                stQW.inc(gWb + dvc, gpsimd.dma_start(
                                    out=WOL[:, o:o + D // 2],
                                    in_=wo_lo[dvc][:, d2h * (D // 2):(d2h + 1) * (D // 2)]))
                        block.gpsimd(fg_wo)

                        def ft_wo(tensor, gWb=gWb, v_oh=v_oh, v=cV.n):
                            tensor.wait_ge(cV.h, max(v_oh, v))
                            for dvc in range(KC):
                                stQW.wait(tensor, gWb + dvc)
                                o = (dvc % 2) * (D // 2)
                                oh = HI[:, dvc * QB:(dvc + 1) * QB]
                                ol = LO[:, dvc * QB:(dvc + 1) * QB]
                                for j in range(D2C // 2):
                                    pj = bank(j, QB)
                                    wh = WOH[:, o + j * 128:o + (j + 1) * 128]
                                    wl = WOL[:, o + j * 128:o + (j + 1) * 128]
                                    tensor.matmul(pj, wh, oh, start=(dvc == 0), stop=False)
                                    tensor.matmul(pj, wh, ol, start=False, stop=False)
                                    mmi = tensor.matmul(pj, wl, oh, start=False,
                                                        stop=(dvc == KC - 1))
                                cT.inc(mmi)
                        block.tensor(ft_wo)
                        stQW.g += KC
                        t_wo = cT.n

                        v_ev0 = cV.n
                        gXb = stX.g

                        def fs_x(sync, tq=tq, d2h=d2h, v_ev0=v_ev0, gXb=gXb):
                            sync.wait_ge(cV.h, v_ev0)
                            for j in range(D2C // 2):
                                c = d2h * (D2C // 2) + j
                                if j >= 2:
                                    sync.wait_ge(cV.h, v_ev0 + j - 1)
                                stX.inc(gXb + j, sync.dma_start(
                                    out=XS[:, (j % 2) * 512:(j % 2) * 512 + QB],
                                    in_=xmid[c][:, tq:tq + QB]))
                        block.sync(fs_x)
                        stX.g += D2C // 2

                        def fv_y(vector, t_wo=t_wo, gXb=gXb, d2h=d2h, s_ys=sA.n):
                            vector.wait_ge(cT.h, t_wo)
                            if d2h > 0 or True:
                                vector.wait_ge(sA.h, s_ys)
                            for j in range(D2C // 2):
                                c = d2h * (D2C // 2) + j
                                stX.wait(vector, gXb + j)
                                cV.inc(vector.scalar_tensor_tensor(
                                    YS[:, (j % 2) * 512:(j % 2) * 512 + QB], bank(j, QB),
                                    CONST[:, O_BO + c:O_BO + c + 1],
                                    XS[:, (j % 2) * 512:(j % 2) * 512 + QB],
                                    ALU.add, ALU.add))
                        block.vector(fv_y)
                        v_y0 = cV.n - D2C // 2

                        def fa_st(scalar, v_y0=v_y0, tq=tq, d2h=d2h):
                            for j in range(D2C // 2):
                                c = d2h * (D2C // 2) + j
                                scalar.wait_ge(cV.h, v_y0 + j + 1)
                                sA.inc(scalar.dma_start(
                                    out=xmid2[c][:, tq:tq + QB],
                                    in_=YS[:, (j % 2) * 512:(j % 2) * 512 + QB]))
                        block.scalar(fa_st)
                    barrier()
                if debug:
                    def fg3(gpsimd):
                        gD.inc(gpsimd.dma_start(out=dbg["dbg_xmid2"], in_=xmid2))
                        gpsimd.wait_ge(gD.h, gD.n)
                    block.gpsimd(fg3)
                    barrier()

        PH = phases if phases is not None else ["consts", "wconv", "ffn1", "qkv",
                                                "cc", "attn", "ffn2"]
        if "consts" in PH:
            p_consts()
        if "wconv" in PH:
            p_wconv()
        if "ffn1" in PH:
            p_ffn(x_t, xmid, "f1")
        if "ffn1ln" in PH:
            p_ffn(x_t, xmid, "f1", parts=("ln",))
        if "ffn1gu" in PH:
            p_ffn(x_t, xmid, "f1", parts=("ln", "gu"))
        if debug:
            def fgd(gpsimd):
                gD.inc(gpsimd.dma_start(out=dbg["dbg_xmid"], in_=xmid))
                gpsimd.wait_ge(gD.h, gD.n)
            block.gpsimd(fgd)
            barrier()
        if "qkv" in PH:
            p_qkv()
        if "cc" in PH:
            p_collective()
        if "attn" in PH:
            p_attn()
        if "ffn2" in PH:
            p_ffn(xmid2, out_t, "f2", final=True)

    return nc



def _rope_tables(seq_len, head_dim):
    k = np.arange(0, head_dim, 2, dtype=np.float32) / np.float32(head_dim)
    inv_freq = (1.0 / (10000.0 ** k)).astype(np.float32)
    t = np.arange(seq_len, dtype=np.float32)
    freqs = (t[:, None] * inv_freq[None, :]).astype(np.float32)
    emb = np.concatenate([freqs, freqs], axis=-1)
    cos = np.cos(emb.astype(np.float64)).astype(np.float32)
    sin = np.sin(emb.astype(np.float64)).astype(np.float32)
    return cos, sin


def make_inputs(cfg, b, half, x, wqkv, wout, bout, w_gate, w_up, w_down,
                s_gate, s_up, s_down):
    D, I, T, KC, IC, DQC = cfg.D, cfg.I, cfg.T, cfg.KC, cfg.IC, cfg.DQC
    cos, sin = _rope_tables(cfg.S, cfg.DH)
    sl = slice(half * T, (half + 1) * T)
    return {
        "x_t": np.ascontiguousarray(
            np.asarray(x[b, sl, :], dtype=np.float32).T.reshape(KC, 128, T)),
        "wqkv_t": np.ascontiguousarray(np.asarray(wqkv, np.float32).reshape(KC, 128, 3 * D)),
        "wout_t": np.ascontiguousarray(np.asarray(wout, np.float32).reshape(KC, 128, D)),
        "wg_t": np.ascontiguousarray(
            np.asarray(w_gate).reshape(KC, 128, IC, 128).transpose(2, 1, 0, 3)
            .reshape(IC, 128, D)),
        "wu_t": np.ascontiguousarray(
            np.asarray(w_up).reshape(KC, 128, IC, 128).transpose(2, 1, 0, 3)
            .reshape(IC, 128, D)),
        "wd_t": np.ascontiguousarray(np.asarray(w_down).reshape(IC, 128, D)),
        "sg_pp": np.ascontiguousarray(np.asarray(s_gate, np.float32).reshape(IC, 128).T),
        "su_pp": np.ascontiguousarray(np.asarray(s_up, np.float32).reshape(IC, 128).T),
        "sd_pp": np.ascontiguousarray(np.asarray(s_down, np.float32).reshape(KC, 128).T),
        "bout_pp": np.ascontiguousarray(np.asarray(bout, np.float32).reshape(KC, 128).T),
        "cos_t": np.ascontiguousarray(cos[sl, :].T.reshape(DQC, 128, T)),
        "sin_t": np.ascontiguousarray(sin[sl, :].T.reshape(DQC, 128, T)),
    }


def kernel(L0_attn_x, L0_attn_wqkv, L0_attn_wout, L0_attn_bout, L0_ffn_w_gate,
           L0_ffn_w_up, L0_ffn_w_down, L0_ffn_s_gate, L0_ffn_s_up, L0_ffn_s_down):
    cfg = CFG()
    x = np.asarray(L0_attn_x, dtype=np.float32)
    B = x.shape[0]
    nc = build_program(cfg)
    in_maps = []
    for core in range(NCORES):
        b, half = core // 2, core % 2
        in_maps.append(make_inputs(cfg, b, half, x, L0_attn_wqkv, L0_attn_wout,
                                   L0_attn_bout, L0_ffn_w_gate, L0_ffn_w_up,
                                   L0_ffn_w_down, L0_ffn_s_gate, L0_ffn_s_up,
                                   L0_ffn_s_down))
    res = run_bass_kernel_spmd(nc, in_maps, core_ids=list(range(NCORES)))
    out = np.empty((B, cfg.S, cfg.D), dtype=np.float32)
    for core in range(NCORES):
        b, half = core // 2, core % 2
        o = res.results[core]["out_t"].reshape(cfg.D, cfg.T)
        out[b, half * cfg.T:(half + 1) * cfg.T, :] = o.T
    return out


# version 09
# speedup vs baseline: 1.0428x; 1.0428x over previous
import contextlib
import math

import numpy as np
import ml_dtypes

import concourse.bass as bass
import concourse.mybir as mybir
from concourse.bass_utils import run_bass_kernel_spmd

F32 = mybir.dt.float32
BF16 = mybir.dt.bfloat16
I8 = mybir.dt.int8
AF = mybir.ActivationFunctionType
ALU = mybir.AluOpType
AXX = mybir.AxisListType.X

NCORES = 8


class CFG:
    def __init__(self, D=2048, I=8192, T=2048, H=2, TS=512, QB=512, eps=1e-5):
        self.D, self.I, self.T, self.H = D, I, T, H
        self.S = 2 * T
        self.DH = D // H
        self.TS, self.QB = TS, QB
        self.KC = D // 128
        self.IC = I // 128
        self.NT = T // TS
        self.NQB = T // QB
        self.T2C = self.S // 128
        self.DQC = self.DH // 128
        self.SC = 1.0 / math.sqrt(self.DH)
        self.eps = eps


def build_program(cfg: CFG, debug=False, phases=None):
    nc = bass.Bass("TRN2", target_bir_lowering=False, detect_race_conditions=False)
    D, I, T, H = cfg.D, cfg.I, cfg.T, cfg.H
    KC, IC, TS, NT = cfg.KC, cfg.IC, cfg.TS, cfg.NT
    QB, NQB, T2C, DQC = cfg.QB, cfg.NQB, cfg.T2C, cfg.DQC
    DH, S = cfg.DH, cfg.S
    D2C = KC

    x_t = nc.dram_tensor("x_t", [KC, 128, T], F32, kind="ExternalInput").ap()
    wqkv_t = nc.dram_tensor("wqkv_t", [KC, 128, 3 * D], F32, kind="ExternalInput").ap()
    wout_t = nc.dram_tensor("wout_t", [KC, 128, D], F32, kind="ExternalInput").ap()
    wg_t = nc.dram_tensor("wg_t", [IC, 128, D], I8, kind="ExternalInput").ap()
    wu_t = nc.dram_tensor("wu_t", [IC, 128, D], I8, kind="ExternalInput").ap()
    wd_t = nc.dram_tensor("wd_t", [IC, 128, D], I8, kind="ExternalInput").ap()
    sg_pp = nc.dram_tensor("sg_pp", [128, IC], F32, kind="ExternalInput").ap()
    su_pp = nc.dram_tensor("su_pp", [128, IC], F32, kind="ExternalInput").ap()
    sd_pp = nc.dram_tensor("sd_pp", [128, KC], F32, kind="ExternalInput").ap()
    bout_pp = nc.dram_tensor("bout_pp", [128, KC], F32, kind="ExternalInput").ap()
    cos_t = nc.dram_tensor("cos_t", [DQC, 128, T], F32, kind="ExternalInput").ap()
    sin_t = nc.dram_tensor("sin_t", [DQC, 128, T], F32, kind="ExternalInput").ap()

    out_t = nc.dram_tensor("out_t", [KC, 128, T], F32, kind="ExternalOutput").ap()

    xmid = nc.dram_tensor("xmid", [KC, 128, T], F32, kind="Internal").ap()
    xmid2 = nc.dram_tensor("xmid2", [KC, 128, T], F32, kind="Internal").ap()
    qkv_hi = nc.dram_tensor("qkv_hi", [KC, 128, 3 * D], BF16, kind="Internal").ap()
    qkv_lo = nc.dram_tensor("qkv_lo", [KC, 128, 3 * D], BF16, kind="Internal").ap()
    wo_hi = nc.dram_tensor("wo_hi", [KC, 128, D], BF16, kind="Internal").ap()
    wo_lo = nc.dram_tensor("wo_lo", [KC, 128, D], BF16, kind="Internal").ap()
    fuse_d = nc.dram_tensor("fuse_d", [IC, 2, 128, TS], BF16, kind="Internal").ap()
    qpark = nc.dram_tensor("qpark", [2, KC, 128, T], BF16, kind="Internal").ap()
    NT_ = T // TS
    NT1_ = TS // 128
    k_in = [nc.dram_tensor(f"k_in{i}", [2, KC, 128, TS], BF16, kind="Internal")
            for i in range(NT_)]
    k_all = [nc.dram_tensor(f"k_all{i}", [2, 2, KC, 128, TS], BF16, kind="Internal")
             for i in range(NT_)]
    v_in = [nc.dram_tensor(f"v_in{i}", [2, NT1_, 128, D], BF16, kind="Internal")
            for i in range(NT_)]
    v_all = [nc.dram_tensor(f"v_all{i}", [2, 2, NT1_, 128, D], BF16, kind="Internal")
             for i in range(NT_)]

    dbg = {}
    if debug:
        dbg["dbg_xmid"] = nc.dram_tensor("dbg_xmid", [KC, 128, T], F32,
                                         kind="ExternalOutput").ap()
        dbg["dbg_xmid2"] = nc.dram_tensor("dbg_xmid2", [KC, 128, T], F32,
                                          kind="ExternalOutput").ap()
        dbg["dbg_k"] = nc.dram_tensor("dbg_k", [NT_, 2, KC, 128, TS], BF16,
                                      kind="ExternalOutput").ap()
        dbg["dbg_v"] = nc.dram_tensor("dbg_v", [NT_, 2, NT1_, 128, D], BF16,
                                      kind="ExternalOutput").ap()
        dbg["dbg_q"] = nc.dram_tensor("dbg_q", [2, KC, 128, T], BF16,
                                      kind="ExternalOutput").ap()

    ones_col_f32 = nc.inline_tensor(np.ones((128, 1), dtype=np.float32), "ones_col_f32")
    ones_row_f32 = nc.inline_tensor(np.ones((1, 128), dtype=np.float32), "ones_row_f32")
    ones_col_bf = nc.inline_tensor(np.ones((128, 1), dtype=ml_dtypes.bfloat16), "ones_col_bf")

    class Ctr:
        def __init__(self, handle, step=1):
            self.h, self.n, self.step = handle, 0, step

        def inc(self, ins, k=None):
            k = self.step if k is None else k
            self.n += k
            ins.then_inc(self.h, k)
            return self.n

    class DStream:

        def __init__(self, ctrs, dpg):
            self.c = ctrs
            self.dpg = dpg
            self.g = 0

        def inc(self, g, ins):
            self.c[g % len(self.c)].inc(ins)

        def wait(self, eng, g):
            eng.wait_ge(self.c[g % len(self.c)].h,
                        16 * self.dpg * (g // len(self.c) + 1))

    with contextlib.ExitStack() as stk:
        e = stk.enter_context

        O_SG, O_SU = 0, IC
        O_SD, O_BO = 2 * IC, 2 * IC + D2C
        O_OC = 2 * IC + 2 * D2C
        O_NM = O_OC + 1
        CONST = e(nc.sbuf_tensor("CONST", [128, O_NM + 1], F32))
        ONESB = e(nc.sbuf_tensor("ONESB", [128, 1], BF16))
        ONESR = e(nc.sbuf_tensor("ONESR", [1, 128], F32))
        SMALL = e(nc.sbuf_tensor("SMALL", [1, 4 * max(TS, QB)], F32))
        MROW = e(nc.sbuf_tensor("MROW", [1, 8], F32))
        HI = e(nc.sbuf_tensor("HI", [128, KC * TS], BF16))
        LO = e(nc.sbuf_tensor("LO", [128, KC * TS], BF16))
        PS = e(nc.psum_tensor("PS", [128, 8 * 512], F32))

        def bank(j, w=None):
            w = TS if w is None else w
            return PS[:, j * 512:j * 512 + w]

        def bankrow(j, w=None):
            w = TS if w is None else w
            return PS[0:1, j * 512:j * 512 + w]

        sS = Ctr(e(nc.semaphore("sS")), 16)
        sA = Ctr(e(nc.semaphore("sA")), 16)
        gD = Ctr(e(nc.semaphore("gD")), 16)
        cT = Ctr(e(nc.semaphore("cT")))
        cV = Ctr(e(nc.semaphore("cV")))
        cA = Ctr(e(nc.semaphore("cA")))
        cC = Ctr(e(nc.semaphore("cC")))
        bar = {n: Ctr(e(nc.semaphore("b" + n))) for n in "STVAG"}

        def mkst(name, ns, dpg):
            return DStream([Ctr(e(nc.semaphore(f"{name}{i}")), 16)
                            for i in range(ns)], dpg)

        stL = mkst("stL", 2, 1)
        stWC = mkst("stWC", 2, 2)
        stW = mkst("stW", 2, 2)
        stWD = mkst("stWD", 2, 1)
        stF = mkst("stF", 4, 2)
        stAF = mkst("stAF", 2, 2)
        stR = mkst("stR", 2, 2)
        stQW = mkst("stQW", 2, 2)
        stK = mkst("stK", 2, 2 * DQC)
        stQ = mkst("stQ", 1, 2 * DQC)
        stV = mkst("stV", 2, 2)
        stX = mkst("stX", 2, 1)
        stNB = mkst("stNB", 2, KC)
        block = e(nc.Block())
        ENG = {"S": block.sync, "T": block.tensor, "V": block.vector,
               "A": block.scalar, "G": block.gpsimd}

        nbar = [0]

        def barrier():
            nbar[0] += 1
            k = nbar[0]
            for name in "STVAG":
                def f(eng, name=name, k=k):
                    eng.nop().then_inc(bar[name].h, 1)
                    for o in "STVAG":
                        if o != name:
                            eng.wait_ge(bar[o].h, k)
                ENG[name](f)
                bar[name].n += 1

        def p_consts():
            def fsync(sync):
                for dst_, src_ in [(CONST[:, O_SG:O_SG + IC], sg_pp),
                                   (CONST[:, O_SU:O_SU + IC], su_pp),
                                   (CONST[:, O_SD:O_SD + D2C], sd_pp),
                                   (CONST[:, O_BO:O_BO + D2C], bout_pp),
                                   (CONST[:, O_OC:O_OC + 1], ones_col_f32.ap()),
                                   (ONESB[:, :], ones_col_bf.ap()),
                                   (ONESR[:, :], ones_row_f32.ap())]:
                    sS.inc(sync.dma_start(out=dst_, in_=src_))
                sync.wait_ge(sS.h, sS.n)
            block.sync(fsync)

            def fscal(scalar):
                scalar.wait_ge(sS.h, sS.n)
                cA.inc(scalar.mul(CONST[:, O_SD:O_SD + D2C],
                                  CONST[:, O_SD:O_SD + D2C], 0.5))
            block.scalar(fscal)

            def fvec(vector):
                vector.memset(MROW[:, :], 0.0)
                cV.inc(vector.memset(MROW[:, 7:8], float(cfg.eps)))
            block.vector(fvec)
            barrier()

        def p_wconv():
            W = 3 * D
            tiles = [(wqkv_t, qkv_hi, qkv_lo, 3 * D, kc) for kc in range(KC)] + \
                    [(wout_t, wo_hi, wo_lo, D, kc) for kc in range(KC)]
            with nc.sbuf_tensor("cv_W32", [128, 2 * W], F32) as W32, \
                 nc.sbuf_tensor("cv_WHI", [128, 2 * W], BF16) as WHI, \
                 nc.sbuf_tensor("cv_WLO", [128, 2 * W], BF16) as WLO:
                cV0 = cV.n
                gL, gC = stL.g, stWC.g

                def fsync(sync):
                    for i, (src, _, _, w, kc) in enumerate(tiles):
                        if i >= 2:
                            sync.wait_ge(cV.h, cV0 + (i - 1))
                        stL.inc(gL + i, sync.dma_start(
                            out=W32[:, (i % 2) * W:(i % 2) * W + w], in_=src[kc]))
                block.sync(fsync)

                def fvec(vector):
                    for i, (_, _, _, w, kc) in enumerate(tiles):
                        o = (i % 2) * W
                        stL.wait(vector, gL + i)
                        if i >= 2:
                            stWC.wait(vector, gC + i - 2)
                        vector.tensor_copy(WHI[:, o:o + w], W32[:, o:o + w])
                        cV.inc(vector.tensor_tensor(WLO[:, o:o + w], W32[:, o:o + w],
                                                    WHI[:, o:o + w], ALU.subtract))
                block.vector(fvec)

                def fscal(scalar):
                    for i, (_, dhi, dlo, w, kc) in enumerate(tiles):
                        o = (i % 2) * W
                        scalar.wait_ge(cV.h, cV0 + (i + 1))
                        stWC.inc(gC + i, scalar.dma_start(out=dhi[kc], in_=WHI[:, o:o + w]))
                        stWC.inc(gC + i, scalar.dma_start(out=dlo[kc], in_=WLO[:, o:o + w]))
                block.scalar(fscal)
                stL.g += len(tiles)
                stWC.g += len(tiles)
                barrier()

        def emit_ln(ts, src, XA, XB):
            def fsync(sync):
                for c in range(KC):
                    sS.inc(sync.dma_start(out=XA[:, c * TS:(c + 1) * TS],
                                          in_=src[c][:, ts:ts + TS]))
            block.sync(fsync)
            s_loaded = sS.n

            def fscal(scalar):
                scalar.wait_ge(sS.h, s_loaded)
                cA.inc(scalar.square(XB[:, 0:KC * TS], XA[:, 0:KC * TS]))
            block.scalar(fscal)
            a_sq = cA.n

            def ftens(tensor):
                tensor.wait_ge(sS.h, s_loaded)
                for c in range(KC):
                    tensor.matmul(bankrow(6), CONST[:, O_OC:O_OC + 1],
                                  XA[:, c * TS:(c + 1) * TS],
                                  start=(c == 0), stop=(c == KC - 1))
                tensor.wait_ge(cA.h, a_sq)
                for c in range(KC):
                    mmi = tensor.matmul(bankrow(7), CONST[:, O_OC:O_OC + 1],
                                        XB[:, c * TS:(c + 1) * TS],
                                        start=(c == 0), stop=(c == KC - 1))
                cT.inc(mmi)
            block.tensor(ftens)
            t_stats = cT.n

            def fscal2(scalar):
                scalar.wait_ge(cT.h, t_stats)
                scalar.mul(SMALL[:, 0:TS], bankrow(6), 1.0 / D)
                scalar.mul(SMALL[:, TS:2 * TS], bankrow(7), 1.0 / D)
                cA.inc(scalar.square(SMALL[:, 2 * TS:3 * TS], SMALL[:, 0:TS]))
            block.scalar(fscal2)
            a_m = cA.n

            def fvec(vector):
                vector.wait_ge(cA.h, a_m)
                cV.inc(vector.tensor_sub(SMALL[:, TS:2 * TS], SMALL[:, TS:2 * TS],
                                         SMALL[:, 2 * TS:3 * TS]))
            block.vector(fvec)
            v_var = cV.n

            def fscal3(scalar):
                scalar.wait_ge(cV.h, v_var)
                cA.inc(scalar.activation(SMALL[:, 2 * TS:3 * TS], SMALL[:, TS:2 * TS],
                                         AF.Sqrt, bias=MROW[:, 7:8]))
            block.scalar(fscal3)
            a_sd = cA.n

            def fvec2(vector):
                vector.wait_ge(cA.h, a_sd)
                cV.inc(vector.reciprocal(SMALL[:, TS:2 * TS], SMALL[:, 2 * TS:3 * TS]))
            block.vector(fvec2)
            v_r = cV.n

            def ftens2(tensor):
                tensor.wait_ge(cV.h, v_r)
                tensor.matmul(bank(4), ONESR[:, :], SMALL[:, 0:TS], start=True, stop=True)
                mmi = tensor.matmul(bank(5), ONESR[:, :], SMALL[:, TS:2 * TS],
                                    start=True, stop=True)
                cT.inc(mmi)
            block.tensor(ftens2)
            t_b = cT.n

            def fvec3(vector):
                vector.wait_ge(cT.h, t_b)
                for c in range(KC):
                    sl = slice(c * TS, (c + 1) * TS)
                    vector.tensor_sub(XB[:, sl], XA[:, sl], bank(4))
                    vector.tensor_mul(XB[:, sl], XB[:, sl], bank(5))
                vector.tensor_copy(HI[:, 0:KC * TS], XB[:, 0:KC * TS])
                cV.inc(vector.tensor_tensor(LO[:, 0:KC * TS], XB[:, 0:KC * TS],
                                            HI[:, 0:KC * TS], ALU.subtract))
            block.vector(fvec3)
            return cV.n

        def p_ffn(src, dst, tag, final=False, parts=("ln", "gu", "dn")):
            with nc.sbuf_tensor(f"{tag}_XA", [128, KC * TS], F32) as XA, \
                 nc.sbuf_tensor(f"{tag}_XB", [128, KC * TS], F32) as XB, \
                 nc.sbuf_tensor(f"{tag}_WG", [128, 2 * D], BF16) as WG, \
                 nc.sbuf_tensor(f"{tag}_WU", [128, 2 * D], BF16) as WU, \
                 nc.sbuf_tensor(f"{tag}_WD", [128, 2 * D], BF16) as WD, \
                 nc.sbuf_tensor(f"{tag}_GS", [128, 2 * TS], F32) as GS, \
                 nc.sbuf_tensor(f"{tag}_FH", [128, 2 * TS], BF16) as FH, \
                 nc.sbuf_tensor(f"{tag}_FL", [128, 2 * TS], BF16) as FL, \
                 nc.sbuf_tensor(f"{tag}_FHR", [128, 4 * TS], BF16) as FHR, \
                 nc.sbuf_tensor(f"{tag}_FLR", [128, 4 * TS], BF16) as FLR:
                for it in range(NT):
                    ts = it * TS
                    v_ln = emit_ln(ts, src, XA, XB)
                    if "gu" not in parts:
                        barrier()
                        continue

                    t0, v0, a0 = cT.n, cV.n, cA.n
                    gW0 = stW.g

                    def fg(gpsimd, gW0=gW0, t0=t0):
                        for ic in range(IC):
                            if ic >= 2:
                                gpsimd.wait_ge(cT.h, t0 + (ic - 1))
                            o = (ic % 2) * D
                            stW.inc(gW0 + ic, gpsimd.dma_start(out=WG[:, o:o + D], in_=wg_t[ic]))
                            stW.inc(gW0 + ic, gpsimd.dma_start(out=WU[:, o:o + D], in_=wu_t[ic]))
                    block.gpsimd(fg)

                    def ft(tensor, gW0=gW0, v0=v0, v_ln=v_ln):
                        tensor.wait_ge(cV.h, v_ln)
                        for ic in range(IC):
                            stW.wait(tensor, gW0 + ic)
                            if ic >= 2:
                                tensor.wait_ge(cV.h, v0 + (ic - 1))
                            o = (ic % 2) * D
                            pg = bank(ic % 2)
                            pu = bank(2 + ic % 2)
                            for kc in range(KC):
                                wgc = WG[:, o + kc * 128:o + (kc + 1) * 128]
                                wuc = WU[:, o + kc * 128:o + (kc + 1) * 128]
                                xh = HI[:, kc * TS:(kc + 1) * TS]
                                xl = LO[:, kc * TS:(kc + 1) * TS]
                                tensor.matmul(pg, wgc, xh, start=(kc == 0), stop=False)
                                tensor.matmul(pg, wgc, xl, start=False, stop=(kc == KC - 1))
                                tensor.matmul(pu, wuc, xh, start=(kc == 0), stop=False)
                                mmi = tensor.matmul(pu, wuc, xl, start=False,
                                                    stop=(kc == KC - 1))
                            cT.inc(mmi)
                    block.tensor(ft)

                    gAF0 = stAF.g

                    def fa(scalar, t0=t0, v0=v0, gAF0=gAF0):
                        for ic in range(IC + 1):
                            if ic < IC:
                                scalar.wait_ge(cT.h, t0 + (ic + 1))
                                if ic >= 2:
                                    scalar.wait_ge(cV.h, v0 + (ic - 1))
                                cA.inc(scalar.activation(
                                    GS[:, (ic % 2) * TS:(ic % 2 + 1) * TS], bank(ic % 2),
                                    AF.Silu, scale=CONST[:, O_SG + ic:O_SG + ic + 1]))
                            if ic >= 1:
                                j = ic - 1
                                scalar.wait_ge(cV.h, v0 + (j + 1))
                                sl = slice((j % 2) * TS, (j % 2 + 1) * TS)
                                stAF.inc(gAF0 + j, scalar.dma_start(out=fuse_d[j, 0], in_=FH[:, sl]))
                                stAF.inc(gAF0 + j, scalar.dma_start(out=fuse_d[j, 1], in_=FL[:, sl]))
                    block.scalar(fa)

                    def fv(vector, a0=a0, gAF0=gAF0):
                        for ic in range(IC):
                            vector.wait_ge(cA.h, a0 + (ic + 1))
                            if ic >= 2:
                                stAF.wait(vector, gAF0 + ic - 2)
                            sl = slice((ic % 2) * TS, (ic % 2 + 1) * TS)
                            vector.scalar_tensor_tensor(GS[:, sl], bank(2 + ic % 2),
                                                        CONST[:, O_SU + ic:O_SU + ic + 1],
                                                        GS[:, sl], ALU.mult, ALU.mult)
                            vector.tensor_copy(FH[:, sl], GS[:, sl])
                            cV.inc(vector.tensor_tensor(FL[:, sl], GS[:, sl], FH[:, sl],
                                                        ALU.subtract))
                    block.vector(fv)
                    stAF.g += IC
                    af_t = [stAF.c[s].n for s in range(2)]

                    def wait_fuse(eng):
                        for s in range(2):
                            eng.wait_ge(stAF.c[s].h, af_t[s])
                    if "dn" not in parts:
                        def fz(scalar):
                            scalar.wait_ge(sA.h, sA.n)
                        block.scalar(fz)
                        barrier()
                        continue

                    t1, v1 = cT.n, cV.n
                    gF0, gWD0 = stF.g, stWD.g

                    def fs_dn(sync, gF0=gF0, t1=t1):
                        wait_fuse(sync)
                        for i in range(2 * IC):
                            ic = i % IC
                            if i >= 4:
                                sync.wait_ge(cT.h, t1 + (i - 3))
                            b = i % 4
                            stF.inc(gF0 + i, sync.dma_start(out=FHR[:, b * TS:(b + 1) * TS],
                                                            in_=fuse_d[ic, 0]))
                            stF.inc(gF0 + i, sync.dma_start(out=FLR[:, b * TS:(b + 1) * TS],
                                                            in_=fuse_d[ic, 1]))
                    block.sync(fs_dn)

                    def fg_dn(gpsimd, gWD0=gWD0, t1=t1):
                        wait_fuse(gpsimd)
                        for i in range(2 * IC):
                            half, ic = i // IC, i % IC
                            if i >= 2:
                                gpsimd.wait_ge(cT.h, t1 + (i - 1))
                            o = (i % 2) * D
                            stWD.inc(gWD0 + i, gpsimd.dma_start(
                                out=WD[:, o:o + D // 2],
                                in_=wd_t[ic][:, half * (D // 2):(half + 1) * (D // 2)]))
                    block.gpsimd(fg_dn)

                    def ft_dn(tensor, gF0=gF0, gWD0=gWD0, v1=v1):
                        for i in range(2 * IC):
                            half, ic = i // IC, i % IC
                            stF.wait(tensor, gF0 + i)
                            stWD.wait(tensor, gWD0 + i)
                            if half == 1 and ic == 0:
                                tensor.wait_ge(cV.h, v1 + D2C // 2)
                            b = i % 4
                            o = (i % 2) * D
                            for j in range(D2C // 2):
                                py = bank(j)
                                wdc = WD[:, o + j * 128:o + (j + 1) * 128]
                                tensor.matmul(py, wdc, FHR[:, b * TS:(b + 1) * TS],
                                              start=(ic == 0), stop=False)
                                mmi = tensor.matmul(py, wdc, FLR[:, b * TS:(b + 1) * TS],
                                                    start=False, stop=(ic == IC - 1))
                            cT.inc(mmi)
                    block.tensor(ft_dn)
                    stF.g += 2 * IC
                    stWD.g += 2 * IC

                    def fv_dn(vector, t1=t1):
                        for half in range(2):
                            vector.wait_ge(cT.h, t1 + (half + 1) * IC)
                            for j in range(D2C // 2):
                                c = half * (D2C // 2) + j
                                cV.inc(vector.scalar_tensor_tensor(
                                    XB[:, c * TS:(c + 1) * TS], bank(j),
                                    CONST[:, O_SD + c:O_SD + c + 1],
                                    XA[:, c * TS:(c + 1) * TS], ALU.mult, ALU.add))
                    block.vector(fv_dn)
                    v_dn0 = cV.n - D2C

                    def fa3(scalar, v_dn0=v_dn0, ts=ts):
                        for c in range(D2C):
                            scalar.wait_ge(cV.h, v_dn0 + c + 1)
                            sA.inc(scalar.dma_start(out=dst[c][:, ts:ts + TS],
                                                    in_=XB[:, c * TS:(c + 1) * TS]))
                    block.scalar(fa3)
                    if final and it == NT - 1:
                        def fa4(scalar):
                            scalar.wait_ge(sA.h, sA.n)
                        block.scalar(fa4)
                    barrier()

        def p_qkv():
            OCG = min(8, KC)
            NGH = KC // OCG
            NT1 = TS // 128
            VB = min(512, D // 2)
            NDB = (D // 2) // VB
            SLT = max(OCG * 128, D // 2)
            with nc.sbuf_tensor("qk_XA", [128, KC * TS], F32) as XA, \
                 nc.sbuf_tensor("qk_XB", [128, KC * TS], F32) as XB, \
                 nc.sbuf_tensor("qk_WQH", [128, 2 * SLT], BF16) as WQH, \
                 nc.sbuf_tensor("qk_WQL", [128, 2 * SLT], BF16) as WQL, \
                 nc.sbuf_tensor("qk_RC", [128, 2 * TS], F32) as RC, \
                 nc.sbuf_tensor("qk_RS", [128, 2 * TS], F32) as RS, \
                 nc.sbuf_tensor("qk_QH", [128, KC * TS], BF16) as QH, \
                 nc.sbuf_tensor("qk_QL", [128, KC * TS], BF16) as QL, \
                 nc.sbuf_tensor("qk_KH", [128, KC * TS], BF16) as KH, \
                 nc.sbuf_tensor("qk_KL", [128, KC * TS], BF16) as KL:
                for it in range(NT):
                    ts = it * TS
                    sA_prev = sA.n
                    v_ln = emit_ln(ts, xmid, XA, XB)

                    qk_banks_free = [0]
                    for p in range(2):
                        t2, v2 = cT.n, cV.n
                        qk_banks_free[0] = v2 + NGH
                        gQ0 = stQW.g

                        def fg_qk(gpsimd, gQ0=gQ0, t2=t2, p=p):
                            gpsimd.wait_ge(cT.h, t2)
                            for gi in range(NGH):
                                for kc in range(KC):
                                    i = gi * KC + kc
                                    if i >= 2:
                                        gpsimd.wait_ge(cT.h, t2 + (i - 1))
                                    o = (i % 2) * SLT
                                    co = (p * KC + gi * OCG) * 128
                                    stQW.inc(gQ0 + i, gpsimd.dma_start(
                                        out=WQH[:, o:o + OCG * 128],
                                        in_=qkv_hi[kc][:, co:co + OCG * 128]))
                                    stQW.inc(gQ0 + i, gpsimd.dma_start(
                                        out=WQL[:, o:o + OCG * 128],
                                        in_=qkv_lo[kc][:, co:co + OCG * 128]))
                        block.gpsimd(fg_qk)

                        def ft_qk(tensor, gQ0=gQ0, v2=v2, v_ln=v_ln, p=p):
                            tensor.wait_ge(cV.h, v_ln)
                            tensor.wait_ge(cV.h, v2)
                            for gi in range(NGH):
                                if gi >= 1:
                                    tensor.wait_ge(cV.h, v2 + gi)
                                for kc in range(KC):
                                    i = gi * KC + kc
                                    stQW.wait(tensor, gQ0 + i)
                                    o = (i % 2) * SLT
                                    xh = HI[:, kc * TS:(kc + 1) * TS]
                                    xl = LO[:, kc * TS:(kc + 1) * TS]
                                    for j in range(OCG):
                                        pj = bank(j)
                                        wh = WQH[:, o + j * 128:o + (j + 1) * 128]
                                        wl = WQL[:, o + j * 128:o + (j + 1) * 128]
                                        tensor.matmul(pj, wh, xh, start=(kc == 0), stop=False)
                                        tensor.matmul(pj, wh, xl, start=False, stop=False)
                                        mmi = tensor.matmul(pj, wl, xh, start=False,
                                                            stop=(kc == KC - 1))
                                    cT.inc(mmi)
                        block.tensor(ft_qk)
                        stQW.g += NGH * KC
                        t_qk0 = cT.n - NGH * KC

                        def fv_ev(vector, t_qk0=t_qk0):
                            for gi in range(NGH):
                                vector.wait_ge(cT.h, t_qk0 + (gi + 1) * KC)
                                for j in range(OCG):
                                    oc = gi * OCG + j
                                    vi = vector.tensor_copy(XB[:, oc * TS:(oc + 1) * TS],
                                                            bank(j))
                                cV.inc(vi)
                        block.vector(fv_ev)

                        gR0 = stR.g
                        v_rope0 = cV.n

                        def fv_rope(vector, gR0=gR0, p=p, sA_prev=sA_prev, it=it):
                            if p == 0 and it > 0:
                                vector.wait_ge(sA.h, sA_prev)
                            for c in range(DQC):
                                b = c % 2
                                stR.wait(vector, gR0 + c)
                                s0 = slice(c * TS, (c + 1) * TS)
                                s1_ = slice((DQC + c) * TS, (DQC + c + 1) * TS)
                                rc = RC[:, b * TS:(b + 1) * TS]
                                rs_ = RS[:, b * TS:(b + 1) * TS]
                                vector.tensor_mul(XA[:, 0:TS], XB[:, s1_], rs_)
                                vector.tensor_mul(XA[:, TS:2 * TS], XB[:, s0], rs_)
                                vector.tensor_mul(XB[:, s0], XB[:, s0], rc)
                                vector.tensor_sub(XB[:, s0], XB[:, s0], XA[:, 0:TS])
                                vector.tensor_mul(XB[:, s1_], XB[:, s1_], rc)
                                cV.inc(vector.tensor_add(XB[:, s1_], XB[:, s1_],
                                                         XA[:, TS:2 * TS]))
                            DH_, DL_ = (QH, QL) if p == 0 else (KH, KL)
                            vector.tensor_copy(DH_[:, 0:KC * TS], XB[:, 0:KC * TS])
                            cV.inc(vector.tensor_tensor(DL_[:, 0:KC * TS], XB[:, 0:KC * TS],
                                                        DH_[:, 0:KC * TS], ALU.subtract))
                        block.vector(fv_rope)
                        v_cast = cV.n

                        def fs_rt(sync, ts=ts, v_rope0=v_rope0, gR0=gR0):
                            sync.wait_ge(cV.h, v_rope0)
                            for c in range(DQC):
                                b = c % 2
                                if c >= 2:
                                    sync.wait_ge(cV.h, v_rope0 + (c - 1))
                                stR.inc(gR0 + c, sync.dma_start(out=RC[:, b * TS:(b + 1) * TS],
                                                                in_=cos_t[c][:, ts:ts + TS]))
                                stR.inc(gR0 + c, sync.dma_start(out=RS[:, b * TS:(b + 1) * TS],
                                                                in_=sin_t[c][:, ts:ts + TS]))
                        block.sync(fs_rt)
                        stR.g += DQC

                        def fa_pk(scalar, v_cast=v_cast, ts=ts, p=p, it=it):
                            scalar.wait_ge(cV.h, v_cast)
                            for c in range(KC):
                                if p == 0:
                                    sA.inc(scalar.dma_start(
                                        out=qpark[0, c][:, ts:ts + TS],
                                        in_=QH[:, c * TS:(c + 1) * TS]))
                                    sA.inc(scalar.dma_start(
                                        out=qpark[1, c][:, ts:ts + TS],
                                        in_=QL[:, c * TS:(c + 1) * TS]))
                                else:
                                    sA.inc(scalar.dma_start(
                                        out=k_in[it].ap()[0, c],
                                        in_=KH[:, c * TS:(c + 1) * TS]))
                                    sA.inc(scalar.dma_start(
                                        out=k_in[it].ap()[1, c],
                                        in_=KL[:, c * TS:(c + 1) * TS]))
                        block.scalar(fa_pk)
                        if p == 1:
                            a_kstage = sA.n

                            def fg_agk(gpsimd, a_kstage=a_kstage, it=it):
                                gpsimd.wait_ge(sA.h, a_kstage)
                                cC.inc(gpsimd.collective_compute(
                                    "AllGather", ALU.bypass,
                                    replica_groups=[[2 * i, 2 * i + 1]
                                                    for i in range(NCORES // 2)],
                                    ins=[k_in[it].ap().opt()],
                                    outs=[k_all[it].ap().opt()]))
                            block.gpsimd(fg_agk)
                    a_park = sA.n

                    for dvh in range(2):
                        t3 = cT.n
                        gV0 = stQW.g

                        def fg_v(gpsimd, gV0=gV0, t3=t3, dvh=dvh):
                            gpsimd.wait_ge(cT.h, t3)
                            for kc in range(KC):
                                if kc >= 2:
                                    gpsimd.wait_ge(cT.h, t3 + (kc - 1))
                                o = (kc % 2) * SLT
                                vo = 2 * D + dvh * (D // 2)
                                stQW.inc(gV0 + kc, gpsimd.dma_start(
                                    out=WQH[:, o:o + D // 2],
                                    in_=qkv_hi[kc][:, vo:vo + D // 2]))
                                stQW.inc(gV0 + kc, gpsimd.dma_start(
                                    out=WQL[:, o:o + D // 2],
                                    in_=qkv_lo[kc][:, vo:vo + D // 2]))
                        block.gpsimd(fg_v)

                        def ft_v(tensor, gV0=gV0, dvh=dvh,
                                 v=(qk_banks_free[0] if dvh == 0 else cV.n)):
                            tensor.wait_ge(cV.h, v)
                            for kc in range(KC):
                                stQW.wait(tensor, gV0 + kc)
                                o = (kc % 2) * SLT
                                for t1 in range(NT1):
                                    xh = HI[:, kc * TS + t1 * 128:kc * TS + (t1 + 1) * 128]
                                    xl = LO[:, kc * TS + t1 * 128:kc * TS + (t1 + 1) * 128]
                                    for db_ in range(NDB):
                                        pj = bank(t1 * NDB + db_, VB)
                                        wh = WQH[:, o + db_ * VB:o + (db_ + 1) * VB]
                                        wl = WQL[:, o + db_ * VB:o + (db_ + 1) * VB]
                                        tensor.matmul(pj, xh, wh, start=(kc == 0), stop=False)
                                        tensor.matmul(pj, xh, wl, start=False, stop=False)
                                        mmi = tensor.matmul(pj, xl, wh, start=False,
                                                            stop=(kc == KC - 1))
                                cT.inc(mmi)
                        block.tensor(ft_v)
                        stQW.g += KC
                        t_v_end = cT.n

                        def fv_v(vector, t_v_end=t_v_end, a_park=a_park, dvh=dvh):
                            vector.wait_ge(cT.h, t_v_end)
                            if dvh == 0:
                                vector.wait_ge(sA.h, a_park)
                            n = NT1 * NDB
                            for s_ in range(n):
                                sl = slice(s_ * VB, (s_ + 1) * VB)
                                vector.tensor_copy(XB[:, sl], bank(s_, VB))
                            for s_ in range(n):
                                sl = slice(s_ * VB, (s_ + 1) * VB)
                                vector.tensor_copy(QH[:, sl], XB[:, sl])
                                vi = vector.tensor_tensor(QL[:, sl], XB[:, sl], QH[:, sl],
                                                          ALU.subtract)
                            cV.inc(vi)
                        block.vector(fv_v)
                        v_vc = cV.n

                        def fa_v(scalar, v_vc=v_vc, dvh=dvh, ts=ts, it=it):
                            scalar.wait_ge(cV.h, v_vc)
                            for t1 in range(NT1):
                                for db_ in range(NDB):
                                    s_ = t1 * NDB + db_
                                    sl = slice(s_ * VB, (s_ + 1) * VB)
                                    vo = dvh * (D // 2) + db_ * VB
                                    sA.inc(scalar.dma_start(
                                        out=v_in[it].ap()[0, t1][:, vo:vo + VB],
                                        in_=QH[:, sl]))
                                    sA.inc(scalar.dma_start(
                                        out=v_in[it].ap()[1, t1][:, vo:vo + VB],
                                        in_=QL[:, sl]))
                        block.scalar(fa_v)
                        if dvh == 1:
                            a_vstage = sA.n

                            def fg_agv(gpsimd, a_vstage=a_vstage, it=it):
                                gpsimd.wait_ge(sA.h, a_vstage)
                                cC.inc(gpsimd.collective_compute(
                                    "AllGather", ALU.bypass,
                                    replica_groups=[[2 * i, 2 * i + 1]
                                                    for i in range(NCORES // 2)],
                                    ins=[v_in[it].ap().opt()],
                                    outs=[v_all[it].ap().opt()]))
                            block.gpsimd(fg_agv)
                    barrier()

                gNB0 = stNB.g
                a_vdone = sA.n
                v_nb0 = cV.n
                a_nb0 = cA.n
                t_nb0 = cT.n
                steps = []
                for it2 in range(NT):
                    steps.append(("q", it2))
                    steps.append(("k", it2))

                def fs_nb(sync, gNB0=gNB0, a_vdone=a_vdone, a_nb0=a_nb0):
                    sync.wait_ge(sA.h, a_vdone)
                    for s_, (qk, it2) in enumerate(steps):
                        if s_ >= 2:
                            sync.wait_ge(cA.h, a_nb0 + s_ - 1)
                        buf = QH if s_ % 2 == 0 else QL
                        for c in range(KC):
                            srcap = (qpark[0, c][:, it2 * TS:(it2 + 1) * TS] if qk == "q"
                                     else k_in[it2].ap()[0, c])
                            stNB.inc(gNB0 + s_, sync.dma_start(
                                out=buf[:, c * TS:(c + 1) * TS], in_=srcap))
                block.sync(fs_nb)
                stNB.g += len(steps)

                def fa_nb(scalar, gNB0=gNB0, t_nb0=t_nb0):
                    for s_, (qk, it2) in enumerate(steps):
                        stNB.wait(scalar, gNB0 + s_)
                        if s_ >= 1:
                            scalar.wait_ge(cT.h, t_nb0 + s_)
                        buf = QH if s_ % 2 == 0 else QL
                        cA.inc(scalar.square(KH[:, 0:KC * TS], buf[:, 0:KC * TS]))
                block.scalar(fa_nb)

                def ft_nb(tensor, a_nb0=a_nb0, v_nb0=v_nb0):
                    tensor.wait_ge(cV.h, v_nb0)
                    for s_, (qk, it2) in enumerate(steps):
                        tensor.wait_ge(cA.h, a_nb0 + s_ + 1)
                        bk = 4 if qk == "q" else 5
                        if s_ >= 2:
                            tensor.wait_ge(cV.h, v_nb0 + s_ - 1)
                        for c in range(KC):
                            mmi = tensor.matmul(bankrow(bk), ONESB[:, :],
                                                KH[:, c * TS:(c + 1) * TS],
                                                start=(c == 0), stop=(c == KC - 1))
                        cT.inc(mmi)
                block.tensor(ft_nb)

                def fv_nb(vector, t_nb0=t_nb0, v_nb0=v_nb0):
                    for s_, (qk, it2) in enumerate(steps):
                        vector.wait_ge(cT.h, t_nb0 + s_ + 1)
                        bk = 4 if qk == "q" else 5
                        m = 0 if qk == "q" else 1
                        vector.reduce_max(MROW[:, 4:5], bankrow(bk), axis=AXX)
                        cV.inc(vector.tensor_max(MROW[:, m:m + 1], MROW[:, m:m + 1],
                                                 MROW[:, 4:5]))
                block.vector(fv_nb)

                def fv_m(vector):
                    cV.inc(vector.tensor_mul(MROW[:, 2:3], MROW[:, 0:1], MROW[:, 1:2]))
                block.vector(fv_m)

                def fa_m(scalar, v=cV.n):
                    scalar.wait_ge(cV.h, v)
                    scalar.activation(MROW[:, 3:4], MROW[:, 2:3], AF.Sqrt)
                    cA.inc(scalar.mul(MROW[:, 3:4], MROW[:, 3:4], -float(cfg.SC)))
                block.scalar(fa_m)

                def ft_m(tensor, a=cA.n):
                    tensor.wait_ge(cA.h, a)
                    mmi = tensor.matmul(PS[:, 0:1], ONESR[:, :], MROW[:, 3:4],
                                        start=True, stop=True)
                    cT.inc(mmi)
                block.tensor(ft_m)

                def fv_m2(vector, t=cT.n):
                    vector.wait_ge(cT.h, t)
                    cV.inc(vector.tensor_copy(CONST[:, O_NM:O_NM + 1], PS[:, 0:1]))
                block.vector(fv_m2)
                barrier()

        def p_collective():
            if debug:
                def fg2(gpsimd):
                    gpsimd.wait_ge(cC.h, cC.n)
                    for i in range(NT):
                        gD.inc(gpsimd.dma_start(out=dbg["dbg_k"][i], in_=k_in[i].ap()))
                        gD.inc(gpsimd.dma_start(out=dbg["dbg_v"][i], in_=v_in[i].ap()))
                    gD.inc(gpsimd.dma_start(out=dbg["dbg_q"], in_=qpark))
                    gpsimd.wait_ge(gD.h, gD.n)
                block.gpsimd(fg2)
                barrier()

        def p_attn():
            NKSL = S // 512
            NB = 512 // 128
            with nc.sbuf_tensor("at_EH", [128, T2C * QB], BF16) as EH, \
                 nc.sbuf_tensor("at_EL", [128, T2C * QB], BF16) as EL, \
                 nc.sbuf_tensor("at_QSH", [128, DQC * QB], BF16) as QSH, \
                 nc.sbuf_tensor("at_QSL", [128, DQC * QB], BF16) as QSL, \
                 nc.sbuf_tensor("at_KSH", [128, 2 * DQC * 512], BF16) as KSH, \
                 nc.sbuf_tensor("at_KSL", [128, 2 * DQC * 512], BF16) as KSL, \
                 nc.sbuf_tensor("at_VSH", [128, 2 * DH], BF16) as VSH, \
                 nc.sbuf_tensor("at_VSL", [128, 2 * DH], BF16) as VSL, \
                 nc.sbuf_tensor("at_WOH", [128, 2 * (D // 2)], BF16) as WOH, \
                 nc.sbuf_tensor("at_WOL", [128, 2 * (D // 2)], BF16) as WOL, \
                 nc.sbuf_tensor("at_E32", [128, 2 * 512], F32) as E32, \
                 nc.sbuf_tensor("at_RB", [128, QB], F32) as RB, \
                 nc.sbuf_tensor("at_OB", [128, DQC * QB], F32) as OB, \
                 nc.sbuf_tensor("at_XS", [128, 2 * 512], F32) as XS, \
                 nc.sbuf_tensor("at_YS", [128, 2 * 512], F32) as YS:
                prev_t_av = [0]
                for qb in range(NQB):
                    tq = qb * QB
                    for h in range(H):
                        v_prev = cV.n
                        gQb = stQ.g
                        gKb = stK.g

                        def fs_q(sync, tq=tq, h=h, v_prev=v_prev, qb=qb, gQb=gQb):
                            if not (qb == 0 and h == 0):
                                sync.wait_ge(cV.h, v_prev)
                            for ver, dst_ in ((0, QSH), (1, QSL)):
                                for c in range(DQC):
                                    stQ.inc(gQb, sync.dma_start(
                                        out=dst_[:, c * QB:(c + 1) * QB],
                                        in_=qpark[ver, h * DQC + c][:, tq:tq + QB]))
                        block.sync(fs_q)
                        stQ.g += 1

                        t4, v4, a4 = cT.n, cV.n, cA.n

                        def fs_k(sync, gKb=gKb, t4=t4, h=h, qb=qb):
                            for sl_ in range(NKSL):
                                r, lo_ = sl_ // (NKSL // 2), (sl_ % (NKSL // 2)) * 512
                                kt, off = lo_ // TS, lo_ % TS
                                if qb == 0 and h == 0:
                                    sync.wait_ge(cC.h, 2 * kt + 1)
                                if sl_ >= 2:
                                    sync.wait_ge(cT.h, t4 + (sl_ - 1) * NB)
                                ko = (sl_ % 2) * DQC * 512
                                for ver, dst_ in ((0, KSH), (1, KSL)):
                                    for c in range(DQC):
                                        stK.inc(gKb + sl_, sync.dma_start(
                                            out=dst_[:, ko + c * 512:ko + (c + 1) * 512],
                                            in_=k_all[kt].ap()[r, ver, h * DQC + c]
                                                [:, off:off + 512]))
                        block.sync(fs_k)
                        stK.g += NKSL

                        def ft_s(tensor, gQb=gQb, gKb=gKb, a4=a4, v4=v4, v_prev=v_prev,
                                 qb=qb, h=h):
                            if not (qb == 0 and h == 0):
                                tensor.wait_ge(cV.h, v_prev)
                            stQ.wait(tensor, gQb)
                            for sl_ in range(NKSL):
                                stK.wait(tensor, gKb + sl_)
                                ko = (sl_ % 2) * DQC * 512
                                for b in range(NB):
                                    t2c = sl_ * NB + b
                                    pj = bank(t2c % 4, QB)
                                    if t2c >= 4:
                                        tensor.wait_ge(cA.h, a4 + (t2c - 3))
                                    for c in range(DQC):
                                        kh = KSH[:, ko + c * 512 + b * 128:ko + c * 512 + (b + 1) * 128]
                                        kl = KSL[:, ko + c * 512 + b * 128:ko + c * 512 + (b + 1) * 128]
                                        qh = QSH[:, c * QB:(c + 1) * QB]
                                        ql = QSL[:, c * QB:(c + 1) * QB]
                                        tensor.matmul(pj, kh, qh, start=(c == 0), stop=False)
                                        tensor.matmul(pj, kh, ql, start=False, stop=False)
                                        mmi = tensor.matmul(pj, kl, qh, start=False,
                                                            stop=(c == DQC - 1))
                                    cT.inc(mmi)
                            for t2c in range(T2C):
                                tensor.wait_ge(cV.h, v4 + t2c + 1)
                                tensor.matmul(bankrow(4, QB), ONESB[:, :],
                                              EH[:, t2c * QB:(t2c + 1) * QB],
                                              start=(t2c == 0), stop=False)
                                mmi = tensor.matmul(bankrow(4, QB), ONESB[:, :],
                                                    EL[:, t2c * QB:(t2c + 1) * QB],
                                                    start=False, stop=(t2c == T2C - 1))
                            cT.inc(mmi)
                        block.tensor(ft_s)
                        t_sums = cT.n

                        def fa_e(scalar, t4=t4, v4=v4):
                            for t2c in range(T2C):
                                scalar.wait_ge(cT.h, t4 + t2c + 1)
                                if t2c >= 2:
                                    scalar.wait_ge(cV.h, v4 + t2c - 1)
                                cA.inc(scalar.activation(
                                    E32[:, (t2c % 2) * 512:(t2c % 2) * 512 + QB],
                                    bank(t2c % 4, QB), AF.Exp,
                                    bias=CONST[:, O_NM:O_NM + 1], scale=float(cfg.SC)))
                        block.scalar(fa_e)

                        def fv_e(vector, a4=a4):
                            for t2c in range(T2C):
                                vector.wait_ge(cA.h, a4 + t2c + 1)
                                sl_ = slice(t2c * QB, (t2c + 1) * QB)
                                e3 = E32[:, (t2c % 2) * 512:(t2c % 2) * 512 + QB]
                                vector.tensor_copy(EH[:, sl_], e3)
                                cV.inc(vector.tensor_tensor(EL[:, sl_], e3, EH[:, sl_],
                                                            ALU.subtract))
                        block.vector(fv_e)

                        def fv_r(vector, t=t_sums):
                            vector.wait_ge(cT.h, t)
                            cV.inc(vector.reciprocal(SMALL[:, 0:QB], bankrow(4, QB)))
                        block.vector(fv_r)

                        def ft_rb(tensor, v=cV.n):
                            tensor.wait_ge(cV.h, v)
                            mmi = tensor.matmul(bank(5, QB), ONESR[:, :], SMALL[:, 0:QB],
                                                start=True, stop=True)
                            cT.inc(mmi)
                        block.tensor(ft_rb)

                        def fv_rb(vector, t=cT.n):
                            vector.wait_ge(cT.h, t)
                            cV.inc(vector.tensor_copy(RB[:, 0:QB], bank(5, QB)))
                        block.vector(fv_rb)
                        v_rb = cV.n

                        gVb = stV.g
                        tav_base = cT.n

                        def fg_v(gpsimd, gVb=gVb, t5=prev_t_av[0], tav_base=tav_base,
                                 h=h, qb=qb):
                            gpsimd.wait_ge(cT.h, t5)
                            for t2c in range(T2C):
                                r, lc = t2c // (T2C // 2), t2c % (T2C // 2)
                                vt, t1l = lc // NT1_, lc % NT1_
                                if qb == 0 and h == 0:
                                    gpsimd.wait_ge(cC.h, 2 * vt + 2)
                                if t2c >= 2:
                                    gpsimd.wait_ge(cT.h, tav_base + t2c - 1)
                                o = (t2c % 2) * DH
                                stV.inc(gVb + t2c, gpsimd.dma_start(
                                    out=VSH[:, o:o + DH],
                                    in_=v_all[vt].ap()[r, 0, t1l][:, h * DH:(h + 1) * DH]))
                                stV.inc(gVb + t2c, gpsimd.dma_start(
                                    out=VSL[:, o:o + DH],
                                    in_=v_all[vt].ap()[r, 1, t1l][:, h * DH:(h + 1) * DH]))
                        block.gpsimd(fg_v)

                        def ft_av(tensor, gVb=gVb, v_rb=v_rb):
                            tensor.wait_ge(cV.h, v_rb)
                            for t2c in range(T2C):
                                stV.wait(tensor, gVb + t2c)
                                o = (t2c % 2) * DH
                                eh = EH[:, t2c * QB:(t2c + 1) * QB]
                                el = EL[:, t2c * QB:(t2c + 1) * QB]
                                for c in range(DQC):
                                    pj = bank(c, QB)
                                    vh = VSH[:, o + c * 128:o + (c + 1) * 128]
                                    vl = VSL[:, o + c * 128:o + (c + 1) * 128]
                                    tensor.matmul(pj, vh, eh, start=(t2c == 0), stop=False)
                                    tensor.matmul(pj, vh, el, start=False, stop=False)
                                    mmi = tensor.matmul(pj, vl, eh, start=False,
                                                        stop=(t2c == T2C - 1))
                                cT.inc(mmi)
                        block.tensor(ft_av)
                        stV.g += T2C
                        t_av = cT.n
                        prev_t_av[0] = t_av

                        def fv_o(vector, t_av=t_av, v_rb=v_rb, h=h):
                            vector.wait_ge(cT.h, t_av)
                            vector.wait_ge(cV.h, v_rb)
                            for c in range(DQC):
                                sl_ = slice(c * QB, (c + 1) * QB)
                                vector.tensor_mul(OB[:, sl_], bank(c, QB), RB[:, 0:QB])
                                g = h * DQC + c
                                vector.tensor_copy(HI[:, g * QB:(g + 1) * QB], OB[:, sl_])
                                vi = vector.tensor_tensor(LO[:, g * QB:(g + 1) * QB], OB[:, sl_],
                                                          HI[:, g * QB:(g + 1) * QB],
                                                          ALU.subtract)
                            cV.inc(vi)
                        block.vector(fv_o)

                    v_oh = cV.n
                    for d2h in range(2):
                        t6 = cT.n
                        gWb = stQW.g

                        def fg_wo(gpsimd, gWb=gWb, t6=t6, d2h=d2h):
                            gpsimd.wait_ge(cT.h, t6)
                            for dvc in range(KC):
                                if dvc >= 2:
                                    gpsimd.wait_ge(cT.h, t6 + dvc - 1)
                                o = (dvc % 2) * (D // 2)
                                stQW.inc(gWb + dvc, gpsimd.dma_start(
                                    out=WOH[:, o:o + D // 2],
                                    in_=wo_hi[dvc][:, d2h * (D // 2):(d2h + 1) * (D // 2)]))
                                stQW.inc(gWb + dvc, gpsimd.dma_start(
                                    out=WOL[:, o:o + D // 2],
                                    in_=wo_lo[dvc][:, d2h * (D // 2):(d2h + 1) * (D // 2)]))
                        block.gpsimd(fg_wo)

                        def ft_wo(tensor, gWb=gWb, v_oh=v_oh, v=cV.n):
                            tensor.wait_ge(cV.h, max(v_oh, v))
                            for dvc in range(KC):
                                stQW.wait(tensor, gWb + dvc)
                                o = (dvc % 2) * (D // 2)
                                oh = HI[:, dvc * QB:(dvc + 1) * QB]
                                ol = LO[:, dvc * QB:(dvc + 1) * QB]
                                for j in range(D2C // 2):
                                    pj = bank(j, QB)
                                    wh = WOH[:, o + j * 128:o + (j + 1) * 128]
                                    wl = WOL[:, o + j * 128:o + (j + 1) * 128]
                                    tensor.matmul(pj, wh, oh, start=(dvc == 0), stop=False)
                                    tensor.matmul(pj, wh, ol, start=False, stop=False)
                                    mmi = tensor.matmul(pj, wl, oh, start=False,
                                                        stop=(dvc == KC - 1))
                                cT.inc(mmi)
                        block.tensor(ft_wo)
                        stQW.g += KC
                        t_wo = cT.n

                        v_ev0 = cV.n
                        gXb = stX.g

                        def fs_x(sync, tq=tq, d2h=d2h, v_ev0=v_ev0, gXb=gXb):
                            sync.wait_ge(cV.h, v_ev0)
                            for j in range(D2C // 2):
                                c = d2h * (D2C // 2) + j
                                if j >= 2:
                                    sync.wait_ge(cV.h, v_ev0 + j - 1)
                                stX.inc(gXb + j, sync.dma_start(
                                    out=XS[:, (j % 2) * 512:(j % 2) * 512 + QB],
                                    in_=xmid[c][:, tq:tq + QB]))
                        block.sync(fs_x)
                        stX.g += D2C // 2

                        def fv_y(vector, t_wo=t_wo, gXb=gXb, d2h=d2h, s_ys=sA.n):
                            vector.wait_ge(cT.h, t_wo)
                            if d2h > 0 or True:
                                vector.wait_ge(sA.h, s_ys)
                            for j in range(D2C // 2):
                                c = d2h * (D2C // 2) + j
                                stX.wait(vector, gXb + j)
                                cV.inc(vector.scalar_tensor_tensor(
                                    YS[:, (j % 2) * 512:(j % 2) * 512 + QB], bank(j, QB),
                                    CONST[:, O_BO + c:O_BO + c + 1],
                                    XS[:, (j % 2) * 512:(j % 2) * 512 + QB],
                                    ALU.add, ALU.add))
                        block.vector(fv_y)
                        v_y0 = cV.n - D2C // 2

                        def fa_st(scalar, v_y0=v_y0, tq=tq, d2h=d2h):
                            for j in range(D2C // 2):
                                c = d2h * (D2C // 2) + j
                                scalar.wait_ge(cV.h, v_y0 + j + 1)
                                sA.inc(scalar.dma_start(
                                    out=xmid2[c][:, tq:tq + QB],
                                    in_=YS[:, (j % 2) * 512:(j % 2) * 512 + QB]))
                        block.scalar(fa_st)
                    barrier()
                if debug:
                    def fg3(gpsimd):
                        gD.inc(gpsimd.dma_start(out=dbg["dbg_xmid2"], in_=xmid2))
                        gpsimd.wait_ge(gD.h, gD.n)
                    block.gpsimd(fg3)
                    barrier()

        PH = phases if phases is not None else ["consts", "wconv", "ffn1", "qkv",
                                                "cc", "attn", "ffn2"]
        if "consts" in PH:
            p_consts()
        if "wconv" in PH:
            p_wconv()
        if "ffn1" in PH:
            p_ffn(x_t, xmid, "f1")
        if "ffn1ln" in PH:
            p_ffn(x_t, xmid, "f1", parts=("ln",))
        if "ffn1gu" in PH:
            p_ffn(x_t, xmid, "f1", parts=("ln", "gu"))
        if debug:
            def fgd(gpsimd):
                gD.inc(gpsimd.dma_start(out=dbg["dbg_xmid"], in_=xmid))
                gpsimd.wait_ge(gD.h, gD.n)
            block.gpsimd(fgd)
            barrier()
        if "qkv" in PH:
            p_qkv()
        if "cc" in PH:
            p_collective()
        if "attn" in PH:
            p_attn()
        if "ffn2" in PH:
            p_ffn(xmid2, out_t, "f2", final=True)

    return nc



def _rope_tables(seq_len, head_dim):
    k = np.arange(0, head_dim, 2, dtype=np.float32) / np.float32(head_dim)
    inv_freq = (1.0 / (10000.0 ** k)).astype(np.float32)
    t = np.arange(seq_len, dtype=np.float32)
    freqs = (t[:, None] * inv_freq[None, :]).astype(np.float32)
    emb = np.concatenate([freqs, freqs], axis=-1)
    cos = np.cos(emb.astype(np.float64)).astype(np.float32)
    sin = np.sin(emb.astype(np.float64)).astype(np.float32)
    return cos, sin


def make_inputs(cfg, b, half, x, wqkv, wout, bout, w_gate, w_up, w_down,
                s_gate, s_up, s_down):
    D, I, T, KC, IC, DQC = cfg.D, cfg.I, cfg.T, cfg.KC, cfg.IC, cfg.DQC
    cos, sin = _rope_tables(cfg.S, cfg.DH)
    sl = slice(half * T, (half + 1) * T)
    return {
        "x_t": np.ascontiguousarray(
            np.asarray(x[b, sl, :], dtype=np.float32).T.reshape(KC, 128, T)),
        "wqkv_t": np.ascontiguousarray(np.asarray(wqkv, np.float32).reshape(KC, 128, 3 * D)),
        "wout_t": np.ascontiguousarray(np.asarray(wout, np.float32).reshape(KC, 128, D)),
        "wg_t": np.ascontiguousarray(
            np.asarray(w_gate).reshape(KC, 128, IC, 128).transpose(2, 1, 0, 3)
            .reshape(IC, 128, D)),
        "wu_t": np.ascontiguousarray(
            np.asarray(w_up).reshape(KC, 128, IC, 128).transpose(2, 1, 0, 3)
            .reshape(IC, 128, D)),
        "wd_t": np.ascontiguousarray(np.asarray(w_down).reshape(IC, 128, D)),
        "sg_pp": np.ascontiguousarray(np.asarray(s_gate, np.float32).reshape(IC, 128).T),
        "su_pp": np.ascontiguousarray(np.asarray(s_up, np.float32).reshape(IC, 128).T),
        "sd_pp": np.ascontiguousarray(np.asarray(s_down, np.float32).reshape(KC, 128).T),
        "bout_pp": np.ascontiguousarray(np.asarray(bout, np.float32).reshape(KC, 128).T),
        "cos_t": np.ascontiguousarray(cos[sl, :].T.reshape(DQC, 128, T)),
        "sin_t": np.ascontiguousarray(sin[sl, :].T.reshape(DQC, 128, T)),
    }


def kernel(L0_attn_x, L0_attn_wqkv, L0_attn_wout, L0_attn_bout, L0_ffn_w_gate,
           L0_ffn_w_up, L0_ffn_w_down, L0_ffn_s_gate, L0_ffn_s_up, L0_ffn_s_down):
    cfg = CFG()
    x = np.asarray(L0_attn_x, dtype=np.float32)
    B = x.shape[0]
    nc = build_program(cfg)
    in_maps = []
    for core in range(NCORES):
        b, half = core // 2, core % 2
        in_maps.append(make_inputs(cfg, b, half, x, L0_attn_wqkv, L0_attn_wout,
                                   L0_attn_bout, L0_ffn_w_gate, L0_ffn_w_up,
                                   L0_ffn_w_down, L0_ffn_s_gate, L0_ffn_s_up,
                                   L0_ffn_s_down))
    res = run_bass_kernel_spmd(nc, in_maps, core_ids=list(range(NCORES)))
    out = np.empty((B, cfg.S, cfg.D), dtype=np.float32)
    for core in range(NCORES):
        b, half = core // 2, core % 2
        o = res.results[core]["out_t"].reshape(cfg.D, cfg.T)
        out[b, half * cfg.T:(half + 1) * cfg.T, :] = o.T
    return out


# version 11
# speedup vs baseline: 1.0521x; 1.0089x over previous
import contextlib
import math

import numpy as np
import ml_dtypes

import concourse.bass as bass
import concourse.mybir as mybir
from concourse.bass_utils import run_bass_kernel_spmd

F32 = mybir.dt.float32
BF16 = mybir.dt.bfloat16
I8 = mybir.dt.int8
AF = mybir.ActivationFunctionType
ALU = mybir.AluOpType
AXX = mybir.AxisListType.X

NCORES = 8


class CFG:
    def __init__(self, D=2048, I=8192, T=2048, H=2, TS=512, QB=512, eps=1e-5):
        self.D, self.I, self.T, self.H = D, I, T, H
        self.S = 2 * T
        self.DH = D // H
        self.TS, self.QB = TS, QB
        self.KC = D // 128
        self.IC = I // 128
        self.NT = T // TS
        self.NQB = T // QB
        self.T2C = self.S // 128
        self.DQC = self.DH // 128
        self.SC = 1.0 / math.sqrt(self.DH)
        self.eps = eps


def build_program(cfg: CFG, debug=False, phases=None):
    nc = bass.Bass("TRN2", target_bir_lowering=False, detect_race_conditions=False)
    D, I, T, H = cfg.D, cfg.I, cfg.T, cfg.H
    KC, IC, TS, NT = cfg.KC, cfg.IC, cfg.TS, cfg.NT
    QB, NQB, T2C, DQC = cfg.QB, cfg.NQB, cfg.T2C, cfg.DQC
    DH, S = cfg.DH, cfg.S
    D2C = KC

    x_t = nc.dram_tensor("x_t", [KC, 128, T], F32, kind="ExternalInput").ap()
    wqkv_t = nc.dram_tensor("wqkv_t", [KC, 128, 3 * D], F32, kind="ExternalInput").ap()
    wout_t = nc.dram_tensor("wout_t", [KC, 128, D], F32, kind="ExternalInput").ap()
    wg_t = nc.dram_tensor("wg_t", [IC, 128, D], I8, kind="ExternalInput").ap()
    wu_t = nc.dram_tensor("wu_t", [IC, 128, D], I8, kind="ExternalInput").ap()
    wd_t = nc.dram_tensor("wd_t", [IC, 128, D], I8, kind="ExternalInput").ap()
    sg_pp = nc.dram_tensor("sg_pp", [128, IC], F32, kind="ExternalInput").ap()
    su_pp = nc.dram_tensor("su_pp", [128, IC], F32, kind="ExternalInput").ap()
    sd_pp = nc.dram_tensor("sd_pp", [128, KC], F32, kind="ExternalInput").ap()
    bout_pp = nc.dram_tensor("bout_pp", [128, KC], F32, kind="ExternalInput").ap()
    cos_t = nc.dram_tensor("cos_t", [DQC, 128, T], F32, kind="ExternalInput").ap()
    sin_t = nc.dram_tensor("sin_t", [DQC, 128, T], F32, kind="ExternalInput").ap()

    out_t = nc.dram_tensor("out_t", [KC, 128, T], F32, kind="ExternalOutput").ap()

    xmid = nc.dram_tensor("xmid", [KC, 128, T], F32, kind="Internal").ap()
    xmid2 = nc.dram_tensor("xmid2", [KC, 128, T], F32, kind="Internal").ap()
    qkv_hi = nc.dram_tensor("qkv_hi", [KC, 128, 3 * D], BF16, kind="Internal").ap()
    qkv_lo = nc.dram_tensor("qkv_lo", [KC, 128, 3 * D], BF16, kind="Internal").ap()
    wo_hi = nc.dram_tensor("wo_hi", [KC, 128, D], BF16, kind="Internal").ap()
    wo_lo = nc.dram_tensor("wo_lo", [KC, 128, D], BF16, kind="Internal").ap()
    fuse_d = nc.dram_tensor("fuse_d", [IC, 2, 128, TS], BF16, kind="Internal").ap()
    qpark = nc.dram_tensor("qpark", [2, KC, 128, T], BF16, kind="Internal").ap()
    NT_ = T // TS
    NT1_ = TS // 128
    k_in = [nc.dram_tensor(f"k_in{i}", [2, KC, 128, TS], BF16, kind="Internal")
            for i in range(NT_)]
    k_all = [nc.dram_tensor(f"k_all{i}", [2, 2, KC, 128, TS], BF16, kind="Internal")
             for i in range(NT_)]
    v_in = [nc.dram_tensor(f"v_in{i}", [2, NT1_, 128, D], BF16, kind="Internal")
            for i in range(NT_)]
    v_all = [nc.dram_tensor(f"v_all{i}", [2, 2, NT1_, 128, D], BF16, kind="Internal")
             for i in range(NT_)]

    dbg = {}
    if debug:
        dbg["dbg_xmid"] = nc.dram_tensor("dbg_xmid", [KC, 128, T], F32,
                                         kind="ExternalOutput").ap()
        dbg["dbg_xmid2"] = nc.dram_tensor("dbg_xmid2", [KC, 128, T], F32,
                                          kind="ExternalOutput").ap()
        dbg["dbg_k"] = nc.dram_tensor("dbg_k", [NT_, 2, KC, 128, TS], BF16,
                                      kind="ExternalOutput").ap()
        dbg["dbg_v"] = nc.dram_tensor("dbg_v", [NT_, 2, NT1_, 128, D], BF16,
                                      kind="ExternalOutput").ap()
        dbg["dbg_q"] = nc.dram_tensor("dbg_q", [2, KC, 128, T], BF16,
                                      kind="ExternalOutput").ap()

    ones_col_f32 = nc.inline_tensor(np.ones((128, 1), dtype=np.float32), "ones_col_f32")
    ones_row_f32 = nc.inline_tensor(np.ones((1, 128), dtype=np.float32), "ones_row_f32")
    ones_col_bf = nc.inline_tensor(np.ones((128, 1), dtype=ml_dtypes.bfloat16), "ones_col_bf")

    class Ctr:
        def __init__(self, handle, step=1):
            self.h, self.n, self.step = handle, 0, step

        def inc(self, ins, k=None):
            k = self.step if k is None else k
            self.n += k
            ins.then_inc(self.h, k)
            return self.n

    class DStream:

        def __init__(self, ctrs, dpg):
            self.c = ctrs
            self.dpg = dpg
            self.g = 0

        def inc(self, g, ins):
            self.c[g % len(self.c)].inc(ins)

        def wait(self, eng, g):
            eng.wait_ge(self.c[g % len(self.c)].h,
                        16 * self.dpg * (g // len(self.c) + 1))

    with contextlib.ExitStack() as stk:
        e = stk.enter_context

        O_SG, O_SU = 0, IC
        O_SD, O_BO = 2 * IC, 2 * IC + D2C
        O_OC = 2 * IC + 2 * D2C
        O_NM = O_OC + 1
        CONST = e(nc.sbuf_tensor("CONST", [128, O_NM + 1], F32))
        ONESB = e(nc.sbuf_tensor("ONESB", [128, 1], BF16))
        ONESR = e(nc.sbuf_tensor("ONESR", [1, 128], F32))
        SMALL = e(nc.sbuf_tensor("SMALL", [1, 4 * max(TS, QB)], F32))
        MROW = e(nc.sbuf_tensor("MROW", [1, 8], F32))
        HI = e(nc.sbuf_tensor("HI", [128, KC * TS], BF16))
        LO = e(nc.sbuf_tensor("LO", [128, KC * TS], BF16))
        PS = e(nc.psum_tensor("PS", [128, 8 * 512], F32))

        def bank(j, w=None):
            w = TS if w is None else w
            return PS[:, j * 512:j * 512 + w]

        def bankrow(j, w=None):
            w = TS if w is None else w
            return PS[0:1, j * 512:j * 512 + w]

        sS = Ctr(e(nc.semaphore("sS")), 16)
        sA = Ctr(e(nc.semaphore("sA")), 16)
        gD = Ctr(e(nc.semaphore("gD")), 16)
        cT = Ctr(e(nc.semaphore("cT")))
        cV = Ctr(e(nc.semaphore("cV")))
        cA = Ctr(e(nc.semaphore("cA")))
        cC = Ctr(e(nc.semaphore("cC")))
        bar = {n: Ctr(e(nc.semaphore("b" + n))) for n in "STVAG"}

        def mkst(name, ns, dpg):
            return DStream([Ctr(e(nc.semaphore(f"{name}{i}")), 16)
                            for i in range(ns)], dpg)

        stL = mkst("stL", 2, 1)
        stWC = mkst("stWC", 2, 2)
        stW = mkst("stW", 3, 2)
        stWD = mkst("stWD", 2, 1)
        stF = mkst("stF", 4, 2)
        stAF = mkst("stAF", 2, 2)
        stR = mkst("stR", 2, 2)
        stQW = mkst("stQW", 3, 2)
        stK = mkst("stK", 2, 2 * DQC)
        stQ = mkst("stQ", 1, 2 * DQC)
        stV = mkst("stV", 2, 2)
        stX = mkst("stX", 2, 1)
        stNB = mkst("stNB", 2, KC)
        block = e(nc.Block())
        ENG = {"S": block.sync, "T": block.tensor, "V": block.vector,
               "A": block.scalar, "G": block.gpsimd}

        nbar = [0]

        def barrier():
            nbar[0] += 1
            k = nbar[0]
            for name in "STVAG":
                def f(eng, name=name, k=k):
                    eng.nop().then_inc(bar[name].h, 1)
                    for o in "STVAG":
                        if o != name:
                            eng.wait_ge(bar[o].h, k)
                ENG[name](f)
                bar[name].n += 1

        def p_consts():
            def fsync(sync):
                for dst_, src_ in [(CONST[:, O_SG:O_SG + IC], sg_pp),
                                   (CONST[:, O_SU:O_SU + IC], su_pp),
                                   (CONST[:, O_SD:O_SD + D2C], sd_pp),
                                   (CONST[:, O_BO:O_BO + D2C], bout_pp),
                                   (CONST[:, O_OC:O_OC + 1], ones_col_f32.ap()),
                                   (ONESB[:, :], ones_col_bf.ap()),
                                   (ONESR[:, :], ones_row_f32.ap())]:
                    sS.inc(sync.dma_start(out=dst_, in_=src_))
                sync.wait_ge(sS.h, sS.n)
            block.sync(fsync)

            def fscal(scalar):
                scalar.wait_ge(sS.h, sS.n)
                cA.inc(scalar.mul(CONST[:, O_SD:O_SD + D2C],
                                  CONST[:, O_SD:O_SD + D2C], 0.5))
            block.scalar(fscal)

            def fvec(vector):
                vector.memset(MROW[:, :], 0.0)
                cV.inc(vector.memset(MROW[:, 7:8], float(cfg.eps)))
            block.vector(fvec)
            barrier()

        def p_wconv():
            W = 3 * D
            tiles = [(wqkv_t, qkv_hi, qkv_lo, 3 * D, kc) for kc in range(KC)] + \
                    [(wout_t, wo_hi, wo_lo, D, kc) for kc in range(KC)]
            with nc.sbuf_tensor("cv_W32", [128, 2 * W], F32) as W32, \
                 nc.sbuf_tensor("cv_WHI", [128, 2 * W], BF16) as WHI, \
                 nc.sbuf_tensor("cv_WLO", [128, 2 * W], BF16) as WLO:
                cV0 = cV.n
                gL, gC = stL.g, stWC.g

                def fsync(sync):
                    for i, (src, _, _, w, kc) in enumerate(tiles):
                        if i >= 2:
                            sync.wait_ge(cV.h, cV0 + (i - 1))
                        stL.inc(gL + i, sync.dma_start(
                            out=W32[:, (i % 2) * W:(i % 2) * W + w], in_=src[kc]))
                block.sync(fsync)

                def fvec(vector):
                    for i, (_, _, _, w, kc) in enumerate(tiles):
                        o = (i % 2) * W
                        stL.wait(vector, gL + i)
                        if i >= 2:
                            stWC.wait(vector, gC + i - 2)
                        vector.tensor_copy(WHI[:, o:o + w], W32[:, o:o + w])
                        cV.inc(vector.tensor_tensor(WLO[:, o:o + w], W32[:, o:o + w],
                                                    WHI[:, o:o + w], ALU.subtract))
                block.vector(fvec)

                def fscal(scalar):
                    for i, (_, dhi, dlo, w, kc) in enumerate(tiles):
                        o = (i % 2) * W
                        scalar.wait_ge(cV.h, cV0 + (i + 1))
                        stWC.inc(gC + i, scalar.dma_start(out=dhi[kc], in_=WHI[:, o:o + w]))
                        stWC.inc(gC + i, scalar.dma_start(out=dlo[kc], in_=WLO[:, o:o + w]))
                block.scalar(fscal)
                stL.g += len(tiles)
                stWC.g += len(tiles)
                barrier()

        def emit_ln(ts, src, XA, XB):
            def fsync(sync):
                for c in range(KC):
                    sS.inc(sync.dma_start(out=XA[:, c * TS:(c + 1) * TS],
                                          in_=src[c][:, ts:ts + TS]))
            block.sync(fsync)
            s_loaded = sS.n

            def fscal(scalar):
                scalar.wait_ge(sS.h, s_loaded)
                cA.inc(scalar.square(XB[:, 0:KC * TS], XA[:, 0:KC * TS]))
            block.scalar(fscal)
            a_sq = cA.n

            def ftens(tensor):
                tensor.wait_ge(sS.h, s_loaded)
                for c in range(KC):
                    tensor.matmul(bankrow(6), CONST[:, O_OC:O_OC + 1],
                                  XA[:, c * TS:(c + 1) * TS],
                                  start=(c == 0), stop=(c == KC - 1))
                tensor.wait_ge(cA.h, a_sq)
                for c in range(KC):
                    mmi = tensor.matmul(bankrow(7), CONST[:, O_OC:O_OC + 1],
                                        XB[:, c * TS:(c + 1) * TS],
                                        start=(c == 0), stop=(c == KC - 1))
                cT.inc(mmi)
            block.tensor(ftens)
            t_stats = cT.n

            def fscal2(scalar):
                scalar.wait_ge(cT.h, t_stats)
                scalar.mul(SMALL[:, 0:TS], bankrow(6), 1.0 / D)
                scalar.mul(SMALL[:, TS:2 * TS], bankrow(7), 1.0 / D)
                cA.inc(scalar.square(SMALL[:, 2 * TS:3 * TS], SMALL[:, 0:TS]))
            block.scalar(fscal2)
            a_m = cA.n

            def fvec(vector):
                vector.wait_ge(cA.h, a_m)
                cV.inc(vector.tensor_sub(SMALL[:, TS:2 * TS], SMALL[:, TS:2 * TS],
                                         SMALL[:, 2 * TS:3 * TS]))
            block.vector(fvec)
            v_var = cV.n

            def fscal3(scalar):
                scalar.wait_ge(cV.h, v_var)
                cA.inc(scalar.activation(SMALL[:, 2 * TS:3 * TS], SMALL[:, TS:2 * TS],
                                         AF.Sqrt, bias=MROW[:, 7:8]))
            block.scalar(fscal3)
            a_sd = cA.n

            def fvec2(vector):
                vector.wait_ge(cA.h, a_sd)
                cV.inc(vector.reciprocal(SMALL[:, TS:2 * TS], SMALL[:, 2 * TS:3 * TS]))
            block.vector(fvec2)
            v_r = cV.n

            def ftens2(tensor):
                tensor.wait_ge(cV.h, v_r)
                tensor.matmul(bank(4), ONESR[:, :], SMALL[:, 0:TS], start=True, stop=True)
                mmi = tensor.matmul(bank(5), ONESR[:, :], SMALL[:, TS:2 * TS],
                                    start=True, stop=True)
                cT.inc(mmi)
            block.tensor(ftens2)
            t_b = cT.n

            def fvec3(vector):
                vector.wait_ge(cT.h, t_b)
                for c in range(KC):
                    sl = slice(c * TS, (c + 1) * TS)
                    vector.tensor_sub(XB[:, sl], XA[:, sl], bank(4))
                    vector.tensor_mul(XB[:, sl], XB[:, sl], bank(5))
                vector.tensor_copy(HI[:, 0:KC * TS], XB[:, 0:KC * TS])
                cV.inc(vector.tensor_tensor(LO[:, 0:KC * TS], XB[:, 0:KC * TS],
                                            HI[:, 0:KC * TS], ALU.subtract))
            block.vector(fvec3)
            return cV.n

        def p_ffn(src, dst, tag, final=False, parts=("ln", "gu", "dn")):
            with nc.sbuf_tensor(f"{tag}_XA", [128, KC * TS], F32) as XA, \
                 nc.sbuf_tensor(f"{tag}_XB", [128, KC * TS], F32) as XB, \
                 nc.sbuf_tensor(f"{tag}_WG", [128, 3 * D], BF16) as WG, \
                 nc.sbuf_tensor(f"{tag}_WU", [128, 3 * D], BF16) as WU, \
                 nc.sbuf_tensor(f"{tag}_WD", [128, 2 * D], BF16) as WD, \
                 nc.sbuf_tensor(f"{tag}_GS", [128, 2 * TS], F32) as GS, \
                 nc.sbuf_tensor(f"{tag}_FH", [128, 2 * TS], BF16) as FH, \
                 nc.sbuf_tensor(f"{tag}_FL", [128, 2 * TS], BF16) as FL, \
                 nc.sbuf_tensor(f"{tag}_FHR", [128, 4 * TS], BF16) as FHR, \
                 nc.sbuf_tensor(f"{tag}_FLR", [128, 4 * TS], BF16) as FLR:
                for it in range(NT):
                    ts = it * TS
                    v_ln = emit_ln(ts, src, XA, XB)
                    if "gu" not in parts:
                        barrier()
                        continue

                    t0, v0, a0 = cT.n, cV.n, cA.n
                    gW0 = stW.g

                    def fg(gpsimd, gW0=gW0, t0=t0):
                        for ic in range(IC):
                            if ic >= 3:
                                gpsimd.wait_ge(cT.h, t0 + (ic - 2))
                            o = (ic % 3) * D
                            stW.inc(gW0 + ic, gpsimd.dma_start(out=WG[:, o:o + D], in_=wg_t[ic]))
                            stW.inc(gW0 + ic, gpsimd.dma_start(out=WU[:, o:o + D], in_=wu_t[ic]))
                    block.gpsimd(fg)

                    def ft(tensor, gW0=gW0, v0=v0, v_ln=v_ln):
                        tensor.wait_ge(cV.h, v_ln)
                        for ic in range(IC):
                            stW.wait(tensor, gW0 + ic)
                            if ic >= 2:
                                tensor.wait_ge(cV.h, v0 + (ic - 1))
                            o = (ic % 3) * D
                            pg = bank(ic % 2)
                            pu = bank(2 + ic % 2)
                            for kc in range(KC):
                                wgc = WG[:, o + kc * 128:o + (kc + 1) * 128]
                                wuc = WU[:, o + kc * 128:o + (kc + 1) * 128]
                                xh = HI[:, kc * TS:(kc + 1) * TS]
                                xl = LO[:, kc * TS:(kc + 1) * TS]
                                tensor.matmul(pg, wgc, xh, start=(kc == 0), stop=False)
                                tensor.matmul(pg, wgc, xl, start=False, stop=(kc == KC - 1))
                                tensor.matmul(pu, wuc, xh, start=(kc == 0), stop=False)
                                mmi = tensor.matmul(pu, wuc, xl, start=False,
                                                    stop=(kc == KC - 1))
                            cT.inc(mmi)
                    block.tensor(ft)

                    gAF0 = stAF.g

                    def fa(scalar, t0=t0, v0=v0, gAF0=gAF0):
                        for ic in range(IC + 1):
                            if ic < IC:
                                scalar.wait_ge(cT.h, t0 + (ic + 1))
                                if ic >= 2:
                                    scalar.wait_ge(cV.h, v0 + (ic - 1))
                                cA.inc(scalar.activation(
                                    GS[:, (ic % 2) * TS:(ic % 2 + 1) * TS], bank(ic % 2),
                                    AF.Silu, scale=CONST[:, O_SG + ic:O_SG + ic + 1]))
                            if ic >= 1:
                                j = ic - 1
                                scalar.wait_ge(cV.h, v0 + (j + 1))
                                sl = slice((j % 2) * TS, (j % 2 + 1) * TS)
                                stAF.inc(gAF0 + j, scalar.dma_start(out=fuse_d[j, 0], in_=FH[:, sl]))
                                stAF.inc(gAF0 + j, scalar.dma_start(out=fuse_d[j, 1], in_=FL[:, sl]))
                    block.scalar(fa)

                    def fv(vector, a0=a0, gAF0=gAF0):
                        for ic in range(IC):
                            vector.wait_ge(cA.h, a0 + (ic + 1))
                            if ic >= 2:
                                stAF.wait(vector, gAF0 + ic - 2)
                            sl = slice((ic % 2) * TS, (ic % 2 + 1) * TS)
                            vector.scalar_tensor_tensor(GS[:, sl], bank(2 + ic % 2),
                                                        CONST[:, O_SU + ic:O_SU + ic + 1],
                                                        GS[:, sl], ALU.mult, ALU.mult)
                            vector.tensor_copy(FH[:, sl], GS[:, sl])
                            cV.inc(vector.tensor_tensor(FL[:, sl], GS[:, sl], FH[:, sl],
                                                        ALU.subtract))
                    block.vector(fv)
                    stAF.g += IC
                    af_t = [stAF.c[s].n for s in range(2)]

                    def wait_fuse(eng):
                        for s in range(2):
                            eng.wait_ge(stAF.c[s].h, af_t[s])
                    if "dn" not in parts:
                        def fz(scalar):
                            scalar.wait_ge(sA.h, sA.n)
                        block.scalar(fz)
                        barrier()
                        continue

                    t1, v1 = cT.n, cV.n
                    gF0, gWD0 = stF.g, stWD.g

                    def fs_dn(sync, gF0=gF0, t1=t1):
                        wait_fuse(sync)
                        for i in range(2 * IC):
                            ic = i % IC
                            if i >= 4:
                                sync.wait_ge(cT.h, t1 + (i - 3))
                            b = i % 4
                            stF.inc(gF0 + i, sync.dma_start(out=FHR[:, b * TS:(b + 1) * TS],
                                                            in_=fuse_d[ic, 0]))
                            stF.inc(gF0 + i, sync.dma_start(out=FLR[:, b * TS:(b + 1) * TS],
                                                            in_=fuse_d[ic, 1]))
                    block.sync(fs_dn)

                    def fg_dn(gpsimd, gWD0=gWD0, t1=t1):
                        wait_fuse(gpsimd)
                        for i in range(2 * IC):
                            half, ic = i // IC, i % IC
                            if i >= 2:
                                gpsimd.wait_ge(cT.h, t1 + (i - 1))
                            o = (i % 2) * D
                            stWD.inc(gWD0 + i, gpsimd.dma_start(
                                out=WD[:, o:o + D // 2],
                                in_=wd_t[ic][:, half * (D // 2):(half + 1) * (D // 2)]))
                    block.gpsimd(fg_dn)

                    def ft_dn(tensor, gF0=gF0, gWD0=gWD0, v1=v1):
                        for i in range(2 * IC):
                            half, ic = i // IC, i % IC
                            stF.wait(tensor, gF0 + i)
                            stWD.wait(tensor, gWD0 + i)
                            if half == 1 and ic == 0:
                                tensor.wait_ge(cV.h, v1 + D2C // 2)
                            b = i % 4
                            o = (i % 2) * D
                            for j in range(D2C // 2):
                                py = bank(j)
                                wdc = WD[:, o + j * 128:o + (j + 1) * 128]
                                tensor.matmul(py, wdc, FHR[:, b * TS:(b + 1) * TS],
                                              start=(ic == 0), stop=False)
                                mmi = tensor.matmul(py, wdc, FLR[:, b * TS:(b + 1) * TS],
                                                    start=False, stop=(ic == IC - 1))
                            cT.inc(mmi)
                    block.tensor(ft_dn)
                    stF.g += 2 * IC
                    stWD.g += 2 * IC

                    def fv_dn(vector, t1=t1):
                        for half in range(2):
                            vector.wait_ge(cT.h, t1 + (half + 1) * IC)
                            for j in range(D2C // 2):
                                c = half * (D2C // 2) + j
                                cV.inc(vector.scalar_tensor_tensor(
                                    XB[:, c * TS:(c + 1) * TS], bank(j),
                                    CONST[:, O_SD + c:O_SD + c + 1],
                                    XA[:, c * TS:(c + 1) * TS], ALU.mult, ALU.add))
                    block.vector(fv_dn)
                    v_dn0 = cV.n - D2C

                    def fa3(scalar, v_dn0=v_dn0, ts=ts):
                        for c in range(D2C):
                            scalar.wait_ge(cV.h, v_dn0 + c + 1)
                            sA.inc(scalar.dma_start(out=dst[c][:, ts:ts + TS],
                                                    in_=XB[:, c * TS:(c + 1) * TS]))
                    block.scalar(fa3)
                    if final and it == NT - 1:
                        def fa4(scalar):
                            scalar.wait_ge(sA.h, sA.n)
                        block.scalar(fa4)
                    barrier()

        def p_qkv():
            OCG = min(8, KC)
            NGH = KC // OCG
            NT1 = TS // 128
            VB = min(512, D // 2)
            NDB = (D // 2) // VB
            SLT = max(OCG * 128, D // 2)
            with nc.sbuf_tensor("qk_XA", [128, KC * TS], F32) as XA, \
                 nc.sbuf_tensor("qk_XB", [128, KC * TS], F32) as XB, \
                 nc.sbuf_tensor("qk_WQH", [128, 3 * SLT], BF16) as WQH, \
                 nc.sbuf_tensor("qk_WQL", [128, 3 * SLT], BF16) as WQL, \
                 nc.sbuf_tensor("qk_RC", [128, 2 * TS], F32) as RC, \
                 nc.sbuf_tensor("qk_RS", [128, 2 * TS], F32) as RS, \
                 nc.sbuf_tensor("qk_QH", [128, KC * TS], BF16) as QH, \
                 nc.sbuf_tensor("qk_QL", [128, KC * TS], BF16) as QL, \
                 nc.sbuf_tensor("qk_KH", [128, KC * TS], BF16) as KH, \
                 nc.sbuf_tensor("qk_KL", [128, KC * TS], BF16) as KL:
                for it in range(NT):
                    ts = it * TS
                    sA_prev = sA.n
                    v_ln = emit_ln(ts, xmid, XA, XB)

                    qk_banks_free = [0]
                    for p in range(2):
                        t2, v2 = cT.n, cV.n
                        qk_banks_free[0] = v2 + NGH
                        gQ0 = stQW.g

                        def fg_qk(gpsimd, gQ0=gQ0, t2=t2, p=p):
                            gpsimd.wait_ge(cT.h, t2)
                            for gi in range(NGH):
                                for kc in range(KC):
                                    i = gi * KC + kc
                                    if i >= 3:
                                        gpsimd.wait_ge(cT.h, t2 + (i - 2))
                                    o = (i % 3) * SLT
                                    co = (p * KC + gi * OCG) * 128
                                    stQW.inc(gQ0 + i, gpsimd.dma_start(
                                        out=WQH[:, o:o + OCG * 128],
                                        in_=qkv_hi[kc][:, co:co + OCG * 128]))
                                    stQW.inc(gQ0 + i, gpsimd.dma_start(
                                        out=WQL[:, o:o + OCG * 128],
                                        in_=qkv_lo[kc][:, co:co + OCG * 128]))
                        block.gpsimd(fg_qk)

                        def ft_qk(tensor, gQ0=gQ0, v2=v2, v_ln=v_ln, p=p):
                            tensor.wait_ge(cV.h, v_ln)
                            tensor.wait_ge(cV.h, v2)
                            for gi in range(NGH):
                                if gi >= 1:
                                    tensor.wait_ge(cV.h, v2 + gi)
                                for kc in range(KC):
                                    i = gi * KC + kc
                                    stQW.wait(tensor, gQ0 + i)
                                    o = (i % 3) * SLT
                                    xh = HI[:, kc * TS:(kc + 1) * TS]
                                    xl = LO[:, kc * TS:(kc + 1) * TS]
                                    for j in range(OCG):
                                        pj = bank(j)
                                        wh = WQH[:, o + j * 128:o + (j + 1) * 128]
                                        wl = WQL[:, o + j * 128:o + (j + 1) * 128]
                                        tensor.matmul(pj, wh, xh, start=(kc == 0), stop=False)
                                        tensor.matmul(pj, wh, xl, start=False, stop=False)
                                        mmi = tensor.matmul(pj, wl, xh, start=False,
                                                            stop=(kc == KC - 1))
                                    cT.inc(mmi)
                        block.tensor(ft_qk)
                        stQW.g += NGH * KC
                        t_qk0 = cT.n - NGH * KC

                        def fv_ev(vector, t_qk0=t_qk0):
                            for gi in range(NGH):
                                vector.wait_ge(cT.h, t_qk0 + (gi + 1) * KC)
                                for j in range(OCG):
                                    oc = gi * OCG + j
                                    vi = vector.tensor_copy(XB[:, oc * TS:(oc + 1) * TS],
                                                            bank(j))
                                cV.inc(vi)
                        block.vector(fv_ev)

                        gR0 = stR.g
                        v_rope0 = cV.n

                        def fv_rope(vector, gR0=gR0, p=p, sA_prev=sA_prev, it=it):
                            if p == 0 and it > 0:
                                vector.wait_ge(sA.h, sA_prev)
                            for c in range(DQC):
                                b = c % 2
                                stR.wait(vector, gR0 + c)
                                s0 = slice(c * TS, (c + 1) * TS)
                                s1_ = slice((DQC + c) * TS, (DQC + c + 1) * TS)
                                rc = RC[:, b * TS:(b + 1) * TS]
                                rs_ = RS[:, b * TS:(b + 1) * TS]
                                vector.tensor_mul(XA[:, 0:TS], XB[:, s1_], rs_)
                                vector.tensor_mul(XA[:, TS:2 * TS], XB[:, s0], rs_)
                                vector.tensor_mul(XB[:, s0], XB[:, s0], rc)
                                vector.tensor_sub(XB[:, s0], XB[:, s0], XA[:, 0:TS])
                                vector.tensor_mul(XB[:, s1_], XB[:, s1_], rc)
                                cV.inc(vector.tensor_add(XB[:, s1_], XB[:, s1_],
                                                         XA[:, TS:2 * TS]))
                            DH_, DL_ = (QH, QL) if p == 0 else (KH, KL)
                            vector.tensor_copy(DH_[:, 0:KC * TS], XB[:, 0:KC * TS])
                            cV.inc(vector.tensor_tensor(DL_[:, 0:KC * TS], XB[:, 0:KC * TS],
                                                        DH_[:, 0:KC * TS], ALU.subtract))
                        block.vector(fv_rope)
                        v_cast = cV.n

                        def fs_rt(sync, ts=ts, v_rope0=v_rope0, gR0=gR0):
                            sync.wait_ge(cV.h, v_rope0)
                            for c in range(DQC):
                                b = c % 2
                                if c >= 2:
                                    sync.wait_ge(cV.h, v_rope0 + (c - 1))
                                stR.inc(gR0 + c, sync.dma_start(out=RC[:, b * TS:(b + 1) * TS],
                                                                in_=cos_t[c][:, ts:ts + TS]))
                                stR.inc(gR0 + c, sync.dma_start(out=RS[:, b * TS:(b + 1) * TS],
                                                                in_=sin_t[c][:, ts:ts + TS]))
                        block.sync(fs_rt)
                        stR.g += DQC

                        def fa_pk(scalar, v_cast=v_cast, ts=ts, p=p, it=it):
                            scalar.wait_ge(cV.h, v_cast)
                            for c in range(KC):
                                if p == 0:
                                    sA.inc(scalar.dma_start(
                                        out=qpark[0, c][:, ts:ts + TS],
                                        in_=QH[:, c * TS:(c + 1) * TS]))
                                    sA.inc(scalar.dma_start(
                                        out=qpark[1, c][:, ts:ts + TS],
                                        in_=QL[:, c * TS:(c + 1) * TS]))
                                else:
                                    sA.inc(scalar.dma_start(
                                        out=k_in[it].ap()[0, c],
                                        in_=KH[:, c * TS:(c + 1) * TS]))
                                    sA.inc(scalar.dma_start(
                                        out=k_in[it].ap()[1, c],
                                        in_=KL[:, c * TS:(c + 1) * TS]))
                        block.scalar(fa_pk)
                        if p == 1:
                            a_kstage = sA.n

                            def fg_agk(gpsimd, a_kstage=a_kstage, it=it):
                                gpsimd.wait_ge(sA.h, a_kstage)
                                cC.inc(gpsimd.collective_compute(
                                    "AllGather", ALU.bypass,
                                    replica_groups=[[2 * i, 2 * i + 1]
                                                    for i in range(NCORES // 2)],
                                    ins=[k_in[it].ap().opt()],
                                    outs=[k_all[it].ap().opt()]))
                            block.gpsimd(fg_agk)
                    a_park = sA.n

                    for dvh in range(2):
                        t3 = cT.n
                        gV0 = stQW.g

                        def fg_v(gpsimd, gV0=gV0, t3=t3, dvh=dvh):
                            gpsimd.wait_ge(cT.h, t3)
                            for kc in range(KC):
                                if kc >= 3:
                                    gpsimd.wait_ge(cT.h, t3 + (kc - 2))
                                o = (kc % 3) * SLT
                                vo = 2 * D + dvh * (D // 2)
                                stQW.inc(gV0 + kc, gpsimd.dma_start(
                                    out=WQH[:, o:o + D // 2],
                                    in_=qkv_hi[kc][:, vo:vo + D // 2]))
                                stQW.inc(gV0 + kc, gpsimd.dma_start(
                                    out=WQL[:, o:o + D // 2],
                                    in_=qkv_lo[kc][:, vo:vo + D // 2]))
                        block.gpsimd(fg_v)

                        def ft_v(tensor, gV0=gV0, dvh=dvh,
                                 v=(qk_banks_free[0] if dvh == 0 else cV.n)):
                            tensor.wait_ge(cV.h, v)
                            for kc in range(KC):
                                stQW.wait(tensor, gV0 + kc)
                                o = (kc % 3) * SLT
                                for t1 in range(NT1):
                                    xh = HI[:, kc * TS + t1 * 128:kc * TS + (t1 + 1) * 128]
                                    xl = LO[:, kc * TS + t1 * 128:kc * TS + (t1 + 1) * 128]
                                    for db_ in range(NDB):
                                        pj = bank(t1 * NDB + db_, VB)
                                        wh = WQH[:, o + db_ * VB:o + (db_ + 1) * VB]
                                        wl = WQL[:, o + db_ * VB:o + (db_ + 1) * VB]
                                        tensor.matmul(pj, xh, wh, start=(kc == 0), stop=False)
                                        tensor.matmul(pj, xh, wl, start=False, stop=False)
                                        mmi = tensor.matmul(pj, xl, wh, start=False,
                                                            stop=(kc == KC - 1))
                                cT.inc(mmi)
                        block.tensor(ft_v)
                        stQW.g += KC
                        t_v_end = cT.n

                        def fv_v(vector, t_v_end=t_v_end, a_park=a_park, dvh=dvh):
                            vector.wait_ge(cT.h, t_v_end)
                            if dvh == 0:
                                vector.wait_ge(sA.h, a_park)
                            n = NT1 * NDB
                            for s_ in range(n):
                                sl = slice(s_ * VB, (s_ + 1) * VB)
                                vector.tensor_copy(XB[:, sl], bank(s_, VB))
                            for s_ in range(n):
                                sl = slice(s_ * VB, (s_ + 1) * VB)
                                vector.tensor_copy(QH[:, sl], XB[:, sl])
                                vi = vector.tensor_tensor(QL[:, sl], XB[:, sl], QH[:, sl],
                                                          ALU.subtract)
                            cV.inc(vi)
                        block.vector(fv_v)
                        v_vc = cV.n

                        def fa_v(scalar, v_vc=v_vc, dvh=dvh, ts=ts, it=it):
                            scalar.wait_ge(cV.h, v_vc)
                            for t1 in range(NT1):
                                for db_ in range(NDB):
                                    s_ = t1 * NDB + db_
                                    sl = slice(s_ * VB, (s_ + 1) * VB)
                                    vo = dvh * (D // 2) + db_ * VB
                                    sA.inc(scalar.dma_start(
                                        out=v_in[it].ap()[0, t1][:, vo:vo + VB],
                                        in_=QH[:, sl]))
                                    sA.inc(scalar.dma_start(
                                        out=v_in[it].ap()[1, t1][:, vo:vo + VB],
                                        in_=QL[:, sl]))
                        block.scalar(fa_v)
                        if dvh == 1:
                            a_vstage = sA.n

                            def fg_agv(gpsimd, a_vstage=a_vstage, it=it):
                                gpsimd.wait_ge(sA.h, a_vstage)
                                cC.inc(gpsimd.collective_compute(
                                    "AllGather", ALU.bypass,
                                    replica_groups=[[2 * i, 2 * i + 1]
                                                    for i in range(NCORES // 2)],
                                    ins=[v_in[it].ap().opt()],
                                    outs=[v_all[it].ap().opt()]))
                            block.gpsimd(fg_agv)
                    barrier()

                gNB0 = stNB.g
                a_vdone = sA.n
                v_nb0 = cV.n
                a_nb0 = cA.n
                t_nb0 = cT.n
                steps = []
                for it2 in range(NT):
                    steps.append(("q", it2))
                    steps.append(("k", it2))

                def fs_nb(sync, gNB0=gNB0, a_vdone=a_vdone, a_nb0=a_nb0):
                    sync.wait_ge(sA.h, a_vdone)
                    for s_, (qk, it2) in enumerate(steps):
                        if s_ >= 2:
                            sync.wait_ge(cA.h, a_nb0 + s_ - 1)
                        buf = QH if s_ % 2 == 0 else QL
                        for c in range(KC):
                            srcap = (qpark[0, c][:, it2 * TS:(it2 + 1) * TS] if qk == "q"
                                     else k_in[it2].ap()[0, c])
                            stNB.inc(gNB0 + s_, sync.dma_start(
                                out=buf[:, c * TS:(c + 1) * TS], in_=srcap))
                block.sync(fs_nb)
                stNB.g += len(steps)

                def fa_nb(scalar, gNB0=gNB0, t_nb0=t_nb0):
                    for s_, (qk, it2) in enumerate(steps):
                        stNB.wait(scalar, gNB0 + s_)
                        if s_ >= 1:
                            scalar.wait_ge(cT.h, t_nb0 + s_)
                        buf = QH if s_ % 2 == 0 else QL
                        cA.inc(scalar.square(KH[:, 0:KC * TS], buf[:, 0:KC * TS]))
                block.scalar(fa_nb)

                def ft_nb(tensor, a_nb0=a_nb0, v_nb0=v_nb0):
                    tensor.wait_ge(cV.h, v_nb0)
                    for s_, (qk, it2) in enumerate(steps):
                        tensor.wait_ge(cA.h, a_nb0 + s_ + 1)
                        bk = 4 if qk == "q" else 5
                        if s_ >= 2:
                            tensor.wait_ge(cV.h, v_nb0 + s_ - 1)
                        for c in range(KC):
                            mmi = tensor.matmul(bankrow(bk), ONESB[:, :],
                                                KH[:, c * TS:(c + 1) * TS],
                                                start=(c == 0), stop=(c == KC - 1))
                        cT.inc(mmi)
                block.tensor(ft_nb)

                def fv_nb(vector, t_nb0=t_nb0, v_nb0=v_nb0):
                    for s_, (qk, it2) in enumerate(steps):
                        vector.wait_ge(cT.h, t_nb0 + s_ + 1)
                        bk = 4 if qk == "q" else 5
                        m = 0 if qk == "q" else 1
                        vector.reduce_max(MROW[:, 4:5], bankrow(bk), axis=AXX)
                        cV.inc(vector.tensor_max(MROW[:, m:m + 1], MROW[:, m:m + 1],
                                                 MROW[:, 4:5]))
                block.vector(fv_nb)

                def fv_m(vector):
                    cV.inc(vector.tensor_mul(MROW[:, 2:3], MROW[:, 0:1], MROW[:, 1:2]))
                block.vector(fv_m)

                def fa_m(scalar, v=cV.n):
                    scalar.wait_ge(cV.h, v)
                    scalar.activation(MROW[:, 3:4], MROW[:, 2:3], AF.Sqrt)
                    cA.inc(scalar.mul(MROW[:, 3:4], MROW[:, 3:4], -float(cfg.SC)))
                block.scalar(fa_m)

                def ft_m(tensor, a=cA.n):
                    tensor.wait_ge(cA.h, a)
                    mmi = tensor.matmul(PS[:, 0:1], ONESR[:, :], MROW[:, 3:4],
                                        start=True, stop=True)
                    cT.inc(mmi)
                block.tensor(ft_m)

                def fv_m2(vector, t=cT.n):
                    vector.wait_ge(cT.h, t)
                    cV.inc(vector.tensor_copy(CONST[:, O_NM:O_NM + 1], PS[:, 0:1]))
                block.vector(fv_m2)
                barrier()

        def p_collective():
            if debug:
                def fg2(gpsimd):
                    gpsimd.wait_ge(cC.h, cC.n)
                    for i in range(NT):
                        gD.inc(gpsimd.dma_start(out=dbg["dbg_k"][i], in_=k_in[i].ap()))
                        gD.inc(gpsimd.dma_start(out=dbg["dbg_v"][i], in_=v_in[i].ap()))
                    gD.inc(gpsimd.dma_start(out=dbg["dbg_q"], in_=qpark))
                    gpsimd.wait_ge(gD.h, gD.n)
                block.gpsimd(fg2)
                barrier()

        def p_attn():
            NKSL = S // 512
            NB = 512 // 128
            with nc.sbuf_tensor("at_EH", [128, T2C * QB], BF16) as EH, \
                 nc.sbuf_tensor("at_EL", [128, T2C * QB], BF16) as EL, \
                 nc.sbuf_tensor("at_QSH", [128, DQC * QB], BF16) as QSH, \
                 nc.sbuf_tensor("at_QSL", [128, DQC * QB], BF16) as QSL, \
                 nc.sbuf_tensor("at_KSH", [128, 2 * DQC * 512], BF16) as KSH, \
                 nc.sbuf_tensor("at_KSL", [128, 2 * DQC * 512], BF16) as KSL, \
                 nc.sbuf_tensor("at_VSH", [128, 2 * DH], BF16) as VSH, \
                 nc.sbuf_tensor("at_VSL", [128, 2 * DH], BF16) as VSL, \
                 nc.sbuf_tensor("at_WOH", [128, 3 * (D // 2)], BF16) as WOH, \
                 nc.sbuf_tensor("at_WOL", [128, 3 * (D // 2)], BF16) as WOL, \
                 nc.sbuf_tensor("at_E32", [128, 2 * 512], F32) as E32, \
                 nc.sbuf_tensor("at_RB", [128, QB], F32) as RB, \
                 nc.sbuf_tensor("at_OB", [128, DQC * QB], F32) as OB, \
                 nc.sbuf_tensor("at_XS", [128, 2 * 512], F32) as XS, \
                 nc.sbuf_tensor("at_YS", [128, 2 * 512], F32) as YS:
                prev_t_av = [0]
                for qb in range(NQB):
                    tq = qb * QB
                    for h in range(H):
                        v_prev = cV.n
                        gQb = stQ.g
                        gKb = stK.g

                        def fs_q(sync, tq=tq, h=h, v_prev=v_prev, qb=qb, gQb=gQb):
                            if not (qb == 0 and h == 0):
                                sync.wait_ge(cV.h, v_prev)
                            for ver, dst_ in ((0, QSH), (1, QSL)):
                                for c in range(DQC):
                                    stQ.inc(gQb, sync.dma_start(
                                        out=dst_[:, c * QB:(c + 1) * QB],
                                        in_=qpark[ver, h * DQC + c][:, tq:tq + QB]))
                        block.sync(fs_q)
                        stQ.g += 1

                        t4, v4, a4 = cT.n, cV.n, cA.n

                        def fs_k(sync, gKb=gKb, t4=t4, h=h, qb=qb):
                            for sl_ in range(NKSL):
                                r, lo_ = sl_ // (NKSL // 2), (sl_ % (NKSL // 2)) * 512
                                kt, off = lo_ // TS, lo_ % TS
                                if qb == 0 and h == 0:
                                    sync.wait_ge(cC.h, 2 * kt + 1)
                                if sl_ >= 2:
                                    sync.wait_ge(cT.h, t4 + (sl_ - 1) * NB)
                                ko = (sl_ % 2) * DQC * 512
                                for ver, dst_ in ((0, KSH), (1, KSL)):
                                    for c in range(DQC):
                                        stK.inc(gKb + sl_, sync.dma_start(
                                            out=dst_[:, ko + c * 512:ko + (c + 1) * 512],
                                            in_=k_all[kt].ap()[r, ver, h * DQC + c]
                                                [:, off:off + 512]))
                        block.sync(fs_k)
                        stK.g += NKSL

                        def ft_s(tensor, gQb=gQb, gKb=gKb, a4=a4, v4=v4, v_prev=v_prev,
                                 qb=qb, h=h):
                            if not (qb == 0 and h == 0):
                                tensor.wait_ge(cV.h, v_prev)
                            stQ.wait(tensor, gQb)
                            for sl_ in range(NKSL):
                                stK.wait(tensor, gKb + sl_)
                                ko = (sl_ % 2) * DQC * 512
                                for b in range(NB):
                                    t2c = sl_ * NB + b
                                    pj = bank(t2c % 4, QB)
                                    if t2c >= 4:
                                        tensor.wait_ge(cA.h, a4 + (t2c - 3))
                                    for c in range(DQC):
                                        kh = KSH[:, ko + c * 512 + b * 128:ko + c * 512 + (b + 1) * 128]
                                        kl = KSL[:, ko + c * 512 + b * 128:ko + c * 512 + (b + 1) * 128]
                                        qh = QSH[:, c * QB:(c + 1) * QB]
                                        ql = QSL[:, c * QB:(c + 1) * QB]
                                        tensor.matmul(pj, kh, qh, start=(c == 0), stop=False)
                                        tensor.matmul(pj, kh, ql, start=False, stop=False)
                                        mmi = tensor.matmul(pj, kl, qh, start=False,
                                                            stop=(c == DQC - 1))
                                    cT.inc(mmi)
                            for t2c in range(T2C):
                                tensor.wait_ge(cV.h, v4 + t2c + 1)
                                tensor.matmul(bankrow(4, QB), ONESB[:, :],
                                              EH[:, t2c * QB:(t2c + 1) * QB],
                                              start=(t2c == 0), stop=False)
                                mmi = tensor.matmul(bankrow(4, QB), ONESB[:, :],
                                                    EL[:, t2c * QB:(t2c + 1) * QB],
                                                    start=False, stop=(t2c == T2C - 1))
                            cT.inc(mmi)
                        block.tensor(ft_s)
                        t_sums = cT.n

                        def fa_e(scalar, t4=t4, v4=v4):
                            for t2c in range(T2C):
                                scalar.wait_ge(cT.h, t4 + t2c + 1)
                                if t2c >= 2:
                                    scalar.wait_ge(cV.h, v4 + t2c - 1)
                                cA.inc(scalar.activation(
                                    E32[:, (t2c % 2) * 512:(t2c % 2) * 512 + QB],
                                    bank(t2c % 4, QB), AF.Exp,
                                    bias=CONST[:, O_NM:O_NM + 1], scale=float(cfg.SC)))
                        block.scalar(fa_e)

                        def fv_e(vector, a4=a4):
                            for t2c in range(T2C):
                                vector.wait_ge(cA.h, a4 + t2c + 1)
                                sl_ = slice(t2c * QB, (t2c + 1) * QB)
                                e3 = E32[:, (t2c % 2) * 512:(t2c % 2) * 512 + QB]
                                vector.tensor_copy(EH[:, sl_], e3)
                                cV.inc(vector.tensor_tensor(EL[:, sl_], e3, EH[:, sl_],
                                                            ALU.subtract))
                        block.vector(fv_e)

                        def fv_r(vector, t=t_sums):
                            vector.wait_ge(cT.h, t)
                            cV.inc(vector.reciprocal(SMALL[:, 0:QB], bankrow(4, QB)))
                        block.vector(fv_r)

                        def ft_rb(tensor, v=cV.n):
                            tensor.wait_ge(cV.h, v)
                            mmi = tensor.matmul(bank(5, QB), ONESR[:, :], SMALL[:, 0:QB],
                                                start=True, stop=True)
                            cT.inc(mmi)
                        block.tensor(ft_rb)

                        def fv_rb(vector, t=cT.n):
                            vector.wait_ge(cT.h, t)
                            cV.inc(vector.tensor_copy(RB[:, 0:QB], bank(5, QB)))
                        block.vector(fv_rb)
                        v_rb = cV.n

                        gVb = stV.g
                        tav_base = cT.n

                        def fg_v(gpsimd, gVb=gVb, t5=prev_t_av[0], tav_base=tav_base,
                                 h=h, qb=qb):
                            gpsimd.wait_ge(cT.h, t5)
                            for t2c in range(T2C):
                                r, lc = t2c // (T2C // 2), t2c % (T2C // 2)
                                vt, t1l = lc // NT1_, lc % NT1_
                                if qb == 0 and h == 0:
                                    gpsimd.wait_ge(cC.h, 2 * vt + 2)
                                if t2c >= 2:
                                    gpsimd.wait_ge(cT.h, tav_base + t2c - 1)
                                o = (t2c % 2) * DH
                                stV.inc(gVb + t2c, gpsimd.dma_start(
                                    out=VSH[:, o:o + DH],
                                    in_=v_all[vt].ap()[r, 0, t1l][:, h * DH:(h + 1) * DH]))
                                stV.inc(gVb + t2c, gpsimd.dma_start(
                                    out=VSL[:, o:o + DH],
                                    in_=v_all[vt].ap()[r, 1, t1l][:, h * DH:(h + 1) * DH]))
                        block.gpsimd(fg_v)

                        def ft_av(tensor, gVb=gVb, v_rb=v_rb):
                            tensor.wait_ge(cV.h, v_rb)
                            for t2c in range(T2C):
                                stV.wait(tensor, gVb + t2c)
                                o = (t2c % 2) * DH
                                eh = EH[:, t2c * QB:(t2c + 1) * QB]
                                el = EL[:, t2c * QB:(t2c + 1) * QB]
                                for c in range(DQC):
                                    pj = bank(c, QB)
                                    vh = VSH[:, o + c * 128:o + (c + 1) * 128]
                                    vl = VSL[:, o + c * 128:o + (c + 1) * 128]
                                    tensor.matmul(pj, vh, eh, start=(t2c == 0), stop=False)
                                    tensor.matmul(pj, vh, el, start=False, stop=False)
                                    mmi = tensor.matmul(pj, vl, eh, start=False,
                                                        stop=(t2c == T2C - 1))
                                cT.inc(mmi)
                        block.tensor(ft_av)
                        stV.g += T2C
                        t_av = cT.n
                        prev_t_av[0] = t_av

                        def fv_o(vector, t_av=t_av, v_rb=v_rb, h=h):
                            vector.wait_ge(cT.h, t_av)
                            vector.wait_ge(cV.h, v_rb)
                            for c in range(DQC):
                                sl_ = slice(c * QB, (c + 1) * QB)
                                vector.tensor_mul(OB[:, sl_], bank(c, QB), RB[:, 0:QB])
                                g = h * DQC + c
                                vector.tensor_copy(HI[:, g * QB:(g + 1) * QB], OB[:, sl_])
                                vi = vector.tensor_tensor(LO[:, g * QB:(g + 1) * QB], OB[:, sl_],
                                                          HI[:, g * QB:(g + 1) * QB],
                                                          ALU.subtract)
                            cV.inc(vi)
                        block.vector(fv_o)

                    v_oh = cV.n
                    for d2h in range(2):
                        t6 = cT.n
                        gWb = stQW.g

                        def fg_wo(gpsimd, gWb=gWb, t6=t6, d2h=d2h):
                            gpsimd.wait_ge(cT.h, t6)
                            for dvc in range(KC):
                                if dvc >= 3:
                                    gpsimd.wait_ge(cT.h, t6 + dvc - 2)
                                o = (dvc % 3) * (D // 2)
                                stQW.inc(gWb + dvc, gpsimd.dma_start(
                                    out=WOH[:, o:o + D // 2],
                                    in_=wo_hi[dvc][:, d2h * (D // 2):(d2h + 1) * (D // 2)]))
                                stQW.inc(gWb + dvc, gpsimd.dma_start(
                                    out=WOL[:, o:o + D // 2],
                                    in_=wo_lo[dvc][:, d2h * (D // 2):(d2h + 1) * (D // 2)]))
                        block.gpsimd(fg_wo)

                        def ft_wo(tensor, gWb=gWb, v_oh=v_oh, v=cV.n):
                            tensor.wait_ge(cV.h, max(v_oh, v))
                            for dvc in range(KC):
                                stQW.wait(tensor, gWb + dvc)
                                o = (dvc % 3) * (D // 2)
                                oh = HI[:, dvc * QB:(dvc + 1) * QB]
                                ol = LO[:, dvc * QB:(dvc + 1) * QB]
                                for j in range(D2C // 2):
                                    pj = bank(j, QB)
                                    wh = WOH[:, o + j * 128:o + (j + 1) * 128]
                                    wl = WOL[:, o + j * 128:o + (j + 1) * 128]
                                    tensor.matmul(pj, wh, oh, start=(dvc == 0), stop=False)
                                    tensor.matmul(pj, wh, ol, start=False, stop=False)
                                    mmi = tensor.matmul(pj, wl, oh, start=False,
                                                        stop=(dvc == KC - 1))
                                cT.inc(mmi)
                        block.tensor(ft_wo)
                        stQW.g += KC
                        t_wo = cT.n

                        v_ev0 = cV.n
                        gXb = stX.g

                        def fs_x(sync, tq=tq, d2h=d2h, v_ev0=v_ev0, gXb=gXb):
                            sync.wait_ge(cV.h, v_ev0)
                            for j in range(D2C // 2):
                                c = d2h * (D2C // 2) + j
                                if j >= 2:
                                    sync.wait_ge(cV.h, v_ev0 + j - 1)
                                stX.inc(gXb + j, sync.dma_start(
                                    out=XS[:, (j % 2) * 512:(j % 2) * 512 + QB],
                                    in_=xmid[c][:, tq:tq + QB]))
                        block.sync(fs_x)
                        stX.g += D2C // 2

                        def fv_y(vector, t_wo=t_wo, gXb=gXb, d2h=d2h, s_ys=sA.n):
                            vector.wait_ge(cT.h, t_wo)
                            if d2h > 0 or True:
                                vector.wait_ge(sA.h, s_ys)
                            for j in range(D2C // 2):
                                c = d2h * (D2C // 2) + j
                                stX.wait(vector, gXb + j)
                                cV.inc(vector.scalar_tensor_tensor(
                                    YS[:, (j % 2) * 512:(j % 2) * 512 + QB], bank(j, QB),
                                    CONST[:, O_BO + c:O_BO + c + 1],
                                    XS[:, (j % 2) * 512:(j % 2) * 512 + QB],
                                    ALU.add, ALU.add))
                        block.vector(fv_y)
                        v_y0 = cV.n - D2C // 2

                        def fa_st(scalar, v_y0=v_y0, tq=tq, d2h=d2h):
                            for j in range(D2C // 2):
                                c = d2h * (D2C // 2) + j
                                scalar.wait_ge(cV.h, v_y0 + j + 1)
                                sA.inc(scalar.dma_start(
                                    out=xmid2[c][:, tq:tq + QB],
                                    in_=YS[:, (j % 2) * 512:(j % 2) * 512 + QB]))
                        block.scalar(fa_st)
                    barrier()
                if debug:
                    def fg3(gpsimd):
                        gD.inc(gpsimd.dma_start(out=dbg["dbg_xmid2"], in_=xmid2))
                        gpsimd.wait_ge(gD.h, gD.n)
                    block.gpsimd(fg3)
                    barrier()

        PH = phases if phases is not None else ["consts", "wconv", "ffn1", "qkv",
                                                "cc", "attn", "ffn2"]
        if "consts" in PH:
            p_consts()
        if "wconv" in PH:
            p_wconv()
        if "ffn1" in PH:
            p_ffn(x_t, xmid, "f1")
        if "ffn1ln" in PH:
            p_ffn(x_t, xmid, "f1", parts=("ln",))
        if "ffn1gu" in PH:
            p_ffn(x_t, xmid, "f1", parts=("ln", "gu"))
        if debug:
            def fgd(gpsimd):
                gD.inc(gpsimd.dma_start(out=dbg["dbg_xmid"], in_=xmid))
                gpsimd.wait_ge(gD.h, gD.n)
            block.gpsimd(fgd)
            barrier()
        if "qkv" in PH:
            p_qkv()
        if "cc" in PH:
            p_collective()
        if "attn" in PH:
            p_attn()
        if "ffn2" in PH:
            p_ffn(xmid2, out_t, "f2", final=True)

    return nc



def _rope_tables(seq_len, head_dim):
    k = np.arange(0, head_dim, 2, dtype=np.float32) / np.float32(head_dim)
    inv_freq = (1.0 / (10000.0 ** k)).astype(np.float32)
    t = np.arange(seq_len, dtype=np.float32)
    freqs = (t[:, None] * inv_freq[None, :]).astype(np.float32)
    emb = np.concatenate([freqs, freqs], axis=-1)
    cos = np.cos(emb.astype(np.float64)).astype(np.float32)
    sin = np.sin(emb.astype(np.float64)).astype(np.float32)
    return cos, sin


def make_inputs(cfg, b, half, x, wqkv, wout, bout, w_gate, w_up, w_down,
                s_gate, s_up, s_down):
    D, I, T, KC, IC, DQC = cfg.D, cfg.I, cfg.T, cfg.KC, cfg.IC, cfg.DQC
    cos, sin = _rope_tables(cfg.S, cfg.DH)
    sl = slice(half * T, (half + 1) * T)
    return {
        "x_t": np.ascontiguousarray(
            np.asarray(x[b, sl, :], dtype=np.float32).T.reshape(KC, 128, T)),
        "wqkv_t": np.ascontiguousarray(np.asarray(wqkv, np.float32).reshape(KC, 128, 3 * D)),
        "wout_t": np.ascontiguousarray(np.asarray(wout, np.float32).reshape(KC, 128, D)),
        "wg_t": np.ascontiguousarray(
            np.asarray(w_gate).reshape(KC, 128, IC, 128).transpose(2, 1, 0, 3)
            .reshape(IC, 128, D)),
        "wu_t": np.ascontiguousarray(
            np.asarray(w_up).reshape(KC, 128, IC, 128).transpose(2, 1, 0, 3)
            .reshape(IC, 128, D)),
        "wd_t": np.ascontiguousarray(np.asarray(w_down).reshape(IC, 128, D)),
        "sg_pp": np.ascontiguousarray(np.asarray(s_gate, np.float32).reshape(IC, 128).T),
        "su_pp": np.ascontiguousarray(np.asarray(s_up, np.float32).reshape(IC, 128).T),
        "sd_pp": np.ascontiguousarray(np.asarray(s_down, np.float32).reshape(KC, 128).T),
        "bout_pp": np.ascontiguousarray(np.asarray(bout, np.float32).reshape(KC, 128).T),
        "cos_t": np.ascontiguousarray(cos[sl, :].T.reshape(DQC, 128, T)),
        "sin_t": np.ascontiguousarray(sin[sl, :].T.reshape(DQC, 128, T)),
    }


def kernel(L0_attn_x, L0_attn_wqkv, L0_attn_wout, L0_attn_bout, L0_ffn_w_gate,
           L0_ffn_w_up, L0_ffn_w_down, L0_ffn_s_gate, L0_ffn_s_up, L0_ffn_s_down):
    cfg = CFG()
    x = np.asarray(L0_attn_x, dtype=np.float32)
    B = x.shape[0]
    nc = build_program(cfg)
    in_maps = []
    for core in range(NCORES):
        b, half = core // 2, core % 2
        in_maps.append(make_inputs(cfg, b, half, x, L0_attn_wqkv, L0_attn_wout,
                                   L0_attn_bout, L0_ffn_w_gate, L0_ffn_w_up,
                                   L0_ffn_w_down, L0_ffn_s_gate, L0_ffn_s_up,
                                   L0_ffn_s_down))
    res = run_bass_kernel_spmd(nc, in_maps, core_ids=list(range(NCORES)))
    out = np.empty((B, cfg.S, cfg.D), dtype=np.float32)
    for core in range(NCORES):
        b, half = core // 2, core % 2
        o = res.results[core]["out_t"].reshape(cfg.D, cfg.T)
        out[b, half * cfg.T:(half + 1) * cfg.T, :] = o.T
    return out


# version 12
# speedup vs baseline: 1.0687x; 1.0158x over previous
import contextlib
import math

import numpy as np
import ml_dtypes

import concourse.bass as bass
import concourse.mybir as mybir
from concourse.bass_utils import run_bass_kernel_spmd

F32 = mybir.dt.float32
BF16 = mybir.dt.bfloat16
I8 = mybir.dt.int8
AF = mybir.ActivationFunctionType
ALU = mybir.AluOpType
AXX = mybir.AxisListType.X

NCORES = 8


class CFG:
    def __init__(self, D=2048, I=8192, T=2048, H=2, TS=512, QB=512, eps=1e-5):
        self.D, self.I, self.T, self.H = D, I, T, H
        self.S = 2 * T
        self.DH = D // H
        self.TS, self.QB = TS, QB
        self.KC = D // 128
        self.IC = I // 128
        self.NT = T // TS
        self.NQB = T // QB
        self.T2C = self.S // 128
        self.DQC = self.DH // 128
        self.SC = 1.0 / math.sqrt(self.DH)
        self.eps = eps


def build_program(cfg: CFG, debug=False, phases=None):
    nc = bass.Bass("TRN2", target_bir_lowering=False, detect_race_conditions=False)
    D, I, T, H = cfg.D, cfg.I, cfg.T, cfg.H
    KC, IC, TS, NT = cfg.KC, cfg.IC, cfg.TS, cfg.NT
    QB, NQB, T2C, DQC = cfg.QB, cfg.NQB, cfg.T2C, cfg.DQC
    DH, S = cfg.DH, cfg.S
    D2C = KC

    x_t = nc.dram_tensor("x_t", [KC, 128, T], F32, kind="ExternalInput").ap()
    wqkv_t = nc.dram_tensor("wqkv_t", [KC, 128, 3 * D], F32, kind="ExternalInput").ap()
    wout_t = nc.dram_tensor("wout_t", [KC, 128, D], F32, kind="ExternalInput").ap()
    wg_t = nc.dram_tensor("wg_t", [IC, 128, D], I8, kind="ExternalInput").ap()
    wu_t = nc.dram_tensor("wu_t", [IC, 128, D], I8, kind="ExternalInput").ap()
    wd_t = nc.dram_tensor("wd_t", [IC, 128, D], I8, kind="ExternalInput").ap()
    sg_pp = nc.dram_tensor("sg_pp", [128, IC], F32, kind="ExternalInput").ap()
    su_pp = nc.dram_tensor("su_pp", [128, IC], F32, kind="ExternalInput").ap()
    sd_pp = nc.dram_tensor("sd_pp", [128, KC], F32, kind="ExternalInput").ap()
    bout_pp = nc.dram_tensor("bout_pp", [128, KC], F32, kind="ExternalInput").ap()
    cos_t = nc.dram_tensor("cos_t", [DQC, 128, T], F32, kind="ExternalInput").ap()
    sin_t = nc.dram_tensor("sin_t", [DQC, 128, T], F32, kind="ExternalInput").ap()

    out_t = nc.dram_tensor("out_t", [KC, 128, T], F32, kind="ExternalOutput").ap()

    xmid = nc.dram_tensor("xmid", [KC, 128, T], F32, kind="Internal").ap()
    xmid2 = nc.dram_tensor("xmid2", [KC, 128, T], F32, kind="Internal").ap()
    qkv_hi = nc.dram_tensor("qkv_hi", [KC, 128, 3 * D], BF16, kind="Internal").ap()
    qkv_lo = nc.dram_tensor("qkv_lo", [KC, 128, 3 * D], BF16, kind="Internal").ap()
    wo_hi = nc.dram_tensor("wo_hi", [KC, 128, D], BF16, kind="Internal").ap()
    wo_lo = nc.dram_tensor("wo_lo", [KC, 128, D], BF16, kind="Internal").ap()
    fuse_d = nc.dram_tensor("fuse_d", [IC, 2, 128, TS], BF16, kind="Internal").ap()
    qpark = nc.dram_tensor("qpark", [2, KC, 128, T], BF16, kind="Internal").ap()
    NT_ = T // TS
    NT1_ = TS // 128
    k_in = [nc.dram_tensor(f"k_in{i}", [2, KC, 128, TS], BF16, kind="Internal")
            for i in range(NT_)]
    k_all = [nc.dram_tensor(f"k_all{i}", [2, 2, KC, 128, TS], BF16, kind="Internal")
             for i in range(NT_)]
    v_in = [nc.dram_tensor(f"v_in{i}", [2, NT1_, 128, D], BF16, kind="Internal")
            for i in range(NT_)]
    v_all = [nc.dram_tensor(f"v_all{i}", [2, 2, NT1_, 128, D], BF16, kind="Internal")
             for i in range(NT_)]

    dbg = {}
    if debug:
        dbg["dbg_xmid"] = nc.dram_tensor("dbg_xmid", [KC, 128, T], F32,
                                         kind="ExternalOutput").ap()
        dbg["dbg_xmid2"] = nc.dram_tensor("dbg_xmid2", [KC, 128, T], F32,
                                          kind="ExternalOutput").ap()
        dbg["dbg_k"] = nc.dram_tensor("dbg_k", [NT_, 2, KC, 128, TS], BF16,
                                      kind="ExternalOutput").ap()
        dbg["dbg_v"] = nc.dram_tensor("dbg_v", [NT_, 2, NT1_, 128, D], BF16,
                                      kind="ExternalOutput").ap()
        dbg["dbg_q"] = nc.dram_tensor("dbg_q", [2, KC, 128, T], BF16,
                                      kind="ExternalOutput").ap()

    ones_col_f32 = nc.inline_tensor(np.ones((128, 1), dtype=np.float32), "ones_col_f32")
    ones_row_f32 = nc.inline_tensor(np.ones((1, 128), dtype=np.float32), "ones_row_f32")
    ones_col_bf = nc.inline_tensor(np.ones((128, 1), dtype=ml_dtypes.bfloat16), "ones_col_bf")

    class Ctr:
        def __init__(self, handle, step=1):
            self.h, self.n, self.step = handle, 0, step

        def inc(self, ins, k=None):
            k = self.step if k is None else k
            self.n += k
            ins.then_inc(self.h, k)
            return self.n

    class DStream:

        def __init__(self, ctrs, dpg):
            self.c = ctrs
            self.dpg = dpg
            self.g = 0

        def inc(self, g, ins):
            self.c[g % len(self.c)].inc(ins)

        def wait(self, eng, g):
            eng.wait_ge(self.c[g % len(self.c)].h,
                        16 * self.dpg * (g // len(self.c) + 1))

    with contextlib.ExitStack() as stk:
        e = stk.enter_context

        O_SG, O_SU = 0, IC
        O_SD, O_BO = 2 * IC, 2 * IC + D2C
        O_OC = 2 * IC + 2 * D2C
        O_NM = O_OC + 1
        CONST = e(nc.sbuf_tensor("CONST", [128, O_NM + 1], F32))
        ONESB = e(nc.sbuf_tensor("ONESB", [128, 1], BF16))
        ONESR = e(nc.sbuf_tensor("ONESR", [1, 128], F32))
        SMALL = e(nc.sbuf_tensor("SMALL", [1, 4 * max(TS, QB)], F32))
        MROW = e(nc.sbuf_tensor("MROW", [1, 8], F32))
        HI = e(nc.sbuf_tensor("HI", [128, KC * TS], BF16))
        LO = e(nc.sbuf_tensor("LO", [128, KC * TS], BF16))
        PS = e(nc.psum_tensor("PS", [128, 8 * 512], F32))

        def bank(j, w=None):
            w = TS if w is None else w
            return PS[:, j * 512:j * 512 + w]

        def bankrow(j, w=None):
            w = TS if w is None else w
            return PS[0:1, j * 512:j * 512 + w]

        sS = Ctr(e(nc.semaphore("sS")), 16)
        sA = Ctr(e(nc.semaphore("sA")), 16)
        gD = Ctr(e(nc.semaphore("gD")), 16)
        cT = Ctr(e(nc.semaphore("cT")))
        cV = Ctr(e(nc.semaphore("cV")))
        cA = Ctr(e(nc.semaphore("cA")))
        cC = Ctr(e(nc.semaphore("cC")))
        bar = {n: Ctr(e(nc.semaphore("b" + n))) for n in "STVAG"}

        def mkst(name, ns, dpg):
            return DStream([Ctr(e(nc.semaphore(f"{name}{i}")), 16)
                            for i in range(ns)], dpg)

        stL = mkst("stL", 2, 1)
        stWC = mkst("stWC", 2, 2)
        stW = mkst("stW", 2, 2)
        stWD = mkst("stWD", 2, 1)
        stF = mkst("stF", 4, 2)
        stAF = mkst("stAF", 2, 2)
        stR = mkst("stR", 2, 2)
        stQW = mkst("stQW", 2, 2)
        stK = mkst("stK", 2, 2 * DQC)
        stQ = mkst("stQ", 1, 2 * DQC)
        stV = mkst("stV", 2, 2)
        stX = mkst("stX", 2, 1)
        stNB = mkst("stNB", 2, KC)
        block = e(nc.Block())
        ENG = {"S": block.sync, "T": block.tensor, "V": block.vector,
               "A": block.scalar, "G": block.gpsimd}

        nbar = [0]

        def barrier():
            nbar[0] += 1
            k = nbar[0]
            for name in "STVAG":
                def f(eng, name=name, k=k):
                    eng.nop().then_inc(bar[name].h, 1)
                    for o in "STVAG":
                        if o != name:
                            eng.wait_ge(bar[o].h, k)
                ENG[name](f)
                bar[name].n += 1

        def p_consts():
            def fsync(sync):
                for dst_, src_ in [(CONST[:, O_SG:O_SG + IC], sg_pp),
                                   (CONST[:, O_SU:O_SU + IC], su_pp),
                                   (CONST[:, O_SD:O_SD + D2C], sd_pp),
                                   (CONST[:, O_BO:O_BO + D2C], bout_pp),
                                   (CONST[:, O_OC:O_OC + 1], ones_col_f32.ap()),
                                   (ONESB[:, :], ones_col_bf.ap()),
                                   (ONESR[:, :], ones_row_f32.ap())]:
                    sS.inc(sync.dma_start(out=dst_, in_=src_))
                sync.wait_ge(sS.h, sS.n)
            block.sync(fsync)

            def fscal(scalar):
                scalar.wait_ge(sS.h, sS.n)
                cA.inc(scalar.mul(CONST[:, O_SD:O_SD + D2C],
                                  CONST[:, O_SD:O_SD + D2C], 0.5))
            block.scalar(fscal)

            def fvec(vector):
                vector.memset(MROW[:, :], 0.0)
                cV.inc(vector.memset(MROW[:, 7:8], float(cfg.eps)))
            block.vector(fvec)
            barrier()

        def p_wconv():
            W = 3 * D
            tiles = [(wqkv_t, qkv_hi, qkv_lo, 3 * D, kc) for kc in range(KC)] + \
                    [(wout_t, wo_hi, wo_lo, D, kc) for kc in range(KC)]
            with nc.sbuf_tensor("cv_W32", [128, 2 * W], F32) as W32, \
                 nc.sbuf_tensor("cv_WHI", [128, 2 * W], BF16) as WHI, \
                 nc.sbuf_tensor("cv_WLO", [128, 2 * W], BF16) as WLO:
                cV0 = cV.n
                gL, gC = stL.g, stWC.g

                def fsync(sync):
                    for i, (src, _, _, w, kc) in enumerate(tiles):
                        if i >= 2:
                            sync.wait_ge(cV.h, cV0 + (i - 1))
                        stL.inc(gL + i, sync.dma_start(
                            out=W32[:, (i % 2) * W:(i % 2) * W + w], in_=src[kc]))
                block.sync(fsync)

                def fvec(vector):
                    for i, (_, _, _, w, kc) in enumerate(tiles):
                        o = (i % 2) * W
                        stL.wait(vector, gL + i)
                        if i >= 2:
                            stWC.wait(vector, gC + i - 2)
                        vector.tensor_copy(WHI[:, o:o + w], W32[:, o:o + w])
                        cV.inc(vector.tensor_tensor(WLO[:, o:o + w], W32[:, o:o + w],
                                                    WHI[:, o:o + w], ALU.subtract))
                block.vector(fvec)

                def fscal(scalar):
                    for i, (_, dhi, dlo, w, kc) in enumerate(tiles):
                        o = (i % 2) * W
                        scalar.wait_ge(cV.h, cV0 + (i + 1))
                        stWC.inc(gC + i, scalar.dma_start(out=dhi[kc], in_=WHI[:, o:o + w]))
                        stWC.inc(gC + i, scalar.dma_start(out=dlo[kc], in_=WLO[:, o:o + w]))
                block.scalar(fscal)
                stL.g += len(tiles)
                stWC.g += len(tiles)
                barrier()

        def emit_ln(ts, src, XA, XB):
            def fsync(sync):
                for c in range(KC):
                    sS.inc(sync.dma_start(out=XA[:, c * TS:(c + 1) * TS],
                                          in_=src[c][:, ts:ts + TS]))
            block.sync(fsync)
            s_loaded = sS.n

            def fscal(scalar):
                scalar.wait_ge(sS.h, s_loaded)
                cA.inc(scalar.square(XB[:, 0:KC * TS], XA[:, 0:KC * TS]))
            block.scalar(fscal)
            a_sq = cA.n

            def ftens(tensor):
                tensor.wait_ge(sS.h, s_loaded)
                for c in range(KC):
                    tensor.matmul(bankrow(6), CONST[:, O_OC:O_OC + 1],
                                  XA[:, c * TS:(c + 1) * TS],
                                  start=(c == 0), stop=(c == KC - 1))
                tensor.wait_ge(cA.h, a_sq)
                for c in range(KC):
                    mmi = tensor.matmul(bankrow(7), CONST[:, O_OC:O_OC + 1],
                                        XB[:, c * TS:(c + 1) * TS],
                                        start=(c == 0), stop=(c == KC - 1))
                cT.inc(mmi)
            block.tensor(ftens)
            t_stats = cT.n

            def fscal2(scalar):
                scalar.wait_ge(cT.h, t_stats)
                scalar.mul(SMALL[:, 0:TS], bankrow(6), 1.0 / D)
                scalar.mul(SMALL[:, TS:2 * TS], bankrow(7), 1.0 / D)
                cA.inc(scalar.square(SMALL[:, 2 * TS:3 * TS], SMALL[:, 0:TS]))
            block.scalar(fscal2)
            a_m = cA.n

            def fvec(vector):
                vector.wait_ge(cA.h, a_m)
                cV.inc(vector.tensor_sub(SMALL[:, TS:2 * TS], SMALL[:, TS:2 * TS],
                                         SMALL[:, 2 * TS:3 * TS]))
            block.vector(fvec)
            v_var = cV.n

            def fscal3(scalar):
                scalar.wait_ge(cV.h, v_var)
                cA.inc(scalar.activation(SMALL[:, 2 * TS:3 * TS], SMALL[:, TS:2 * TS],
                                         AF.Sqrt, bias=MROW[:, 7:8]))
            block.scalar(fscal3)
            a_sd = cA.n

            def fvec2(vector):
                vector.wait_ge(cA.h, a_sd)
                cV.inc(vector.reciprocal(SMALL[:, TS:2 * TS], SMALL[:, 2 * TS:3 * TS]))
            block.vector(fvec2)
            v_r = cV.n

            def ftens2(tensor):
                tensor.wait_ge(cV.h, v_r)
                tensor.matmul(bank(4), ONESR[:, :], SMALL[:, 0:TS], start=True, stop=True)
                mmi = tensor.matmul(bank(5), ONESR[:, :], SMALL[:, TS:2 * TS],
                                    start=True, stop=True)
                cT.inc(mmi)
            block.tensor(ftens2)
            t_b = cT.n

            v_ln0 = cV.n

            def fvec3(vector):
                vector.wait_ge(cT.h, t_b)
                for c in range(KC):
                    sl = slice(c * TS, (c + 1) * TS)
                    vector.tensor_sub(XB[:, sl], XA[:, sl], bank(4))
                    vector.tensor_mul(XB[:, sl], XB[:, sl], bank(5))
                    vector.tensor_copy(HI[:, sl], XB[:, sl])
                    cV.inc(vector.tensor_tensor(LO[:, sl], XB[:, sl],
                                                HI[:, sl], ALU.subtract))
            block.vector(fvec3)
            cV.n = cV.n
            return v_ln0

        def p_ffn(src, dst, tag, final=False, parts=("ln", "gu", "dn")):
            with nc.sbuf_tensor(f"{tag}_XA", [128, KC * TS], F32) as XA, \
                 nc.sbuf_tensor(f"{tag}_XB", [128, KC * TS], F32) as XB, \
                 nc.sbuf_tensor(f"{tag}_WG", [128, 2 * D], BF16) as WG, \
                 nc.sbuf_tensor(f"{tag}_WU", [128, 2 * D], BF16) as WU, \
                 nc.sbuf_tensor(f"{tag}_WD", [128, 2 * D], BF16) as WD, \
                 nc.sbuf_tensor(f"{tag}_GS", [128, 2 * TS], F32) as GS, \
                 nc.sbuf_tensor(f"{tag}_FH", [128, 2 * TS], BF16) as FH, \
                 nc.sbuf_tensor(f"{tag}_FL", [128, 2 * TS], BF16) as FL, \
                 nc.sbuf_tensor(f"{tag}_FHR", [128, 4 * TS], BF16) as FHR, \
                 nc.sbuf_tensor(f"{tag}_FLR", [128, 4 * TS], BF16) as FLR:
                for it in range(NT):
                    ts = it * TS
                    v_ln = emit_ln(ts, src, XA, XB)
                    if "gu" not in parts:
                        barrier()
                        continue

                    t0, v0, a0 = cT.n, cV.n, cA.n
                    gW0 = stW.g

                    def fg(gpsimd, gW0=gW0, t0=t0):
                        for ic in range(IC):
                            if ic >= 2:
                                gpsimd.wait_ge(cT.h, t0 + (ic - 1))
                            o = (ic % 2) * D
                            stW.inc(gW0 + ic, gpsimd.dma_start(out=WG[:, o:o + D], in_=wg_t[ic]))
                            stW.inc(gW0 + ic, gpsimd.dma_start(out=WU[:, o:o + D], in_=wu_t[ic]))
                    block.gpsimd(fg)

                    def ft(tensor, gW0=gW0, v0=v0, v_ln=v_ln):
                        for ic in range(IC):
                            stW.wait(tensor, gW0 + ic)
                            if ic >= 2:
                                tensor.wait_ge(cV.h, v0 + (ic - 1))
                            o = (ic % 2) * D
                            pg = bank(ic % 2)
                            pu = bank(2 + ic % 2)
                            for kc in range(KC):
                                if ic == 0:
                                    tensor.wait_ge(cV.h, v_ln + kc + 1)
                                wgc = WG[:, o + kc * 128:o + (kc + 1) * 128]
                                wuc = WU[:, o + kc * 128:o + (kc + 1) * 128]
                                xh = HI[:, kc * TS:(kc + 1) * TS]
                                xl = LO[:, kc * TS:(kc + 1) * TS]
                                tensor.matmul(pg, wgc, xh, start=(kc == 0), stop=False)
                                tensor.matmul(pg, wgc, xl, start=False, stop=(kc == KC - 1))
                                tensor.matmul(pu, wuc, xh, start=(kc == 0), stop=False)
                                mmi = tensor.matmul(pu, wuc, xl, start=False,
                                                    stop=(kc == KC - 1))
                            cT.inc(mmi)
                    block.tensor(ft)

                    gAF0 = stAF.g

                    def fa(scalar, t0=t0, v0=v0, gAF0=gAF0):
                        for ic in range(IC + 1):
                            if ic < IC:
                                scalar.wait_ge(cT.h, t0 + (ic + 1))
                                if ic >= 2:
                                    scalar.wait_ge(cV.h, v0 + (ic - 1))
                                cA.inc(scalar.activation(
                                    GS[:, (ic % 2) * TS:(ic % 2 + 1) * TS], bank(ic % 2),
                                    AF.Silu, scale=CONST[:, O_SG + ic:O_SG + ic + 1]))
                            if ic >= 1:
                                j = ic - 1
                                scalar.wait_ge(cV.h, v0 + (j + 1))
                                sl = slice((j % 2) * TS, (j % 2 + 1) * TS)
                                stAF.inc(gAF0 + j, scalar.dma_start(out=fuse_d[j, 0], in_=FH[:, sl]))
                                stAF.inc(gAF0 + j, scalar.dma_start(out=fuse_d[j, 1], in_=FL[:, sl]))
                    block.scalar(fa)

                    def fv(vector, a0=a0, gAF0=gAF0):
                        for ic in range(IC):
                            vector.wait_ge(cA.h, a0 + (ic + 1))
                            if ic >= 2:
                                stAF.wait(vector, gAF0 + ic - 2)
                            sl = slice((ic % 2) * TS, (ic % 2 + 1) * TS)
                            vector.scalar_tensor_tensor(GS[:, sl], bank(2 + ic % 2),
                                                        CONST[:, O_SU + ic:O_SU + ic + 1],
                                                        GS[:, sl], ALU.mult, ALU.mult)
                            vector.tensor_copy(FH[:, sl], GS[:, sl])
                            cV.inc(vector.tensor_tensor(FL[:, sl], GS[:, sl], FH[:, sl],
                                                        ALU.subtract))
                    block.vector(fv)
                    stAF.g += IC
                    af_t = [stAF.c[s].n for s in range(2)]

                    def wait_fuse(eng):
                        for s in range(2):
                            eng.wait_ge(stAF.c[s].h, af_t[s])
                    if "dn" not in parts:
                        def fz(scalar):
                            scalar.wait_ge(sA.h, sA.n)
                        block.scalar(fz)
                        barrier()
                        continue

                    t1, v1 = cT.n, cV.n
                    gF0, gWD0 = stF.g, stWD.g

                    def fs_dn(sync, gF0=gF0, t1=t1):
                        wait_fuse(sync)
                        for i in range(2 * IC):
                            ic = i % IC
                            if i >= 4:
                                sync.wait_ge(cT.h, t1 + (i - 3))
                            b = i % 4
                            stF.inc(gF0 + i, sync.dma_start(out=FHR[:, b * TS:(b + 1) * TS],
                                                            in_=fuse_d[ic, 0]))
                            stF.inc(gF0 + i, sync.dma_start(out=FLR[:, b * TS:(b + 1) * TS],
                                                            in_=fuse_d[ic, 1]))
                    block.sync(fs_dn)

                    def fg_dn(gpsimd, gWD0=gWD0, t1=t1):
                        wait_fuse(gpsimd)
                        for i in range(2 * IC):
                            half, ic = i // IC, i % IC
                            if i >= 2:
                                gpsimd.wait_ge(cT.h, t1 + (i - 1))
                            o = (i % 2) * D
                            stWD.inc(gWD0 + i, gpsimd.dma_start(
                                out=WD[:, o:o + D // 2],
                                in_=wd_t[ic][:, half * (D // 2):(half + 1) * (D // 2)]))
                    block.gpsimd(fg_dn)

                    def ft_dn(tensor, gF0=gF0, gWD0=gWD0, v1=v1):
                        for i in range(2 * IC):
                            half, ic = i // IC, i % IC
                            stF.wait(tensor, gF0 + i)
                            stWD.wait(tensor, gWD0 + i)
                            if half == 1 and ic == 0:
                                tensor.wait_ge(cV.h, v1 + D2C // 2)
                            b = i % 4
                            o = (i % 2) * D
                            for j in range(D2C // 2):
                                py = bank(j)
                                wdc = WD[:, o + j * 128:o + (j + 1) * 128]
                                tensor.matmul(py, wdc, FHR[:, b * TS:(b + 1) * TS],
                                              start=(ic == 0), stop=False)
                                mmi = tensor.matmul(py, wdc, FLR[:, b * TS:(b + 1) * TS],
                                                    start=False, stop=(ic == IC - 1))
                            cT.inc(mmi)
                    block.tensor(ft_dn)
                    stF.g += 2 * IC
                    stWD.g += 2 * IC

                    def fv_dn(vector, t1=t1):
                        for half in range(2):
                            vector.wait_ge(cT.h, t1 + (half + 1) * IC)
                            for j in range(D2C // 2):
                                c = half * (D2C // 2) + j
                                cV.inc(vector.scalar_tensor_tensor(
                                    XB[:, c * TS:(c + 1) * TS], bank(j),
                                    CONST[:, O_SD + c:O_SD + c + 1],
                                    XA[:, c * TS:(c + 1) * TS], ALU.mult, ALU.add))
                    block.vector(fv_dn)
                    v_dn0 = cV.n - D2C

                    def fa3(scalar, v_dn0=v_dn0, ts=ts):
                        for c in range(D2C):
                            scalar.wait_ge(cV.h, v_dn0 + c + 1)
                            sA.inc(scalar.dma_start(out=dst[c][:, ts:ts + TS],
                                                    in_=XB[:, c * TS:(c + 1) * TS]))
                    block.scalar(fa3)
                    if final and it == NT - 1:
                        def fa4(scalar):
                            scalar.wait_ge(sA.h, sA.n)
                        block.scalar(fa4)
                    barrier()

        def p_qkv():
            OCG = min(8, KC)
            NGH = KC // OCG
            NT1 = TS // 128
            VB = min(512, D // 2)
            NDB = (D // 2) // VB
            SLT = max(OCG * 128, D // 2)
            with nc.sbuf_tensor("qk_XA", [128, KC * TS], F32) as XA, \
                 nc.sbuf_tensor("qk_XB", [128, KC * TS], F32) as XB, \
                 nc.sbuf_tensor("qk_WQH", [128, 2 * SLT], BF16) as WQH, \
                 nc.sbuf_tensor("qk_WQL", [128, 2 * SLT], BF16) as WQL, \
                 nc.sbuf_tensor("qk_RC", [128, 2 * TS], F32) as RC, \
                 nc.sbuf_tensor("qk_RS", [128, 2 * TS], F32) as RS, \
                 nc.sbuf_tensor("qk_QH", [128, KC * TS], BF16) as QH, \
                 nc.sbuf_tensor("qk_QL", [128, KC * TS], BF16) as QL, \
                 nc.sbuf_tensor("qk_KH", [128, KC * TS], BF16) as KH, \
                 nc.sbuf_tensor("qk_KL", [128, KC * TS], BF16) as KL:
                for it in range(NT):
                    ts = it * TS
                    sA_prev = sA.n
                    v_ln = emit_ln(ts, xmid, XA, XB)

                    qk_banks_free = [0]
                    for p in range(2):
                        t2, v2 = cT.n, cV.n
                        qk_banks_free[0] = v2 + NGH
                        gQ0 = stQW.g

                        def fg_qk(gpsimd, gQ0=gQ0, t2=t2, p=p):
                            gpsimd.wait_ge(cT.h, t2)
                            for gi in range(NGH):
                                for kc in range(KC):
                                    i = gi * KC + kc
                                    if i >= 2:
                                        gpsimd.wait_ge(cT.h, t2 + (i - 1))
                                    o = (i % 2) * SLT
                                    co = (p * KC + gi * OCG) * 128
                                    stQW.inc(gQ0 + i, gpsimd.dma_start(
                                        out=WQH[:, o:o + OCG * 128],
                                        in_=qkv_hi[kc][:, co:co + OCG * 128]))
                                    stQW.inc(gQ0 + i, gpsimd.dma_start(
                                        out=WQL[:, o:o + OCG * 128],
                                        in_=qkv_lo[kc][:, co:co + OCG * 128]))
                        block.gpsimd(fg_qk)

                        def ft_qk(tensor, gQ0=gQ0, v2=v2, v_ln=v_ln, p=p):
                            tensor.wait_ge(cV.h, v2)
                            for gi in range(NGH):
                                if gi >= 1:
                                    tensor.wait_ge(cV.h, v2 + gi)
                                for kc in range(KC):
                                    i = gi * KC + kc
                                    stQW.wait(tensor, gQ0 + i)
                                    if p == 0 and gi == 0:
                                        tensor.wait_ge(cV.h, v_ln + kc + 1)
                                    o = (i % 2) * SLT
                                    xh = HI[:, kc * TS:(kc + 1) * TS]
                                    xl = LO[:, kc * TS:(kc + 1) * TS]
                                    for j in range(OCG):
                                        pj = bank(j)
                                        wh = WQH[:, o + j * 128:o + (j + 1) * 128]
                                        wl = WQL[:, o + j * 128:o + (j + 1) * 128]
                                        tensor.matmul(pj, wh, xh, start=(kc == 0), stop=False)
                                        tensor.matmul(pj, wh, xl, start=False, stop=False)
                                        mmi = tensor.matmul(pj, wl, xh, start=False,
                                                            stop=(kc == KC - 1))
                                    cT.inc(mmi)
                        block.tensor(ft_qk)
                        stQW.g += NGH * KC
                        t_qk0 = cT.n - NGH * KC

                        def fv_ev(vector, t_qk0=t_qk0):
                            for gi in range(NGH):
                                vector.wait_ge(cT.h, t_qk0 + (gi + 1) * KC)
                                for j in range(OCG):
                                    oc = gi * OCG + j
                                    vi = vector.tensor_copy(XB[:, oc * TS:(oc + 1) * TS],
                                                            bank(j))
                                cV.inc(vi)
                        block.vector(fv_ev)

                        gR0 = stR.g
                        v_rope0 = cV.n

                        def fv_rope(vector, gR0=gR0, p=p, sA_prev=sA_prev, it=it):
                            if p == 0 and it > 0:
                                vector.wait_ge(sA.h, sA_prev)
                            for c in range(DQC):
                                b = c % 2
                                stR.wait(vector, gR0 + c)
                                s0 = slice(c * TS, (c + 1) * TS)
                                s1_ = slice((DQC + c) * TS, (DQC + c + 1) * TS)
                                rc = RC[:, b * TS:(b + 1) * TS]
                                rs_ = RS[:, b * TS:(b + 1) * TS]
                                vector.tensor_mul(XA[:, 0:TS], XB[:, s1_], rs_)
                                vector.tensor_mul(XA[:, TS:2 * TS], XB[:, s0], rs_)
                                vector.tensor_mul(XB[:, s0], XB[:, s0], rc)
                                vector.tensor_sub(XB[:, s0], XB[:, s0], XA[:, 0:TS])
                                vector.tensor_mul(XB[:, s1_], XB[:, s1_], rc)
                                cV.inc(vector.tensor_add(XB[:, s1_], XB[:, s1_],
                                                         XA[:, TS:2 * TS]))
                            DH_, DL_ = (QH, QL) if p == 0 else (KH, KL)
                            vector.tensor_copy(DH_[:, 0:KC * TS], XB[:, 0:KC * TS])
                            cV.inc(vector.tensor_tensor(DL_[:, 0:KC * TS], XB[:, 0:KC * TS],
                                                        DH_[:, 0:KC * TS], ALU.subtract))
                        block.vector(fv_rope)
                        v_cast = cV.n

                        def fs_rt(sync, ts=ts, v_rope0=v_rope0, gR0=gR0):
                            sync.wait_ge(cV.h, v_rope0)
                            for c in range(DQC):
                                b = c % 2
                                if c >= 2:
                                    sync.wait_ge(cV.h, v_rope0 + (c - 1))
                                stR.inc(gR0 + c, sync.dma_start(out=RC[:, b * TS:(b + 1) * TS],
                                                                in_=cos_t[c][:, ts:ts + TS]))
                                stR.inc(gR0 + c, sync.dma_start(out=RS[:, b * TS:(b + 1) * TS],
                                                                in_=sin_t[c][:, ts:ts + TS]))
                        block.sync(fs_rt)
                        stR.g += DQC

                        def fa_pk(scalar, v_cast=v_cast, ts=ts, p=p, it=it):
                            scalar.wait_ge(cV.h, v_cast)
                            for c in range(KC):
                                if p == 0:
                                    sA.inc(scalar.dma_start(
                                        out=qpark[0, c][:, ts:ts + TS],
                                        in_=QH[:, c * TS:(c + 1) * TS]))
                                    sA.inc(scalar.dma_start(
                                        out=qpark[1, c][:, ts:ts + TS],
                                        in_=QL[:, c * TS:(c + 1) * TS]))
                                else:
                                    sA.inc(scalar.dma_start(
                                        out=k_in[it].ap()[0, c],
                                        in_=KH[:, c * TS:(c + 1) * TS]))
                                    sA.inc(scalar.dma_start(
                                        out=k_in[it].ap()[1, c],
                                        in_=KL[:, c * TS:(c + 1) * TS]))
                        block.scalar(fa_pk)
                        if p == 1:
                            a_kstage = sA.n

                            def fg_agk(gpsimd, a_kstage=a_kstage, it=it):
                                gpsimd.wait_ge(sA.h, a_kstage)
                                cC.inc(gpsimd.collective_compute(
                                    "AllGather", ALU.bypass,
                                    replica_groups=[[2 * i, 2 * i + 1]
                                                    for i in range(NCORES // 2)],
                                    ins=[k_in[it].ap().opt()],
                                    outs=[k_all[it].ap().opt()]))
                            block.gpsimd(fg_agk)
                    a_park = sA.n

                    for dvh in range(2):
                        t3 = cT.n
                        gV0 = stQW.g

                        def fg_v(gpsimd, gV0=gV0, t3=t3, dvh=dvh):
                            gpsimd.wait_ge(cT.h, t3)
                            for kc in range(KC):
                                if kc >= 2:
                                    gpsimd.wait_ge(cT.h, t3 + (kc - 1))
                                o = (kc % 2) * SLT
                                vo = 2 * D + dvh * (D // 2)
                                stQW.inc(gV0 + kc, gpsimd.dma_start(
                                    out=WQH[:, o:o + D // 2],
                                    in_=qkv_hi[kc][:, vo:vo + D // 2]))
                                stQW.inc(gV0 + kc, gpsimd.dma_start(
                                    out=WQL[:, o:o + D // 2],
                                    in_=qkv_lo[kc][:, vo:vo + D // 2]))
                        block.gpsimd(fg_v)

                        def ft_v(tensor, gV0=gV0, dvh=dvh,
                                 v=(qk_banks_free[0] if dvh == 0 else cV.n)):
                            tensor.wait_ge(cV.h, v)
                            for kc in range(KC):
                                stQW.wait(tensor, gV0 + kc)
                                o = (kc % 2) * SLT
                                for t1 in range(NT1):
                                    xh = HI[:, kc * TS + t1 * 128:kc * TS + (t1 + 1) * 128]
                                    xl = LO[:, kc * TS + t1 * 128:kc * TS + (t1 + 1) * 128]
                                    for db_ in range(NDB):
                                        pj = bank(t1 * NDB + db_, VB)
                                        wh = WQH[:, o + db_ * VB:o + (db_ + 1) * VB]
                                        wl = WQL[:, o + db_ * VB:o + (db_ + 1) * VB]
                                        tensor.matmul(pj, xh, wh, start=(kc == 0), stop=False)
                                        tensor.matmul(pj, xh, wl, start=False, stop=False)
                                        mmi = tensor.matmul(pj, xl, wh, start=False,
                                                            stop=(kc == KC - 1))
                                cT.inc(mmi)
                        block.tensor(ft_v)
                        stQW.g += KC
                        t_v_end = cT.n

                        def fv_v(vector, t_v_end=t_v_end, a_park=a_park, dvh=dvh):
                            vector.wait_ge(cT.h, t_v_end)
                            if dvh == 0:
                                vector.wait_ge(sA.h, a_park)
                            n = NT1 * NDB
                            for s_ in range(n):
                                sl = slice(s_ * VB, (s_ + 1) * VB)
                                vector.tensor_copy(XB[:, sl], bank(s_, VB))
                            for s_ in range(n):
                                sl = slice(s_ * VB, (s_ + 1) * VB)
                                vector.tensor_copy(QH[:, sl], XB[:, sl])
                                vi = vector.tensor_tensor(QL[:, sl], XB[:, sl], QH[:, sl],
                                                          ALU.subtract)
                            cV.inc(vi)
                        block.vector(fv_v)
                        v_vc = cV.n

                        def fa_v(scalar, v_vc=v_vc, dvh=dvh, ts=ts, it=it):
                            scalar.wait_ge(cV.h, v_vc)
                            for t1 in range(NT1):
                                for db_ in range(NDB):
                                    s_ = t1 * NDB + db_
                                    sl = slice(s_ * VB, (s_ + 1) * VB)
                                    vo = dvh * (D // 2) + db_ * VB
                                    sA.inc(scalar.dma_start(
                                        out=v_in[it].ap()[0, t1][:, vo:vo + VB],
                                        in_=QH[:, sl]))
                                    sA.inc(scalar.dma_start(
                                        out=v_in[it].ap()[1, t1][:, vo:vo + VB],
                                        in_=QL[:, sl]))
                        block.scalar(fa_v)
                        if dvh == 1:
                            a_vstage = sA.n

                            def fg_agv(gpsimd, a_vstage=a_vstage, it=it):
                                gpsimd.wait_ge(sA.h, a_vstage)
                                cC.inc(gpsimd.collective_compute(
                                    "AllGather", ALU.bypass,
                                    replica_groups=[[2 * i, 2 * i + 1]
                                                    for i in range(NCORES // 2)],
                                    ins=[v_in[it].ap().opt()],
                                    outs=[v_all[it].ap().opt()]))
                            block.gpsimd(fg_agv)
                    barrier()

                gNB0 = stNB.g
                a_vdone = sA.n
                v_nb0 = cV.n
                a_nb0 = cA.n
                t_nb0 = cT.n
                steps = []
                for it2 in range(NT):
                    steps.append(("q", it2))
                    steps.append(("k", it2))

                def fs_nb(sync, gNB0=gNB0, a_vdone=a_vdone, a_nb0=a_nb0):
                    sync.wait_ge(sA.h, a_vdone)
                    for s_, (qk, it2) in enumerate(steps):
                        if s_ >= 2:
                            sync.wait_ge(cA.h, a_nb0 + s_ - 1)
                        buf = QH if s_ % 2 == 0 else QL
                        for c in range(KC):
                            srcap = (qpark[0, c][:, it2 * TS:(it2 + 1) * TS] if qk == "q"
                                     else k_in[it2].ap()[0, c])
                            stNB.inc(gNB0 + s_, sync.dma_start(
                                out=buf[:, c * TS:(c + 1) * TS], in_=srcap))
                block.sync(fs_nb)
                stNB.g += len(steps)

                def fa_nb(scalar, gNB0=gNB0, t_nb0=t_nb0):
                    for s_, (qk, it2) in enumerate(steps):
                        stNB.wait(scalar, gNB0 + s_)
                        if s_ >= 1:
                            scalar.wait_ge(cT.h, t_nb0 + s_)
                        buf = QH if s_ % 2 == 0 else QL
                        cA.inc(scalar.square(KH[:, 0:KC * TS], buf[:, 0:KC * TS]))
                block.scalar(fa_nb)

                def ft_nb(tensor, a_nb0=a_nb0, v_nb0=v_nb0):
                    tensor.wait_ge(cV.h, v_nb0)
                    for s_, (qk, it2) in enumerate(steps):
                        tensor.wait_ge(cA.h, a_nb0 + s_ + 1)
                        bk = 4 if qk == "q" else 5
                        if s_ >= 2:
                            tensor.wait_ge(cV.h, v_nb0 + s_ - 1)
                        for c in range(KC):
                            mmi = tensor.matmul(bankrow(bk), ONESB[:, :],
                                                KH[:, c * TS:(c + 1) * TS],
                                                start=(c == 0), stop=(c == KC - 1))
                        cT.inc(mmi)
                block.tensor(ft_nb)

                def fv_nb(vector, t_nb0=t_nb0, v_nb0=v_nb0):
                    for s_, (qk, it2) in enumerate(steps):
                        vector.wait_ge(cT.h, t_nb0 + s_ + 1)
                        bk = 4 if qk == "q" else 5
                        m = 0 if qk == "q" else 1
                        vector.reduce_max(MROW[:, 4:5], bankrow(bk), axis=AXX)
                        cV.inc(vector.tensor_max(MROW[:, m:m + 1], MROW[:, m:m + 1],
                                                 MROW[:, 4:5]))
                block.vector(fv_nb)

                def fv_m(vector):
                    cV.inc(vector.tensor_mul(MROW[:, 2:3], MROW[:, 0:1], MROW[:, 1:2]))
                block.vector(fv_m)

                def fa_m(scalar, v=cV.n):
                    scalar.wait_ge(cV.h, v)
                    scalar.activation(MROW[:, 3:4], MROW[:, 2:3], AF.Sqrt)
                    cA.inc(scalar.mul(MROW[:, 3:4], MROW[:, 3:4], -float(cfg.SC)))
                block.scalar(fa_m)

                def ft_m(tensor, a=cA.n):
                    tensor.wait_ge(cA.h, a)
                    mmi = tensor.matmul(PS[:, 0:1], ONESR[:, :], MROW[:, 3:4],
                                        start=True, stop=True)
                    cT.inc(mmi)
                block.tensor(ft_m)

                def fv_m2(vector, t=cT.n):
                    vector.wait_ge(cT.h, t)
                    cV.inc(vector.tensor_copy(CONST[:, O_NM:O_NM + 1], PS[:, 0:1]))
                block.vector(fv_m2)
                barrier()

        def p_collective():
            if debug:
                def fg2(gpsimd):
                    gpsimd.wait_ge(cC.h, cC.n)
                    for i in range(NT):
                        gD.inc(gpsimd.dma_start(out=dbg["dbg_k"][i], in_=k_in[i].ap()))
                        gD.inc(gpsimd.dma_start(out=dbg["dbg_v"][i], in_=v_in[i].ap()))
                    gD.inc(gpsimd.dma_start(out=dbg["dbg_q"], in_=qpark))
                    gpsimd.wait_ge(gD.h, gD.n)
                block.gpsimd(fg2)
                barrier()

        def p_attn():
            NKSL = S // 512
            NB = 512 // 128
            with nc.sbuf_tensor("at_EH", [128, T2C * QB], BF16) as EH, \
                 nc.sbuf_tensor("at_EL", [128, T2C * QB], BF16) as EL, \
                 nc.sbuf_tensor("at_QSH", [128, DQC * QB], BF16) as QSH, \
                 nc.sbuf_tensor("at_QSL", [128, DQC * QB], BF16) as QSL, \
                 nc.sbuf_tensor("at_KSH", [128, 2 * DQC * 512], BF16) as KSH, \
                 nc.sbuf_tensor("at_KSL", [128, 2 * DQC * 512], BF16) as KSL, \
                 nc.sbuf_tensor("at_VSH", [128, 2 * DH], BF16) as VSH, \
                 nc.sbuf_tensor("at_VSL", [128, 2 * DH], BF16) as VSL, \
                 nc.sbuf_tensor("at_WOH", [128, 2 * (D // 2)], BF16) as WOH, \
                 nc.sbuf_tensor("at_WOL", [128, 2 * (D // 2)], BF16) as WOL, \
                 nc.sbuf_tensor("at_E32", [128, 2 * 512], F32) as E32, \
                 nc.sbuf_tensor("at_RB", [128, QB], F32) as RB, \
                 nc.sbuf_tensor("at_OB", [128, DQC * QB], F32) as OB, \
                 nc.sbuf_tensor("at_XS", [128, 2 * 512], F32) as XS, \
                 nc.sbuf_tensor("at_YS", [128, 2 * 512], F32) as YS:
                prev_t_av = [0]
                for qb in range(NQB):
                    tq = qb * QB
                    for h in range(H):
                        v_prev = cV.n
                        gQb = stQ.g
                        gKb = stK.g

                        def fs_q(sync, tq=tq, h=h, v_prev=v_prev, qb=qb, gQb=gQb):
                            if not (qb == 0 and h == 0):
                                sync.wait_ge(cV.h, v_prev)
                            for ver, dst_ in ((0, QSH), (1, QSL)):
                                for c in range(DQC):
                                    stQ.inc(gQb, sync.dma_start(
                                        out=dst_[:, c * QB:(c + 1) * QB],
                                        in_=qpark[ver, h * DQC + c][:, tq:tq + QB]))
                        block.sync(fs_q)
                        stQ.g += 1

                        t4, v4, a4 = cT.n, cV.n, cA.n

                        def fs_k(sync, gKb=gKb, t4=t4, h=h, qb=qb):
                            for sl_ in range(NKSL):
                                r, lo_ = sl_ // (NKSL // 2), (sl_ % (NKSL // 2)) * 512
                                kt, off = lo_ // TS, lo_ % TS
                                if qb == 0 and h == 0:
                                    sync.wait_ge(cC.h, 2 * kt + 1)
                                if sl_ >= 2:
                                    sync.wait_ge(cT.h, t4 + (sl_ - 1) * NB)
                                ko = (sl_ % 2) * DQC * 512
                                for ver, dst_ in ((0, KSH), (1, KSL)):
                                    for c in range(DQC):
                                        stK.inc(gKb + sl_, sync.dma_start(
                                            out=dst_[:, ko + c * 512:ko + (c + 1) * 512],
                                            in_=k_all[kt].ap()[r, ver, h * DQC + c]
                                                [:, off:off + 512]))
                        block.sync(fs_k)
                        stK.g += NKSL

                        def ft_s(tensor, gQb=gQb, gKb=gKb, a4=a4, v4=v4, v_prev=v_prev,
                                 qb=qb, h=h):
                            if not (qb == 0 and h == 0):
                                tensor.wait_ge(cV.h, v_prev)
                            stQ.wait(tensor, gQb)
                            for sl_ in range(NKSL):
                                stK.wait(tensor, gKb + sl_)
                                ko = (sl_ % 2) * DQC * 512
                                for b in range(NB):
                                    t2c = sl_ * NB + b
                                    pj = bank(t2c % 4, QB)
                                    if t2c >= 4:
                                        tensor.wait_ge(cA.h, a4 + (t2c - 3))
                                    for c in range(DQC):
                                        kh = KSH[:, ko + c * 512 + b * 128:ko + c * 512 + (b + 1) * 128]
                                        kl = KSL[:, ko + c * 512 + b * 128:ko + c * 512 + (b + 1) * 128]
                                        qh = QSH[:, c * QB:(c + 1) * QB]
                                        ql = QSL[:, c * QB:(c + 1) * QB]
                                        tensor.matmul(pj, kh, qh, start=(c == 0), stop=False)
                                        tensor.matmul(pj, kh, ql, start=False, stop=False)
                                        mmi = tensor.matmul(pj, kl, qh, start=False,
                                                            stop=(c == DQC - 1))
                                    cT.inc(mmi)
                            for t2c in range(T2C):
                                tensor.wait_ge(cV.h, v4 + t2c + 1)
                                tensor.matmul(bankrow(4, QB), ONESB[:, :],
                                              EH[:, t2c * QB:(t2c + 1) * QB],
                                              start=(t2c == 0), stop=False)
                                mmi = tensor.matmul(bankrow(4, QB), ONESB[:, :],
                                                    EL[:, t2c * QB:(t2c + 1) * QB],
                                                    start=False, stop=(t2c == T2C - 1))
                            cT.inc(mmi)
                        block.tensor(ft_s)
                        t_sums = cT.n

                        def fa_e(scalar, t4=t4, v4=v4):
                            for t2c in range(T2C):
                                scalar.wait_ge(cT.h, t4 + t2c + 1)
                                if t2c >= 2:
                                    scalar.wait_ge(cV.h, v4 + t2c - 1)
                                cA.inc(scalar.activation(
                                    E32[:, (t2c % 2) * 512:(t2c % 2) * 512 + QB],
                                    bank(t2c % 4, QB), AF.Exp,
                                    bias=CONST[:, O_NM:O_NM + 1], scale=float(cfg.SC)))
                        block.scalar(fa_e)

                        def fv_e(vector, a4=a4):
                            for t2c in range(T2C):
                                vector.wait_ge(cA.h, a4 + t2c + 1)
                                sl_ = slice(t2c * QB, (t2c + 1) * QB)
                                e3 = E32[:, (t2c % 2) * 512:(t2c % 2) * 512 + QB]
                                vector.tensor_copy(EH[:, sl_], e3)
                                cV.inc(vector.tensor_tensor(EL[:, sl_], e3, EH[:, sl_],
                                                            ALU.subtract))
                        block.vector(fv_e)

                        def fv_r(vector, t=t_sums):
                            vector.wait_ge(cT.h, t)
                            cV.inc(vector.reciprocal(SMALL[:, 0:QB], bankrow(4, QB)))
                        block.vector(fv_r)

                        def ft_rb(tensor, v=cV.n):
                            tensor.wait_ge(cV.h, v)
                            mmi = tensor.matmul(bank(5, QB), ONESR[:, :], SMALL[:, 0:QB],
                                                start=True, stop=True)
                            cT.inc(mmi)
                        block.tensor(ft_rb)

                        def fv_rb(vector, t=cT.n):
                            vector.wait_ge(cT.h, t)
                            cV.inc(vector.tensor_copy(RB[:, 0:QB], bank(5, QB)))
                        block.vector(fv_rb)
                        v_rb = cV.n

                        gVb = stV.g
                        tav_base = cT.n

                        def fg_v(gpsimd, gVb=gVb, t5=prev_t_av[0], tav_base=tav_base,
                                 h=h, qb=qb):
                            gpsimd.wait_ge(cT.h, t5)
                            for t2c in range(T2C):
                                r, lc = t2c // (T2C // 2), t2c % (T2C // 2)
                                vt, t1l = lc // NT1_, lc % NT1_
                                if qb == 0 and h == 0:
                                    gpsimd.wait_ge(cC.h, 2 * vt + 2)
                                if t2c >= 2:
                                    gpsimd.wait_ge(cT.h, tav_base + t2c - 1)
                                o = (t2c % 2) * DH
                                stV.inc(gVb + t2c, gpsimd.dma_start(
                                    out=VSH[:, o:o + DH],
                                    in_=v_all[vt].ap()[r, 0, t1l][:, h * DH:(h + 1) * DH]))
                                stV.inc(gVb + t2c, gpsimd.dma_start(
                                    out=VSL[:, o:o + DH],
                                    in_=v_all[vt].ap()[r, 1, t1l][:, h * DH:(h + 1) * DH]))
                        block.gpsimd(fg_v)

                        def ft_av(tensor, gVb=gVb, v_rb=v_rb):
                            tensor.wait_ge(cV.h, v_rb)
                            for t2c in range(T2C):
                                stV.wait(tensor, gVb + t2c)
                                o = (t2c % 2) * DH
                                eh = EH[:, t2c * QB:(t2c + 1) * QB]
                                el = EL[:, t2c * QB:(t2c + 1) * QB]
                                for c in range(DQC):
                                    pj = bank(c, QB)
                                    vh = VSH[:, o + c * 128:o + (c + 1) * 128]
                                    vl = VSL[:, o + c * 128:o + (c + 1) * 128]
                                    tensor.matmul(pj, vh, eh, start=(t2c == 0), stop=False)
                                    tensor.matmul(pj, vh, el, start=False, stop=False)
                                    mmi = tensor.matmul(pj, vl, eh, start=False,
                                                        stop=(t2c == T2C - 1))
                                cT.inc(mmi)
                        block.tensor(ft_av)
                        stV.g += T2C
                        t_av = cT.n
                        prev_t_av[0] = t_av

                        def fv_o(vector, t_av=t_av, v_rb=v_rb, h=h):
                            vector.wait_ge(cT.h, t_av)
                            vector.wait_ge(cV.h, v_rb)
                            for c in range(DQC):
                                sl_ = slice(c * QB, (c + 1) * QB)
                                vector.tensor_mul(OB[:, sl_], bank(c, QB), RB[:, 0:QB])
                                g = h * DQC + c
                                vector.tensor_copy(HI[:, g * QB:(g + 1) * QB], OB[:, sl_])
                                vi = vector.tensor_tensor(LO[:, g * QB:(g + 1) * QB], OB[:, sl_],
                                                          HI[:, g * QB:(g + 1) * QB],
                                                          ALU.subtract)
                            cV.inc(vi)
                        block.vector(fv_o)

                    v_oh = cV.n
                    for d2h in range(2):
                        t6 = cT.n
                        gWb = stQW.g

                        def fg_wo(gpsimd, gWb=gWb, t6=t6, d2h=d2h):
                            gpsimd.wait_ge(cT.h, t6)
                            for dvc in range(KC):
                                if dvc >= 2:
                                    gpsimd.wait_ge(cT.h, t6 + dvc - 1)
                                o = (dvc % 2) * (D // 2)
                                stQW.inc(gWb + dvc, gpsimd.dma_start(
                                    out=WOH[:, o:o + D // 2],
                                    in_=wo_hi[dvc][:, d2h * (D // 2):(d2h + 1) * (D // 2)]))
                                stQW.inc(gWb + dvc, gpsimd.dma_start(
                                    out=WOL[:, o:o + D // 2],
                                    in_=wo_lo[dvc][:, d2h * (D // 2):(d2h + 1) * (D // 2)]))
                        block.gpsimd(fg_wo)

                        def ft_wo(tensor, gWb=gWb, v_oh=v_oh, v=cV.n):
                            tensor.wait_ge(cV.h, max(v_oh, v))
                            for dvc in range(KC):
                                stQW.wait(tensor, gWb + dvc)
                                o = (dvc % 2) * (D // 2)
                                oh = HI[:, dvc * QB:(dvc + 1) * QB]
                                ol = LO[:, dvc * QB:(dvc + 1) * QB]
                                for j in range(D2C // 2):
                                    pj = bank(j, QB)
                                    wh = WOH[:, o + j * 128:o + (j + 1) * 128]
                                    wl = WOL[:, o + j * 128:o + (j + 1) * 128]
                                    tensor.matmul(pj, wh, oh, start=(dvc == 0), stop=False)
                                    tensor.matmul(pj, wh, ol, start=False, stop=False)
                                    mmi = tensor.matmul(pj, wl, oh, start=False,
                                                        stop=(dvc == KC - 1))
                                cT.inc(mmi)
                        block.tensor(ft_wo)
                        stQW.g += KC
                        t_wo = cT.n

                        v_ev0 = cV.n
                        gXb = stX.g

                        def fs_x(sync, tq=tq, d2h=d2h, v_ev0=v_ev0, gXb=gXb):
                            sync.wait_ge(cV.h, v_ev0)
                            for j in range(D2C // 2):
                                c = d2h * (D2C // 2) + j
                                if j >= 2:
                                    sync.wait_ge(cV.h, v_ev0 + j - 1)
                                stX.inc(gXb + j, sync.dma_start(
                                    out=XS[:, (j % 2) * 512:(j % 2) * 512 + QB],
                                    in_=xmid[c][:, tq:tq + QB]))
                        block.sync(fs_x)
                        stX.g += D2C // 2

                        def fv_y(vector, t_wo=t_wo, gXb=gXb, d2h=d2h, s_ys=sA.n):
                            vector.wait_ge(cT.h, t_wo)
                            if d2h > 0 or True:
                                vector.wait_ge(sA.h, s_ys)
                            for j in range(D2C // 2):
                                c = d2h * (D2C // 2) + j
                                stX.wait(vector, gXb + j)
                                cV.inc(vector.scalar_tensor_tensor(
                                    YS[:, (j % 2) * 512:(j % 2) * 512 + QB], bank(j, QB),
                                    CONST[:, O_BO + c:O_BO + c + 1],
                                    XS[:, (j % 2) * 512:(j % 2) * 512 + QB],
                                    ALU.add, ALU.add))
                        block.vector(fv_y)
                        v_y0 = cV.n - D2C // 2

                        def fa_st(scalar, v_y0=v_y0, tq=tq, d2h=d2h):
                            for j in range(D2C // 2):
                                c = d2h * (D2C // 2) + j
                                scalar.wait_ge(cV.h, v_y0 + j + 1)
                                sA.inc(scalar.dma_start(
                                    out=xmid2[c][:, tq:tq + QB],
                                    in_=YS[:, (j % 2) * 512:(j % 2) * 512 + QB]))
                        block.scalar(fa_st)
                    barrier()
                if debug:
                    def fg3(gpsimd):
                        gD.inc(gpsimd.dma_start(out=dbg["dbg_xmid2"], in_=xmid2))
                        gpsimd.wait_ge(gD.h, gD.n)
                    block.gpsimd(fg3)
                    barrier()

        PH = phases if phases is not None else ["consts", "wconv", "ffn1", "qkv",
                                                "cc", "attn", "ffn2"]
        if "consts" in PH:
            p_consts()
        if "wconv" in PH:
            p_wconv()
        if "ffn1" in PH:
            p_ffn(x_t, xmid, "f1")
        if "ffn1ln" in PH:
            p_ffn(x_t, xmid, "f1", parts=("ln",))
        if "ffn1gu" in PH:
            p_ffn(x_t, xmid, "f1", parts=("ln", "gu"))
        if debug:
            def fgd(gpsimd):
                gD.inc(gpsimd.dma_start(out=dbg["dbg_xmid"], in_=xmid))
                gpsimd.wait_ge(gD.h, gD.n)
            block.gpsimd(fgd)
            barrier()
        if "qkv" in PH:
            p_qkv()
        if "cc" in PH:
            p_collective()
        if "attn" in PH:
            p_attn()
        if "ffn2" in PH:
            p_ffn(xmid2, out_t, "f2", final=True)

    return nc



def _rope_tables(seq_len, head_dim):
    k = np.arange(0, head_dim, 2, dtype=np.float32) / np.float32(head_dim)
    inv_freq = (1.0 / (10000.0 ** k)).astype(np.float32)
    t = np.arange(seq_len, dtype=np.float32)
    freqs = (t[:, None] * inv_freq[None, :]).astype(np.float32)
    emb = np.concatenate([freqs, freqs], axis=-1)
    cos = np.cos(emb.astype(np.float64)).astype(np.float32)
    sin = np.sin(emb.astype(np.float64)).astype(np.float32)
    return cos, sin


def make_inputs(cfg, b, half, x, wqkv, wout, bout, w_gate, w_up, w_down,
                s_gate, s_up, s_down):
    D, I, T, KC, IC, DQC = cfg.D, cfg.I, cfg.T, cfg.KC, cfg.IC, cfg.DQC
    cos, sin = _rope_tables(cfg.S, cfg.DH)
    sl = slice(half * T, (half + 1) * T)
    return {
        "x_t": np.ascontiguousarray(
            np.asarray(x[b, sl, :], dtype=np.float32).T.reshape(KC, 128, T)),
        "wqkv_t": np.ascontiguousarray(np.asarray(wqkv, np.float32).reshape(KC, 128, 3 * D)),
        "wout_t": np.ascontiguousarray(np.asarray(wout, np.float32).reshape(KC, 128, D)),
        "wg_t": np.ascontiguousarray(
            np.asarray(w_gate).reshape(KC, 128, IC, 128).transpose(2, 1, 0, 3)
            .reshape(IC, 128, D)),
        "wu_t": np.ascontiguousarray(
            np.asarray(w_up).reshape(KC, 128, IC, 128).transpose(2, 1, 0, 3)
            .reshape(IC, 128, D)),
        "wd_t": np.ascontiguousarray(np.asarray(w_down).reshape(IC, 128, D)),
        "sg_pp": np.ascontiguousarray(np.asarray(s_gate, np.float32).reshape(IC, 128).T),
        "su_pp": np.ascontiguousarray(np.asarray(s_up, np.float32).reshape(IC, 128).T),
        "sd_pp": np.ascontiguousarray(np.asarray(s_down, np.float32).reshape(KC, 128).T),
        "bout_pp": np.ascontiguousarray(np.asarray(bout, np.float32).reshape(KC, 128).T),
        "cos_t": np.ascontiguousarray(cos[sl, :].T.reshape(DQC, 128, T)),
        "sin_t": np.ascontiguousarray(sin[sl, :].T.reshape(DQC, 128, T)),
    }


def kernel(L0_attn_x, L0_attn_wqkv, L0_attn_wout, L0_attn_bout, L0_ffn_w_gate,
           L0_ffn_w_up, L0_ffn_w_down, L0_ffn_s_gate, L0_ffn_s_up, L0_ffn_s_down):
    cfg = CFG()
    x = np.asarray(L0_attn_x, dtype=np.float32)
    B = x.shape[0]
    nc = build_program(cfg)
    in_maps = []
    for core in range(NCORES):
        b, half = core // 2, core % 2
        in_maps.append(make_inputs(cfg, b, half, x, L0_attn_wqkv, L0_attn_wout,
                                   L0_attn_bout, L0_ffn_w_gate, L0_ffn_w_up,
                                   L0_ffn_w_down, L0_ffn_s_gate, L0_ffn_s_up,
                                   L0_ffn_s_down))
    res = run_bass_kernel_spmd(nc, in_maps, core_ids=list(range(NCORES)))
    out = np.empty((B, cfg.S, cfg.D), dtype=np.float32)
    for core in range(NCORES):
        b, half = core // 2, core % 2
        o = res.results[core]["out_t"].reshape(cfg.D, cfg.T)
        out[b, half * cfg.T:(half + 1) * cfg.T, :] = o.T
    return out


# version 13
# speedup vs baseline: 1.1463x; 1.0726x over previous
import contextlib
import math

import numpy as np
import ml_dtypes

import concourse.bass as bass
import concourse.mybir as mybir
from concourse.bass_utils import run_bass_kernel_spmd

F32 = mybir.dt.float32
BF16 = mybir.dt.bfloat16
I8 = mybir.dt.int8
AF = mybir.ActivationFunctionType
ALU = mybir.AluOpType
AXX = mybir.AxisListType.X

NCORES = 8


class CFG:
    def __init__(self, D=2048, I=8192, T=2048, H=2, TS=512, QB=512, eps=1e-5):
        self.D, self.I, self.T, self.H = D, I, T, H
        self.S = 2 * T
        self.DH = D // H
        self.TS, self.QB = TS, QB
        self.KC = D // 128
        self.IC = I // 128
        self.NT = T // TS
        self.NQB = T // QB
        self.T2C = self.S // 128
        self.DQC = self.DH // 128
        self.SC = 1.0 / math.sqrt(self.DH)
        self.eps = eps


def build_program(cfg: CFG, debug=False, phases=None):
    nc = bass.Bass("TRN2", target_bir_lowering=False, detect_race_conditions=False)
    D, I, T, H = cfg.D, cfg.I, cfg.T, cfg.H
    KC, IC, TS, NT = cfg.KC, cfg.IC, cfg.TS, cfg.NT
    QB, NQB, T2C, DQC = cfg.QB, cfg.NQB, cfg.T2C, cfg.DQC
    DH, S = cfg.DH, cfg.S
    D2C = KC

    x_t = nc.dram_tensor("x_t", [KC, 128, T], F32, kind="ExternalInput").ap()
    wqkv_t = nc.dram_tensor("wqkv_t", [KC, 128, 3 * D], F32, kind="ExternalInput").ap()
    wout_t = nc.dram_tensor("wout_t", [KC, 128, D], F32, kind="ExternalInput").ap()
    wg_t = nc.dram_tensor("wg_t", [IC, 128, D], I8, kind="ExternalInput").ap()
    wu_t = nc.dram_tensor("wu_t", [IC, 128, D], I8, kind="ExternalInput").ap()
    wd_t = nc.dram_tensor("wd_t", [IC, 128, D], I8, kind="ExternalInput").ap()
    sg_pp = nc.dram_tensor("sg_pp", [128, IC], F32, kind="ExternalInput").ap()
    su_pp = nc.dram_tensor("su_pp", [128, IC], F32, kind="ExternalInput").ap()
    sd_pp = nc.dram_tensor("sd_pp", [128, KC], F32, kind="ExternalInput").ap()
    bout_pp = nc.dram_tensor("bout_pp", [128, KC], F32, kind="ExternalInput").ap()
    cos_t = nc.dram_tensor("cos_t", [DQC, 128, T], F32, kind="ExternalInput").ap()
    sin_t = nc.dram_tensor("sin_t", [DQC, 128, T], F32, kind="ExternalInput").ap()

    out_t = nc.dram_tensor("out_t", [KC, 128, T], F32, kind="ExternalOutput").ap()

    xmid = nc.dram_tensor("xmid", [KC, 128, T], F32, kind="Internal").ap()
    xmid2 = nc.dram_tensor("xmid2", [KC, 128, T], F32, kind="Internal").ap()
    qkv_hi = nc.dram_tensor("qkv_hi", [KC, 128, 3 * D], BF16, kind="Internal").ap()
    qkv_lo = nc.dram_tensor("qkv_lo", [KC, 128, 3 * D], BF16, kind="Internal").ap()
    wo_hi = nc.dram_tensor("wo_hi", [KC, 128, D], BF16, kind="Internal").ap()
    wo_lo = nc.dram_tensor("wo_lo", [KC, 128, D], BF16, kind="Internal").ap()
    fuse_d = nc.dram_tensor("fuse_d", [IC, 2, 128, TS], BF16, kind="Internal").ap()
    qpark = nc.dram_tensor("qpark", [2, KC, 128, T], BF16, kind="Internal").ap()
    NT_ = T // TS
    NT1_ = TS // 128
    k_in = [nc.dram_tensor(f"k_in{i}", [2, KC, 128, TS], BF16, kind="Internal")
            for i in range(NT_)]
    k_all = [nc.dram_tensor(f"k_all{i}", [2, 2, KC, 128, TS], BF16, kind="Internal")
             for i in range(NT_)]
    v_in = [nc.dram_tensor(f"v_in{i}", [2, NT1_, 128, D], BF16, kind="Internal")
            for i in range(NT_)]
    v_all = [nc.dram_tensor(f"v_all{i}", [2, 2, NT1_, 128, D], BF16, kind="Internal")
             for i in range(NT_)]

    dbg = {}
    if debug:
        dbg["dbg_xmid"] = nc.dram_tensor("dbg_xmid", [KC, 128, T], F32,
                                         kind="ExternalOutput").ap()
        dbg["dbg_xmid2"] = nc.dram_tensor("dbg_xmid2", [KC, 128, T], F32,
                                          kind="ExternalOutput").ap()
        dbg["dbg_k"] = nc.dram_tensor("dbg_k", [NT_, 2, KC, 128, TS], BF16,
                                      kind="ExternalOutput").ap()
        dbg["dbg_v"] = nc.dram_tensor("dbg_v", [NT_, 2, NT1_, 128, D], BF16,
                                      kind="ExternalOutput").ap()
        dbg["dbg_q"] = nc.dram_tensor("dbg_q", [2, KC, 128, T], BF16,
                                      kind="ExternalOutput").ap()

    ones_col_f32 = nc.inline_tensor(np.ones((128, 1), dtype=np.float32), "ones_col_f32")
    ones_row_f32 = nc.inline_tensor(np.ones((1, 128), dtype=np.float32), "ones_row_f32")
    ones_col_bf = nc.inline_tensor(np.ones((128, 1), dtype=ml_dtypes.bfloat16), "ones_col_bf")

    class Ctr:
        def __init__(self, handle, step=1):
            self.h, self.n, self.step = handle, 0, step

        def inc(self, ins, k=None):
            k = self.step if k is None else k
            self.n += k
            ins.then_inc(self.h, k)
            return self.n

    class DStream:

        def __init__(self, ctrs, dpg):
            self.c = ctrs
            self.dpg = dpg
            self.g = 0

        def inc(self, g, ins):
            self.c[g % len(self.c)].inc(ins)

        def wait(self, eng, g):
            eng.wait_ge(self.c[g % len(self.c)].h,
                        16 * self.dpg * (g // len(self.c) + 1))

    with contextlib.ExitStack() as stk:
        e = stk.enter_context

        O_SG, O_SU = 0, IC
        O_SD, O_BO = 2 * IC, 2 * IC + D2C
        O_OC = 2 * IC + 2 * D2C
        O_NM = O_OC + 1
        CONST = e(nc.sbuf_tensor("CONST", [128, O_NM + 1], F32))
        ONESB = e(nc.sbuf_tensor("ONESB", [128, 1], BF16))
        ONESR = e(nc.sbuf_tensor("ONESR", [1, 128], F32))
        SMALL = e(nc.sbuf_tensor("SMALL", [1, 4 * max(TS, QB)], F32))
        MROW = e(nc.sbuf_tensor("MROW", [1, 8], F32))
        HI = e(nc.sbuf_tensor("HI", [128, KC * TS], BF16))
        LO = e(nc.sbuf_tensor("LO", [128, KC * TS], BF16))
        PS = e(nc.psum_tensor("PS", [128, 8 * 512], F32))

        def bank(j, w=None):
            w = TS if w is None else w
            return PS[:, j * 512:j * 512 + w]

        def bankrow(j, w=None):
            w = TS if w is None else w
            return PS[0:1, j * 512:j * 512 + w]

        sS = Ctr(e(nc.semaphore("sS")), 16)
        sA = Ctr(e(nc.semaphore("sA")), 16)
        gD = Ctr(e(nc.semaphore("gD")), 16)
        cT = Ctr(e(nc.semaphore("cT")))
        cV = Ctr(e(nc.semaphore("cV")))
        cA = Ctr(e(nc.semaphore("cA")))
        cC = Ctr(e(nc.semaphore("cC")))
        bar = {n: Ctr(e(nc.semaphore("b" + n))) for n in "STVAG"}

        def mkst(name, ns, dpg):
            return DStream([Ctr(e(nc.semaphore(f"{name}{i}")), 16)
                            for i in range(ns)], dpg)

        stL = mkst("stL", 2, 1)
        stWC = mkst("stWC", 2, 2)
        stW = mkst("stW", 2, 2)
        stWD = mkst("stWD", 2, 1)
        stF = mkst("stF", 4, 2)
        stAF = mkst("stAF", 2, 2)
        stR = mkst("stR", 2, 2)
        stQW = mkst("stQW", 2, 2)
        stK = mkst("stK", 2, 2 * DQC)
        stQ = mkst("stQ", 1, 2 * DQC)
        stV = mkst("stV", 2, 2)
        stX = mkst("stX", 2, 1)
        stNB = mkst("stNB", 2, KC)
        block = e(nc.Block())
        ENG = {"S": block.sync, "T": block.tensor, "V": block.vector,
               "A": block.scalar, "G": block.gpsimd}

        nbar = [0]

        def barrier():
            nbar[0] += 1
            k = nbar[0]
            for name in "STVAG":
                def f(eng, name=name, k=k):
                    eng.nop().then_inc(bar[name].h, 1)
                    for o in "STVAG":
                        if o != name:
                            eng.wait_ge(bar[o].h, k)
                ENG[name](f)
                bar[name].n += 1

        def p_consts():
            def fsync(sync):
                for dst_, src_ in [(CONST[:, O_SG:O_SG + IC], sg_pp),
                                   (CONST[:, O_SU:O_SU + IC], su_pp),
                                   (CONST[:, O_SD:O_SD + D2C], sd_pp),
                                   (CONST[:, O_BO:O_BO + D2C], bout_pp),
                                   (CONST[:, O_OC:O_OC + 1], ones_col_f32.ap()),
                                   (ONESB[:, :], ones_col_bf.ap()),
                                   (ONESR[:, :], ones_row_f32.ap())]:
                    sS.inc(sync.dma_start(out=dst_, in_=src_))
                sync.wait_ge(sS.h, sS.n)
            block.sync(fsync)

            def fscal(scalar):
                scalar.wait_ge(sS.h, sS.n)
                cA.inc(scalar.mul(CONST[:, O_SD:O_SD + D2C],
                                  CONST[:, O_SD:O_SD + D2C], 0.5))
            block.scalar(fscal)

            def fvec(vector):
                vector.memset(MROW[:, :], 0.0)
                cV.inc(vector.memset(MROW[:, 7:8], float(cfg.eps)))
            block.vector(fvec)
            barrier()

        def p_wconv():
            W = 3 * D
            tiles = [(wqkv_t, qkv_hi, qkv_lo, 3 * D, kc) for kc in range(KC)] + \
                    [(wout_t, wo_hi, wo_lo, D, kc) for kc in range(KC)]
            with nc.sbuf_tensor("cv_W32", [128, 2 * W], F32) as W32, \
                 nc.sbuf_tensor("cv_WHI", [128, 2 * W], BF16) as WHI, \
                 nc.sbuf_tensor("cv_WLO", [128, 2 * W], BF16) as WLO:
                cV0 = cV.n
                gL, gC = stL.g, stWC.g

                def fsync(sync):
                    for i, (src, _, _, w, kc) in enumerate(tiles):
                        if i >= 2:
                            sync.wait_ge(cV.h, cV0 + (i - 1))
                        stL.inc(gL + i, sync.dma_start(
                            out=W32[:, (i % 2) * W:(i % 2) * W + w], in_=src[kc]))
                block.sync(fsync)

                def fvec(vector):
                    for i, (_, _, _, w, kc) in enumerate(tiles):
                        o = (i % 2) * W
                        stL.wait(vector, gL + i)
                        if i >= 2:
                            stWC.wait(vector, gC + i - 2)
                        vector.tensor_copy(WHI[:, o:o + w], W32[:, o:o + w])
                        cV.inc(vector.tensor_tensor(WLO[:, o:o + w], W32[:, o:o + w],
                                                    WHI[:, o:o + w], ALU.subtract))
                block.vector(fvec)

                def fscal(scalar):
                    for i, (_, dhi, dlo, w, kc) in enumerate(tiles):
                        o = (i % 2) * W
                        scalar.wait_ge(cV.h, cV0 + (i + 1))
                        stWC.inc(gC + i, scalar.dma_start(out=dhi[kc], in_=WHI[:, o:o + w]))
                        stWC.inc(gC + i, scalar.dma_start(out=dlo[kc], in_=WLO[:, o:o + w]))
                block.scalar(fscal)
                stL.g += len(tiles)
                stWC.g += len(tiles)
                barrier()

        def emit_ln(ts, src, XA, XB, w_xa=0, w_xb=0, w_banks=0):
            def fsync(sync):
                if w_xa:
                    sync.wait_ge(cV.h, w_xa)
                for c in range(KC):
                    sS.inc(sync.dma_start(out=XA[:, c * TS:(c + 1) * TS],
                                          in_=src[c][:, ts:ts + TS]))
            block.sync(fsync)
            s_loaded = sS.n

            def fscal(scalar):
                scalar.wait_ge(sS.h, s_loaded)
                if w_xb:
                    scalar.wait_ge(sA.h, w_xb)
                cA.inc(scalar.square(XB[:, 0:KC * TS], XA[:, 0:KC * TS]))
            block.scalar(fscal)
            a_sq = cA.n

            def ftens(tensor):
                tensor.wait_ge(sS.h, s_loaded)
                if w_banks:
                    tensor.wait_ge(cV.h, w_banks)
                for c in range(KC):
                    tensor.matmul(bankrow(6), CONST[:, O_OC:O_OC + 1],
                                  XA[:, c * TS:(c + 1) * TS],
                                  start=(c == 0), stop=(c == KC - 1))
                tensor.wait_ge(cA.h, a_sq)
                for c in range(KC):
                    mmi = tensor.matmul(bankrow(7), CONST[:, O_OC:O_OC + 1],
                                        XB[:, c * TS:(c + 1) * TS],
                                        start=(c == 0), stop=(c == KC - 1))
                cT.inc(mmi)
            block.tensor(ftens)
            t_stats = cT.n

            def fscal2(scalar):
                scalar.wait_ge(cT.h, t_stats)
                scalar.mul(SMALL[:, 0:TS], bankrow(6), 1.0 / D)
                scalar.mul(SMALL[:, TS:2 * TS], bankrow(7), 1.0 / D)
                cA.inc(scalar.square(SMALL[:, 2 * TS:3 * TS], SMALL[:, 0:TS]))
            block.scalar(fscal2)
            a_m = cA.n

            def fvec(vector):
                vector.wait_ge(cA.h, a_m)
                cV.inc(vector.tensor_sub(SMALL[:, TS:2 * TS], SMALL[:, TS:2 * TS],
                                         SMALL[:, 2 * TS:3 * TS]))
            block.vector(fvec)
            v_var = cV.n

            def fscal3(scalar):
                scalar.wait_ge(cV.h, v_var)
                cA.inc(scalar.activation(SMALL[:, 2 * TS:3 * TS], SMALL[:, TS:2 * TS],
                                         AF.Sqrt, bias=MROW[:, 7:8]))
            block.scalar(fscal3)
            a_sd = cA.n

            def fvec2(vector):
                vector.wait_ge(cA.h, a_sd)
                cV.inc(vector.reciprocal(SMALL[:, TS:2 * TS], SMALL[:, 2 * TS:3 * TS]))
            block.vector(fvec2)
            v_r = cV.n

            def ftens2(tensor):
                tensor.wait_ge(cV.h, v_r)
                tensor.matmul(bank(4), ONESR[:, :], SMALL[:, 0:TS], start=True, stop=True)
                mmi = tensor.matmul(bank(5), ONESR[:, :], SMALL[:, TS:2 * TS],
                                    start=True, stop=True)
                cT.inc(mmi)
            block.tensor(ftens2)
            t_b = cT.n

            v_ln0 = cV.n

            def fvec3(vector):
                vector.wait_ge(cT.h, t_b)
                for c in range(KC):
                    sl = slice(c * TS, (c + 1) * TS)
                    vector.tensor_sub(XB[:, sl], XA[:, sl], bank(4))
                    vector.tensor_mul(XB[:, sl], XB[:, sl], bank(5))
                    vector.tensor_copy(HI[:, sl], XB[:, sl])
                    cV.inc(vector.tensor_tensor(LO[:, sl], XB[:, sl],
                                                HI[:, sl], ALU.subtract))
            block.vector(fvec3)
            cV.n = cV.n
            return v_ln0

        def p_ffn(src, dst, tag, final=False, parts=("ln", "gu", "dn")):
            with nc.sbuf_tensor(f"{tag}_XA", [128, KC * TS], F32) as XA, \
                 nc.sbuf_tensor(f"{tag}_XB", [128, KC * TS], F32) as XB, \
                 nc.sbuf_tensor(f"{tag}_WG", [128, 2 * D], BF16) as WG, \
                 nc.sbuf_tensor(f"{tag}_WU", [128, 2 * D], BF16) as WU, \
                 nc.sbuf_tensor(f"{tag}_WD", [128, 2 * D], BF16) as WD, \
                 nc.sbuf_tensor(f"{tag}_GS", [128, 2 * TS], F32) as GS, \
                 nc.sbuf_tensor(f"{tag}_FH", [128, 2 * TS], BF16) as FH, \
                 nc.sbuf_tensor(f"{tag}_FL", [128, 2 * TS], BF16) as FL, \
                 nc.sbuf_tensor(f"{tag}_FHR", [128, 4 * TS], BF16) as FHR, \
                 nc.sbuf_tensor(f"{tag}_FLR", [128, 4 * TS], BF16) as FLR:
                prev_dn = [0]
                prev_st = [0]
                prev_gu = [0]
                for it in range(NT):
                    ts = it * TS
                    v_ln = emit_ln(ts, src, XA, XB, w_xa=prev_dn[0],
                                   w_xb=prev_st[0], w_banks=prev_dn[0])
                    if "gu" not in parts:
                        barrier()
                        continue

                    t0, v0, a0 = cT.n, cV.n, cA.n
                    gW0 = stW.g

                    def fg(gpsimd, gW0=gW0, t0=t0, pgu=prev_gu[0]):
                        for ic in range(IC):
                            if ic >= 2:
                                gpsimd.wait_ge(cT.h, t0 + (ic - 1))
                            elif pgu:
                                gpsimd.wait_ge(cT.h, pgu)
                            o = (ic % 2) * D
                            stW.inc(gW0 + ic, gpsimd.dma_start(out=WG[:, o:o + D], in_=wg_t[ic]))
                            stW.inc(gW0 + ic, gpsimd.dma_start(out=WU[:, o:o + D], in_=wu_t[ic]))
                    block.gpsimd(fg)

                    def ft(tensor, gW0=gW0, v0=v0, v_ln=v_ln):
                        for ic in range(IC):
                            stW.wait(tensor, gW0 + ic)
                            if ic >= 2:
                                tensor.wait_ge(cV.h, v0 + (ic - 1))
                            o = (ic % 2) * D
                            pg = bank(ic % 2)
                            pu = bank(2 + ic % 2)
                            for kc in range(KC):
                                if ic == 0:
                                    tensor.wait_ge(cV.h, v_ln + kc + 1)
                                wgc = WG[:, o + kc * 128:o + (kc + 1) * 128]
                                wuc = WU[:, o + kc * 128:o + (kc + 1) * 128]
                                xh = HI[:, kc * TS:(kc + 1) * TS]
                                xl = LO[:, kc * TS:(kc + 1) * TS]
                                tensor.matmul(pg, wgc, xh, start=(kc == 0), stop=False)
                                tensor.matmul(pg, wgc, xl, start=False, stop=(kc == KC - 1))
                                tensor.matmul(pu, wuc, xh, start=(kc == 0), stop=False)
                                mmi = tensor.matmul(pu, wuc, xl, start=False,
                                                    stop=(kc == KC - 1))
                            cT.inc(mmi)
                    block.tensor(ft)
                    prev_gu[0] = cT.n

                    gAF0 = stAF.g

                    def fa(scalar, t0=t0, v0=v0, gAF0=gAF0):
                        for ic in range(IC + 1):
                            if ic < IC:
                                scalar.wait_ge(cT.h, t0 + (ic + 1))
                                if ic >= 2:
                                    scalar.wait_ge(cV.h, v0 + (ic - 1))
                                cA.inc(scalar.activation(
                                    GS[:, (ic % 2) * TS:(ic % 2 + 1) * TS], bank(ic % 2),
                                    AF.Silu, scale=CONST[:, O_SG + ic:O_SG + ic + 1]))
                            if ic >= 1:
                                j = ic - 1
                                scalar.wait_ge(cV.h, v0 + (j + 1))
                                sl = slice((j % 2) * TS, (j % 2 + 1) * TS)
                                stAF.inc(gAF0 + j, scalar.dma_start(out=fuse_d[j, 0], in_=FH[:, sl]))
                                stAF.inc(gAF0 + j, scalar.dma_start(out=fuse_d[j, 1], in_=FL[:, sl]))
                    block.scalar(fa)

                    def fv(vector, a0=a0, gAF0=gAF0):
                        for ic in range(IC):
                            vector.wait_ge(cA.h, a0 + (ic + 1))
                            if ic >= 2:
                                stAF.wait(vector, gAF0 + ic - 2)
                            sl = slice((ic % 2) * TS, (ic % 2 + 1) * TS)
                            vector.scalar_tensor_tensor(GS[:, sl], bank(2 + ic % 2),
                                                        CONST[:, O_SU + ic:O_SU + ic + 1],
                                                        GS[:, sl], ALU.mult, ALU.mult)
                            vector.tensor_copy(FH[:, sl], GS[:, sl])
                            cV.inc(vector.tensor_tensor(FL[:, sl], GS[:, sl], FH[:, sl],
                                                        ALU.subtract))
                    block.vector(fv)
                    stAF.g += IC
                    af_t = [stAF.c[s].n for s in range(2)]

                    def wait_fuse(eng):
                        for s in range(2):
                            eng.wait_ge(stAF.c[s].h, af_t[s])
                    if "dn" not in parts:
                        def fz(scalar):
                            scalar.wait_ge(sA.h, sA.n)
                        block.scalar(fz)
                        barrier()
                        continue

                    t1, v1 = cT.n, cV.n
                    gF0, gWD0 = stF.g, stWD.g

                    def fs_dn(sync, gF0=gF0, t1=t1):
                        wait_fuse(sync)
                        for i in range(2 * IC):
                            ic = i % IC
                            if i >= 4:
                                sync.wait_ge(cT.h, t1 + (i - 3))
                            b = i % 4
                            stF.inc(gF0 + i, sync.dma_start(out=FHR[:, b * TS:(b + 1) * TS],
                                                            in_=fuse_d[ic, 0]))
                            stF.inc(gF0 + i, sync.dma_start(out=FLR[:, b * TS:(b + 1) * TS],
                                                            in_=fuse_d[ic, 1]))
                    block.sync(fs_dn)

                    def fg_dn(gpsimd, gWD0=gWD0, t1=t1):
                        wait_fuse(gpsimd)
                        for i in range(2 * IC):
                            half, ic = i // IC, i % IC
                            if i >= 2:
                                gpsimd.wait_ge(cT.h, t1 + (i - 1))
                            o = (i % 2) * D
                            stWD.inc(gWD0 + i, gpsimd.dma_start(
                                out=WD[:, o:o + D // 2],
                                in_=wd_t[ic][:, half * (D // 2):(half + 1) * (D // 2)]))
                    block.gpsimd(fg_dn)

                    def ft_dn(tensor, gF0=gF0, gWD0=gWD0, v1=v1):
                        for i in range(2 * IC):
                            half, ic = i // IC, i % IC
                            stF.wait(tensor, gF0 + i)
                            stWD.wait(tensor, gWD0 + i)
                            if half == 1 and ic == 0:
                                tensor.wait_ge(cV.h, v1 + D2C // 2)
                            b = i % 4
                            o = (i % 2) * D
                            for j in range(D2C // 2):
                                py = bank(j)
                                wdc = WD[:, o + j * 128:o + (j + 1) * 128]
                                tensor.matmul(py, wdc, FHR[:, b * TS:(b + 1) * TS],
                                              start=(ic == 0), stop=False)
                                mmi = tensor.matmul(py, wdc, FLR[:, b * TS:(b + 1) * TS],
                                                    start=False, stop=(ic == IC - 1))
                            cT.inc(mmi)
                    block.tensor(ft_dn)
                    stF.g += 2 * IC
                    stWD.g += 2 * IC

                    def fv_dn(vector, t1=t1):
                        for half in range(2):
                            vector.wait_ge(cT.h, t1 + (half + 1) * IC)
                            for j in range(D2C // 2):
                                c = half * (D2C // 2) + j
                                cV.inc(vector.scalar_tensor_tensor(
                                    XB[:, c * TS:(c + 1) * TS], bank(j),
                                    CONST[:, O_SD + c:O_SD + c + 1],
                                    XA[:, c * TS:(c + 1) * TS], ALU.mult, ALU.add))
                    block.vector(fv_dn)
                    v_dn0 = cV.n - D2C

                    def fa3(scalar, v_dn0=v_dn0, ts=ts):
                        for c in range(D2C):
                            scalar.wait_ge(cV.h, v_dn0 + c + 1)
                            sA.inc(scalar.dma_start(out=dst[c][:, ts:ts + TS],
                                                    in_=XB[:, c * TS:(c + 1) * TS]))
                    block.scalar(fa3)
                    prev_dn[0] = cV.n
                    prev_st[0] = sA.n
                    if final and it == NT - 1:
                        def fa4(scalar):
                            scalar.wait_ge(sA.h, sA.n)
                        block.scalar(fa4)
                    if it == NT - 1:
                        barrier()

        def p_qkv():
            OCG = min(8, KC)
            NGH = KC // OCG
            NT1 = TS // 128
            VB = min(512, D // 2)
            NDB = (D // 2) // VB
            SLT = max(OCG * 128, D // 2)
            with nc.sbuf_tensor("qk_XA", [128, KC * TS], F32) as XA, \
                 nc.sbuf_tensor("qk_XB", [128, KC * TS], F32) as XB, \
                 nc.sbuf_tensor("qk_WQH", [128, 2 * SLT], BF16) as WQH, \
                 nc.sbuf_tensor("qk_WQL", [128, 2 * SLT], BF16) as WQL, \
                 nc.sbuf_tensor("qk_RC", [128, 2 * TS], F32) as RC, \
                 nc.sbuf_tensor("qk_RS", [128, 2 * TS], F32) as RS, \
                 nc.sbuf_tensor("qk_QH", [128, KC * TS], BF16) as QH, \
                 nc.sbuf_tensor("qk_QL", [128, KC * TS], BF16) as QL, \
                 nc.sbuf_tensor("qk_KH", [128, KC * TS], BF16) as KH, \
                 nc.sbuf_tensor("qk_KL", [128, KC * TS], BF16) as KL:
                for it in range(NT):
                    ts = it * TS
                    sA_prev = sA.n
                    v_ln = emit_ln(ts, xmid, XA, XB)

                    qk_banks_free = [0]
                    for p in range(2):
                        t2, v2 = cT.n, cV.n
                        qk_banks_free[0] = v2 + NGH
                        gQ0 = stQW.g

                        def fg_qk(gpsimd, gQ0=gQ0, t2=t2, p=p):
                            gpsimd.wait_ge(cT.h, t2)
                            for gi in range(NGH):
                                for kc in range(KC):
                                    i = gi * KC + kc
                                    if i >= 2:
                                        gpsimd.wait_ge(cT.h, t2 + (i - 1))
                                    o = (i % 2) * SLT
                                    co = (p * KC + gi * OCG) * 128
                                    stQW.inc(gQ0 + i, gpsimd.dma_start(
                                        out=WQH[:, o:o + OCG * 128],
                                        in_=qkv_hi[kc][:, co:co + OCG * 128]))
                                    stQW.inc(gQ0 + i, gpsimd.dma_start(
                                        out=WQL[:, o:o + OCG * 128],
                                        in_=qkv_lo[kc][:, co:co + OCG * 128]))
                        block.gpsimd(fg_qk)

                        def ft_qk(tensor, gQ0=gQ0, v2=v2, v_ln=v_ln, p=p):
                            tensor.wait_ge(cV.h, v2)
                            for gi in range(NGH):
                                if gi >= 1:
                                    tensor.wait_ge(cV.h, v2 + gi)
                                for kc in range(KC):
                                    i = gi * KC + kc
                                    stQW.wait(tensor, gQ0 + i)
                                    if p == 0 and gi == 0:
                                        tensor.wait_ge(cV.h, v_ln + kc + 1)
                                    o = (i % 2) * SLT
                                    xh = HI[:, kc * TS:(kc + 1) * TS]
                                    xl = LO[:, kc * TS:(kc + 1) * TS]
                                    for j in range(OCG):
                                        pj = bank(j)
                                        wh = WQH[:, o + j * 128:o + (j + 1) * 128]
                                        wl = WQL[:, o + j * 128:o + (j + 1) * 128]
                                        tensor.matmul(pj, wh, xh, start=(kc == 0), stop=False)
                                        tensor.matmul(pj, wh, xl, start=False, stop=False)
                                        mmi = tensor.matmul(pj, wl, xh, start=False,
                                                            stop=(kc == KC - 1))
                                    cT.inc(mmi)
                        block.tensor(ft_qk)
                        stQW.g += NGH * KC
                        t_qk0 = cT.n - NGH * KC

                        def fv_ev(vector, t_qk0=t_qk0):
                            for gi in range(NGH):
                                vector.wait_ge(cT.h, t_qk0 + (gi + 1) * KC)
                                for j in range(OCG):
                                    oc = gi * OCG + j
                                    vi = vector.tensor_copy(XB[:, oc * TS:(oc + 1) * TS],
                                                            bank(j))
                                cV.inc(vi)
                        block.vector(fv_ev)

                        gR0 = stR.g
                        v_rope0 = cV.n

                        def fv_rope(vector, gR0=gR0, p=p, sA_prev=sA_prev, it=it):
                            if p == 0 and it > 0:
                                vector.wait_ge(sA.h, sA_prev)
                            for c in range(DQC):
                                b = c % 2
                                stR.wait(vector, gR0 + c)
                                s0 = slice(c * TS, (c + 1) * TS)
                                s1_ = slice((DQC + c) * TS, (DQC + c + 1) * TS)
                                rc = RC[:, b * TS:(b + 1) * TS]
                                rs_ = RS[:, b * TS:(b + 1) * TS]
                                vector.tensor_mul(XA[:, 0:TS], XB[:, s1_], rs_)
                                vector.tensor_mul(XA[:, TS:2 * TS], XB[:, s0], rs_)
                                vector.tensor_mul(XB[:, s0], XB[:, s0], rc)
                                vector.tensor_sub(XB[:, s0], XB[:, s0], XA[:, 0:TS])
                                vector.tensor_mul(XB[:, s1_], XB[:, s1_], rc)
                                cV.inc(vector.tensor_add(XB[:, s1_], XB[:, s1_],
                                                         XA[:, TS:2 * TS]))
                            DH_, DL_ = (QH, QL) if p == 0 else (KH, KL)
                            vector.tensor_copy(DH_[:, 0:KC * TS], XB[:, 0:KC * TS])
                            cV.inc(vector.tensor_tensor(DL_[:, 0:KC * TS], XB[:, 0:KC * TS],
                                                        DH_[:, 0:KC * TS], ALU.subtract))
                        block.vector(fv_rope)
                        v_cast = cV.n

                        def fs_rt(sync, ts=ts, v_rope0=v_rope0, gR0=gR0):
                            sync.wait_ge(cV.h, v_rope0)
                            for c in range(DQC):
                                b = c % 2
                                if c >= 2:
                                    sync.wait_ge(cV.h, v_rope0 + (c - 1))
                                stR.inc(gR0 + c, sync.dma_start(out=RC[:, b * TS:(b + 1) * TS],
                                                                in_=cos_t[c][:, ts:ts + TS]))
                                stR.inc(gR0 + c, sync.dma_start(out=RS[:, b * TS:(b + 1) * TS],
                                                                in_=sin_t[c][:, ts:ts + TS]))
                        block.sync(fs_rt)
                        stR.g += DQC

                        def fa_pk(scalar, v_cast=v_cast, ts=ts, p=p, it=it):
                            scalar.wait_ge(cV.h, v_cast)
                            for c in range(KC):
                                if p == 0:
                                    sA.inc(scalar.dma_start(
                                        out=qpark[0, c][:, ts:ts + TS],
                                        in_=QH[:, c * TS:(c + 1) * TS]))
                                    sA.inc(scalar.dma_start(
                                        out=qpark[1, c][:, ts:ts + TS],
                                        in_=QL[:, c * TS:(c + 1) * TS]))
                                else:
                                    sA.inc(scalar.dma_start(
                                        out=k_in[it].ap()[0, c],
                                        in_=KH[:, c * TS:(c + 1) * TS]))
                                    sA.inc(scalar.dma_start(
                                        out=k_in[it].ap()[1, c],
                                        in_=KL[:, c * TS:(c + 1) * TS]))
                        block.scalar(fa_pk)
                        if p == 1:
                            a_kstage = sA.n

                            def fg_agk(gpsimd, a_kstage=a_kstage, it=it):
                                gpsimd.wait_ge(sA.h, a_kstage)
                                cC.inc(gpsimd.collective_compute(
                                    "AllGather", ALU.bypass,
                                    replica_groups=[[2 * i, 2 * i + 1]
                                                    for i in range(NCORES // 2)],
                                    ins=[k_in[it].ap().opt()],
                                    outs=[k_all[it].ap().opt()]))
                            block.gpsimd(fg_agk)
                    a_park = sA.n

                    for dvh in range(2):
                        t3 = cT.n
                        gV0 = stQW.g

                        def fg_v(gpsimd, gV0=gV0, t3=t3, dvh=dvh):
                            gpsimd.wait_ge(cT.h, t3)
                            for kc in range(KC):
                                if kc >= 2:
                                    gpsimd.wait_ge(cT.h, t3 + (kc - 1))
                                o = (kc % 2) * SLT
                                vo = 2 * D + dvh * (D // 2)
                                stQW.inc(gV0 + kc, gpsimd.dma_start(
                                    out=WQH[:, o:o + D // 2],
                                    in_=qkv_hi[kc][:, vo:vo + D // 2]))
                                stQW.inc(gV0 + kc, gpsimd.dma_start(
                                    out=WQL[:, o:o + D // 2],
                                    in_=qkv_lo[kc][:, vo:vo + D // 2]))
                        block.gpsimd(fg_v)

                        def ft_v(tensor, gV0=gV0, dvh=dvh,
                                 v=(qk_banks_free[0] if dvh == 0 else cV.n)):
                            tensor.wait_ge(cV.h, v)
                            for kc in range(KC):
                                stQW.wait(tensor, gV0 + kc)
                                o = (kc % 2) * SLT
                                for t1 in range(NT1):
                                    xh = HI[:, kc * TS + t1 * 128:kc * TS + (t1 + 1) * 128]
                                    xl = LO[:, kc * TS + t1 * 128:kc * TS + (t1 + 1) * 128]
                                    for db_ in range(NDB):
                                        pj = bank(t1 * NDB + db_, VB)
                                        wh = WQH[:, o + db_ * VB:o + (db_ + 1) * VB]
                                        wl = WQL[:, o + db_ * VB:o + (db_ + 1) * VB]
                                        tensor.matmul(pj, xh, wh, start=(kc == 0), stop=False)
                                        tensor.matmul(pj, xh, wl, start=False, stop=False)
                                        mmi = tensor.matmul(pj, xl, wh, start=False,
                                                            stop=(kc == KC - 1))
                                cT.inc(mmi)
                        block.tensor(ft_v)
                        stQW.g += KC
                        t_v_end = cT.n

                        def fv_v(vector, t_v_end=t_v_end, a_park=a_park, dvh=dvh):
                            vector.wait_ge(cT.h, t_v_end)
                            if dvh == 0:
                                vector.wait_ge(sA.h, a_park)
                            n = NT1 * NDB
                            for s_ in range(n):
                                sl = slice(s_ * VB, (s_ + 1) * VB)
                                vector.tensor_copy(XB[:, sl], bank(s_, VB))
                            for s_ in range(n):
                                sl = slice(s_ * VB, (s_ + 1) * VB)
                                vector.tensor_copy(QH[:, sl], XB[:, sl])
                                vi = vector.tensor_tensor(QL[:, sl], XB[:, sl], QH[:, sl],
                                                          ALU.subtract)
                            cV.inc(vi)
                        block.vector(fv_v)
                        v_vc = cV.n

                        def fa_v(scalar, v_vc=v_vc, dvh=dvh, ts=ts, it=it):
                            scalar.wait_ge(cV.h, v_vc)
                            for t1 in range(NT1):
                                for db_ in range(NDB):
                                    s_ = t1 * NDB + db_
                                    sl = slice(s_ * VB, (s_ + 1) * VB)
                                    vo = dvh * (D // 2) + db_ * VB
                                    sA.inc(scalar.dma_start(
                                        out=v_in[it].ap()[0, t1][:, vo:vo + VB],
                                        in_=QH[:, sl]))
                                    sA.inc(scalar.dma_start(
                                        out=v_in[it].ap()[1, t1][:, vo:vo + VB],
                                        in_=QL[:, sl]))
                        block.scalar(fa_v)
                        if dvh == 1:
                            a_vstage = sA.n

                            def fg_agv(gpsimd, a_vstage=a_vstage, it=it):
                                gpsimd.wait_ge(sA.h, a_vstage)
                                cC.inc(gpsimd.collective_compute(
                                    "AllGather", ALU.bypass,
                                    replica_groups=[[2 * i, 2 * i + 1]
                                                    for i in range(NCORES // 2)],
                                    ins=[v_in[it].ap().opt()],
                                    outs=[v_all[it].ap().opt()]))
                            block.gpsimd(fg_agv)
                    barrier()

                gNB0 = stNB.g
                a_vdone = sA.n
                v_nb0 = cV.n
                a_nb0 = cA.n
                t_nb0 = cT.n
                steps = []
                for it2 in range(NT):
                    steps.append(("q", it2))
                    steps.append(("k", it2))

                def fs_nb(sync, gNB0=gNB0, a_vdone=a_vdone, a_nb0=a_nb0):
                    sync.wait_ge(sA.h, a_vdone)
                    for s_, (qk, it2) in enumerate(steps):
                        if s_ >= 2:
                            sync.wait_ge(cA.h, a_nb0 + s_ - 1)
                        buf = QH if s_ % 2 == 0 else QL
                        for c in range(KC):
                            srcap = (qpark[0, c][:, it2 * TS:(it2 + 1) * TS] if qk == "q"
                                     else k_in[it2].ap()[0, c])
                            stNB.inc(gNB0 + s_, sync.dma_start(
                                out=buf[:, c * TS:(c + 1) * TS], in_=srcap))
                block.sync(fs_nb)
                stNB.g += len(steps)

                def fa_nb(scalar, gNB0=gNB0, t_nb0=t_nb0):
                    for s_, (qk, it2) in enumerate(steps):
                        stNB.wait(scalar, gNB0 + s_)
                        if s_ >= 1:
                            scalar.wait_ge(cT.h, t_nb0 + s_)
                        buf = QH if s_ % 2 == 0 else QL
                        cA.inc(scalar.square(KH[:, 0:KC * TS], buf[:, 0:KC * TS]))
                block.scalar(fa_nb)

                def ft_nb(tensor, a_nb0=a_nb0, v_nb0=v_nb0):
                    tensor.wait_ge(cV.h, v_nb0)
                    for s_, (qk, it2) in enumerate(steps):
                        tensor.wait_ge(cA.h, a_nb0 + s_ + 1)
                        bk = 4 if qk == "q" else 5
                        if s_ >= 2:
                            tensor.wait_ge(cV.h, v_nb0 + s_ - 1)
                        for c in range(KC):
                            mmi = tensor.matmul(bankrow(bk), ONESB[:, :],
                                                KH[:, c * TS:(c + 1) * TS],
                                                start=(c == 0), stop=(c == KC - 1))
                        cT.inc(mmi)
                block.tensor(ft_nb)

                def fv_nb(vector, t_nb0=t_nb0, v_nb0=v_nb0):
                    for s_, (qk, it2) in enumerate(steps):
                        vector.wait_ge(cT.h, t_nb0 + s_ + 1)
                        bk = 4 if qk == "q" else 5
                        m = 0 if qk == "q" else 1
                        vector.reduce_max(MROW[:, 4:5], bankrow(bk), axis=AXX)
                        cV.inc(vector.tensor_max(MROW[:, m:m + 1], MROW[:, m:m + 1],
                                                 MROW[:, 4:5]))
                block.vector(fv_nb)

                def fv_m(vector):
                    cV.inc(vector.tensor_mul(MROW[:, 2:3], MROW[:, 0:1], MROW[:, 1:2]))
                block.vector(fv_m)

                def fa_m(scalar, v=cV.n):
                    scalar.wait_ge(cV.h, v)
                    scalar.activation(MROW[:, 3:4], MROW[:, 2:3], AF.Sqrt)
                    cA.inc(scalar.mul(MROW[:, 3:4], MROW[:, 3:4], -float(cfg.SC)))
                block.scalar(fa_m)

                def ft_m(tensor, a=cA.n):
                    tensor.wait_ge(cA.h, a)
                    mmi = tensor.matmul(PS[:, 0:1], ONESR[:, :], MROW[:, 3:4],
                                        start=True, stop=True)
                    cT.inc(mmi)
                block.tensor(ft_m)

                def fv_m2(vector, t=cT.n):
                    vector.wait_ge(cT.h, t)
                    cV.inc(vector.tensor_copy(CONST[:, O_NM:O_NM + 1], PS[:, 0:1]))
                block.vector(fv_m2)
                barrier()

        def p_collective():
            if debug:
                def fg2(gpsimd):
                    gpsimd.wait_ge(cC.h, cC.n)
                    for i in range(NT):
                        gD.inc(gpsimd.dma_start(out=dbg["dbg_k"][i], in_=k_in[i].ap()))
                        gD.inc(gpsimd.dma_start(out=dbg["dbg_v"][i], in_=v_in[i].ap()))
                    gD.inc(gpsimd.dma_start(out=dbg["dbg_q"], in_=qpark))
                    gpsimd.wait_ge(gD.h, gD.n)
                block.gpsimd(fg2)
                barrier()

        def p_attn():
            NKSL = S // 512
            NB = 512 // 128
            with nc.sbuf_tensor("at_EH", [128, T2C * QB], BF16) as EH, \
                 nc.sbuf_tensor("at_EL", [128, T2C * QB], BF16) as EL, \
                 nc.sbuf_tensor("at_QSH", [128, DQC * QB], BF16) as QSH, \
                 nc.sbuf_tensor("at_QSL", [128, DQC * QB], BF16) as QSL, \
                 nc.sbuf_tensor("at_KSH", [128, 2 * DQC * 512], BF16) as KSH, \
                 nc.sbuf_tensor("at_KSL", [128, 2 * DQC * 512], BF16) as KSL, \
                 nc.sbuf_tensor("at_VSH", [128, 2 * DH], BF16) as VSH, \
                 nc.sbuf_tensor("at_VSL", [128, 2 * DH], BF16) as VSL, \
                 nc.sbuf_tensor("at_WOH", [128, 2 * (D // 2)], BF16) as WOH, \
                 nc.sbuf_tensor("at_WOL", [128, 2 * (D // 2)], BF16) as WOL, \
                 nc.sbuf_tensor("at_E32", [128, 2 * 512], F32) as E32, \
                 nc.sbuf_tensor("at_RB", [128, QB], F32) as RB, \
                 nc.sbuf_tensor("at_OB", [128, DQC * QB], F32) as OB, \
                 nc.sbuf_tensor("at_XS", [128, 2 * 512], F32) as XS, \
                 nc.sbuf_tensor("at_YS", [128, 2 * 512], F32) as YS:
                prev_t_av = [0]
                for qb in range(NQB):
                    tq = qb * QB
                    for h in range(H):
                        v_prev = cV.n
                        gQb = stQ.g
                        gKb = stK.g

                        def fs_q(sync, tq=tq, h=h, v_prev=v_prev, qb=qb, gQb=gQb):
                            if not (qb == 0 and h == 0):
                                sync.wait_ge(cV.h, v_prev)
                            for ver, dst_ in ((0, QSH), (1, QSL)):
                                for c in range(DQC):
                                    stQ.inc(gQb, sync.dma_start(
                                        out=dst_[:, c * QB:(c + 1) * QB],
                                        in_=qpark[ver, h * DQC + c][:, tq:tq + QB]))
                        block.sync(fs_q)
                        stQ.g += 1

                        t4, v4, a4 = cT.n, cV.n, cA.n

                        def fs_k(sync, gKb=gKb, t4=t4, h=h, qb=qb):
                            for sl_ in range(NKSL):
                                r, lo_ = sl_ // (NKSL // 2), (sl_ % (NKSL // 2)) * 512
                                kt, off = lo_ // TS, lo_ % TS
                                if qb == 0 and h == 0:
                                    sync.wait_ge(cC.h, 2 * kt + 1)
                                if sl_ >= 2:
                                    sync.wait_ge(cT.h, t4 + (sl_ - 1) * NB)
                                ko = (sl_ % 2) * DQC * 512
                                for ver, dst_ in ((0, KSH), (1, KSL)):
                                    for c in range(DQC):
                                        stK.inc(gKb + sl_, sync.dma_start(
                                            out=dst_[:, ko + c * 512:ko + (c + 1) * 512],
                                            in_=k_all[kt].ap()[r, ver, h * DQC + c]
                                                [:, off:off + 512]))
                        block.sync(fs_k)
                        stK.g += NKSL

                        def ft_s(tensor, gQb=gQb, gKb=gKb, a4=a4, v4=v4, v_prev=v_prev,
                                 qb=qb, h=h):
                            if not (qb == 0 and h == 0):
                                tensor.wait_ge(cV.h, v_prev)
                            stQ.wait(tensor, gQb)
                            for sl_ in range(NKSL):
                                stK.wait(tensor, gKb + sl_)
                                ko = (sl_ % 2) * DQC * 512
                                for b in range(NB):
                                    t2c = sl_ * NB + b
                                    pj = bank(t2c % 4, QB)
                                    if t2c >= 4:
                                        tensor.wait_ge(cA.h, a4 + (t2c - 3))
                                    for c in range(DQC):
                                        kh = KSH[:, ko + c * 512 + b * 128:ko + c * 512 + (b + 1) * 128]
                                        kl = KSL[:, ko + c * 512 + b * 128:ko + c * 512 + (b + 1) * 128]
                                        qh = QSH[:, c * QB:(c + 1) * QB]
                                        ql = QSL[:, c * QB:(c + 1) * QB]
                                        tensor.matmul(pj, kh, qh, start=(c == 0), stop=False)
                                        tensor.matmul(pj, kh, ql, start=False, stop=False)
                                        mmi = tensor.matmul(pj, kl, qh, start=False,
                                                            stop=(c == DQC - 1))
                                    cT.inc(mmi)
                            for t2c in range(T2C):
                                tensor.wait_ge(cV.h, v4 + t2c + 1)
                                tensor.matmul(bankrow(4, QB), ONESB[:, :],
                                              EH[:, t2c * QB:(t2c + 1) * QB],
                                              start=(t2c == 0), stop=False)
                                mmi = tensor.matmul(bankrow(4, QB), ONESB[:, :],
                                                    EL[:, t2c * QB:(t2c + 1) * QB],
                                                    start=False, stop=(t2c == T2C - 1))
                            cT.inc(mmi)
                        block.tensor(ft_s)
                        t_sums = cT.n

                        def fa_e(scalar, t4=t4, v4=v4):
                            for t2c in range(T2C):
                                scalar.wait_ge(cT.h, t4 + t2c + 1)
                                if t2c >= 2:
                                    scalar.wait_ge(cV.h, v4 + t2c - 1)
                                cA.inc(scalar.activation(
                                    E32[:, (t2c % 2) * 512:(t2c % 2) * 512 + QB],
                                    bank(t2c % 4, QB), AF.Exp,
                                    bias=CONST[:, O_NM:O_NM + 1], scale=float(cfg.SC)))
                        block.scalar(fa_e)

                        def fv_e(vector, a4=a4):
                            for t2c in range(T2C):
                                vector.wait_ge(cA.h, a4 + t2c + 1)
                                sl_ = slice(t2c * QB, (t2c + 1) * QB)
                                e3 = E32[:, (t2c % 2) * 512:(t2c % 2) * 512 + QB]
                                vector.tensor_copy(EH[:, sl_], e3)
                                cV.inc(vector.tensor_tensor(EL[:, sl_], e3, EH[:, sl_],
                                                            ALU.subtract))
                        block.vector(fv_e)

                        def fv_r(vector, t=t_sums):
                            vector.wait_ge(cT.h, t)
                            cV.inc(vector.reciprocal(SMALL[:, 0:QB], bankrow(4, QB)))
                        block.vector(fv_r)

                        def ft_rb(tensor, v=cV.n):
                            tensor.wait_ge(cV.h, v)
                            mmi = tensor.matmul(bank(5, QB), ONESR[:, :], SMALL[:, 0:QB],
                                                start=True, stop=True)
                            cT.inc(mmi)
                        block.tensor(ft_rb)

                        def fv_rb(vector, t=cT.n):
                            vector.wait_ge(cT.h, t)
                            cV.inc(vector.tensor_copy(RB[:, 0:QB], bank(5, QB)))
                        block.vector(fv_rb)
                        v_rb = cV.n

                        gVb = stV.g
                        tav_base = cT.n

                        def fg_v(gpsimd, gVb=gVb, t5=prev_t_av[0], tav_base=tav_base,
                                 h=h, qb=qb):
                            gpsimd.wait_ge(cT.h, t5)
                            for t2c in range(T2C):
                                r, lc = t2c // (T2C // 2), t2c % (T2C // 2)
                                vt, t1l = lc // NT1_, lc % NT1_
                                if qb == 0 and h == 0:
                                    gpsimd.wait_ge(cC.h, 2 * vt + 2)
                                if t2c >= 2:
                                    gpsimd.wait_ge(cT.h, tav_base + t2c - 1)
                                o = (t2c % 2) * DH
                                stV.inc(gVb + t2c, gpsimd.dma_start(
                                    out=VSH[:, o:o + DH],
                                    in_=v_all[vt].ap()[r, 0, t1l][:, h * DH:(h + 1) * DH]))
                                stV.inc(gVb + t2c, gpsimd.dma_start(
                                    out=VSL[:, o:o + DH],
                                    in_=v_all[vt].ap()[r, 1, t1l][:, h * DH:(h + 1) * DH]))
                        block.gpsimd(fg_v)

                        def ft_av(tensor, gVb=gVb, v_rb=v_rb):
                            tensor.wait_ge(cV.h, v_rb)
                            for t2c in range(T2C):
                                stV.wait(tensor, gVb + t2c)
                                o = (t2c % 2) * DH
                                eh = EH[:, t2c * QB:(t2c + 1) * QB]
                                el = EL[:, t2c * QB:(t2c + 1) * QB]
                                for c in range(DQC):
                                    pj = bank(c, QB)
                                    vh = VSH[:, o + c * 128:o + (c + 1) * 128]
                                    vl = VSL[:, o + c * 128:o + (c + 1) * 128]
                                    tensor.matmul(pj, vh, eh, start=(t2c == 0), stop=False)
                                    tensor.matmul(pj, vh, el, start=False, stop=False)
                                    mmi = tensor.matmul(pj, vl, eh, start=False,
                                                        stop=(t2c == T2C - 1))
                                cT.inc(mmi)
                        block.tensor(ft_av)
                        stV.g += T2C
                        t_av = cT.n
                        prev_t_av[0] = t_av

                        def fv_o(vector, t_av=t_av, v_rb=v_rb, h=h):
                            vector.wait_ge(cT.h, t_av)
                            vector.wait_ge(cV.h, v_rb)
                            for c in range(DQC):
                                sl_ = slice(c * QB, (c + 1) * QB)
                                vector.tensor_mul(OB[:, sl_], bank(c, QB), RB[:, 0:QB])
                                g = h * DQC + c
                                vector.tensor_copy(HI[:, g * QB:(g + 1) * QB], OB[:, sl_])
                                vi = vector.tensor_tensor(LO[:, g * QB:(g + 1) * QB], OB[:, sl_],
                                                          HI[:, g * QB:(g + 1) * QB],
                                                          ALU.subtract)
                            cV.inc(vi)
                        block.vector(fv_o)

                    v_oh = cV.n
                    for d2h in range(2):
                        t6 = cT.n
                        gWb = stQW.g

                        def fg_wo(gpsimd, gWb=gWb, t6=t6, d2h=d2h):
                            gpsimd.wait_ge(cT.h, t6)
                            for dvc in range(KC):
                                if dvc >= 2:
                                    gpsimd.wait_ge(cT.h, t6 + dvc - 1)
                                o = (dvc % 2) * (D // 2)
                                stQW.inc(gWb + dvc, gpsimd.dma_start(
                                    out=WOH[:, o:o + D // 2],
                                    in_=wo_hi[dvc][:, d2h * (D // 2):(d2h + 1) * (D // 2)]))
                                stQW.inc(gWb + dvc, gpsimd.dma_start(
                                    out=WOL[:, o:o + D // 2],
                                    in_=wo_lo[dvc][:, d2h * (D // 2):(d2h + 1) * (D // 2)]))
                        block.gpsimd(fg_wo)

                        def ft_wo(tensor, gWb=gWb, v_oh=v_oh, v=cV.n):
                            tensor.wait_ge(cV.h, max(v_oh, v))
                            for dvc in range(KC):
                                stQW.wait(tensor, gWb + dvc)
                                o = (dvc % 2) * (D // 2)
                                oh = HI[:, dvc * QB:(dvc + 1) * QB]
                                ol = LO[:, dvc * QB:(dvc + 1) * QB]
                                for j in range(D2C // 2):
                                    pj = bank(j, QB)
                                    wh = WOH[:, o + j * 128:o + (j + 1) * 128]
                                    wl = WOL[:, o + j * 128:o + (j + 1) * 128]
                                    tensor.matmul(pj, wh, oh, start=(dvc == 0), stop=False)
                                    tensor.matmul(pj, wh, ol, start=False, stop=False)
                                    mmi = tensor.matmul(pj, wl, oh, start=False,
                                                        stop=(dvc == KC - 1))
                                cT.inc(mmi)
                        block.tensor(ft_wo)
                        stQW.g += KC
                        t_wo = cT.n

                        v_ev0 = cV.n
                        gXb = stX.g

                        def fs_x(sync, tq=tq, d2h=d2h, v_ev0=v_ev0, gXb=gXb):
                            sync.wait_ge(cV.h, v_ev0)
                            for j in range(D2C // 2):
                                c = d2h * (D2C // 2) + j
                                if j >= 2:
                                    sync.wait_ge(cV.h, v_ev0 + j - 1)
                                stX.inc(gXb + j, sync.dma_start(
                                    out=XS[:, (j % 2) * 512:(j % 2) * 512 + QB],
                                    in_=xmid[c][:, tq:tq + QB]))
                        block.sync(fs_x)
                        stX.g += D2C // 2

                        def fv_y(vector, t_wo=t_wo, gXb=gXb, d2h=d2h, s_ys=sA.n):
                            vector.wait_ge(cT.h, t_wo)
                            if d2h > 0 or True:
                                vector.wait_ge(sA.h, s_ys)
                            for j in range(D2C // 2):
                                c = d2h * (D2C // 2) + j
                                stX.wait(vector, gXb + j)
                                cV.inc(vector.scalar_tensor_tensor(
                                    YS[:, (j % 2) * 512:(j % 2) * 512 + QB], bank(j, QB),
                                    CONST[:, O_BO + c:O_BO + c + 1],
                                    XS[:, (j % 2) * 512:(j % 2) * 512 + QB],
                                    ALU.add, ALU.add))
                        block.vector(fv_y)
                        v_y0 = cV.n - D2C // 2

                        def fa_st(scalar, v_y0=v_y0, tq=tq, d2h=d2h):
                            for j in range(D2C // 2):
                                c = d2h * (D2C // 2) + j
                                scalar.wait_ge(cV.h, v_y0 + j + 1)
                                sA.inc(scalar.dma_start(
                                    out=xmid2[c][:, tq:tq + QB],
                                    in_=YS[:, (j % 2) * 512:(j % 2) * 512 + QB]))
                        block.scalar(fa_st)
                    barrier()
                if debug:
                    def fg3(gpsimd):
                        gD.inc(gpsimd.dma_start(out=dbg["dbg_xmid2"], in_=xmid2))
                        gpsimd.wait_ge(gD.h, gD.n)
                    block.gpsimd(fg3)
                    barrier()

        PH = phases if phases is not None else ["consts", "wconv", "ffn1", "qkv",
                                                "cc", "attn", "ffn2"]
        if "consts" in PH:
            p_consts()
        if "wconv" in PH:
            p_wconv()
        if "ffn1" in PH:
            p_ffn(x_t, xmid, "f1")
        if "ffn1ln" in PH:
            p_ffn(x_t, xmid, "f1", parts=("ln",))
        if "ffn1gu" in PH:
            p_ffn(x_t, xmid, "f1", parts=("ln", "gu"))
        if debug:
            def fgd(gpsimd):
                gD.inc(gpsimd.dma_start(out=dbg["dbg_xmid"], in_=xmid))
                gpsimd.wait_ge(gD.h, gD.n)
            block.gpsimd(fgd)
            barrier()
        if "qkv" in PH:
            p_qkv()
        if "cc" in PH:
            p_collective()
        if "attn" in PH:
            p_attn()
        if "ffn2" in PH:
            p_ffn(xmid2, out_t, "f2", final=True)

    return nc



def _rope_tables(seq_len, head_dim):
    k = np.arange(0, head_dim, 2, dtype=np.float32) / np.float32(head_dim)
    inv_freq = (1.0 / (10000.0 ** k)).astype(np.float32)
    t = np.arange(seq_len, dtype=np.float32)
    freqs = (t[:, None] * inv_freq[None, :]).astype(np.float32)
    emb = np.concatenate([freqs, freqs], axis=-1)
    cos = np.cos(emb.astype(np.float64)).astype(np.float32)
    sin = np.sin(emb.astype(np.float64)).astype(np.float32)
    return cos, sin


def make_inputs(cfg, b, half, x, wqkv, wout, bout, w_gate, w_up, w_down,
                s_gate, s_up, s_down):
    D, I, T, KC, IC, DQC = cfg.D, cfg.I, cfg.T, cfg.KC, cfg.IC, cfg.DQC
    cos, sin = _rope_tables(cfg.S, cfg.DH)
    sl = slice(half * T, (half + 1) * T)
    return {
        "x_t": np.ascontiguousarray(
            np.asarray(x[b, sl, :], dtype=np.float32).T.reshape(KC, 128, T)),
        "wqkv_t": np.ascontiguousarray(np.asarray(wqkv, np.float32).reshape(KC, 128, 3 * D)),
        "wout_t": np.ascontiguousarray(np.asarray(wout, np.float32).reshape(KC, 128, D)),
        "wg_t": np.ascontiguousarray(
            np.asarray(w_gate).reshape(KC, 128, IC, 128).transpose(2, 1, 0, 3)
            .reshape(IC, 128, D)),
        "wu_t": np.ascontiguousarray(
            np.asarray(w_up).reshape(KC, 128, IC, 128).transpose(2, 1, 0, 3)
            .reshape(IC, 128, D)),
        "wd_t": np.ascontiguousarray(np.asarray(w_down).reshape(IC, 128, D)),
        "sg_pp": np.ascontiguousarray(np.asarray(s_gate, np.float32).reshape(IC, 128).T),
        "su_pp": np.ascontiguousarray(np.asarray(s_up, np.float32).reshape(IC, 128).T),
        "sd_pp": np.ascontiguousarray(np.asarray(s_down, np.float32).reshape(KC, 128).T),
        "bout_pp": np.ascontiguousarray(np.asarray(bout, np.float32).reshape(KC, 128).T),
        "cos_t": np.ascontiguousarray(cos[sl, :].T.reshape(DQC, 128, T)),
        "sin_t": np.ascontiguousarray(sin[sl, :].T.reshape(DQC, 128, T)),
    }


def kernel(L0_attn_x, L0_attn_wqkv, L0_attn_wout, L0_attn_bout, L0_ffn_w_gate,
           L0_ffn_w_up, L0_ffn_w_down, L0_ffn_s_gate, L0_ffn_s_up, L0_ffn_s_down):
    cfg = CFG()
    x = np.asarray(L0_attn_x, dtype=np.float32)
    B = x.shape[0]
    nc = build_program(cfg)
    in_maps = []
    for core in range(NCORES):
        b, half = core // 2, core % 2
        in_maps.append(make_inputs(cfg, b, half, x, L0_attn_wqkv, L0_attn_wout,
                                   L0_attn_bout, L0_ffn_w_gate, L0_ffn_w_up,
                                   L0_ffn_w_down, L0_ffn_s_gate, L0_ffn_s_up,
                                   L0_ffn_s_down))
    res = run_bass_kernel_spmd(nc, in_maps, core_ids=list(range(NCORES)))
    out = np.empty((B, cfg.S, cfg.D), dtype=np.float32)
    for core in range(NCORES):
        b, half = core // 2, core % 2
        o = res.results[core]["out_t"].reshape(cfg.D, cfg.T)
        out[b, half * cfg.T:(half + 1) * cfg.T, :] = o.T
    return out
